# version 14
# baseline (speedup 1.0000x reference)
import numpy as np
import ml_dtypes
import concourse.bass as bass
import concourse.mybir as mybir
from concourse.bass_utils import run_bass_kernel_spmd

F32 = mybir.dt.float32
BF16 = mybir.dt.bfloat16
AF = mybir.ActivationFunctionType
ALU = mybir.AluOpType
AX = mybir.AxisListType.X

D = 1024
SC = 2048
NT = SC // 128
NH = 16
HD = 64
NP = 8
HID = 2730
WIN = 2 * HID
FFD = 2816
GAMMA = [float(1.0 - 2.0 ** (-5.0 - h)) for h in range(NH)]
EPS1 = 1e-5
EPSH = 1e-6
ACT_TILES = [(ct * 128, min(128, HID - ct * 128)) for ct in range(22)]
SBUF_TOP = 229344


class Prog:
    ENGINES = ("pe", "act", "dve", "sp", "pool")

    def __init__(self, nc):
        self.nc = nc
        self.ops = {e: [] for e in self.ENGINES}
        self.res = {}
        self.streams = {}
        self.fence_tok = None

    def op(self, eng, fn, reads=(), writes=(), tag=""):
        deps = set()
        for k in reads:
            r = self.res.get(k)
            if r is None:
                raise KeyError(f"read before write: {k!r} (op {tag})")
            deps.add(r["w"])
        for k in writes:
            r = self.res.get(k)
            if r is not None:
                deps.add(r["w"])
                deps.update(r["r"])
            elif self.fence_tok is not None:
                deps.add(self.fence_tok)
        idx = len(self.ops[eng])
        tok = ("e", eng, idx)
        self.ops[eng].append({"fn": fn, "deps": deps, "dma": None, "tag": tag})
        for k in reads:
            self.res[k]["r"].append(tok)
        for k in writes:
            self.res[k] = {"w": tok, "r": []}
        return tok

    def dma(self, stream, eng, fn, reads=(), writes=(), tag=""):
        s = self.streams.setdefault(stream, {"engine": eng, "n": 0})
        assert s["engine"] == eng
        deps = set()
        for k in reads:
            r = self.res.get(k)
            if r is not None:
                deps.add(r["w"])
        for k in writes:
            r = self.res.get(k)
            if r is not None:
                deps.add(r["w"])
                deps.update(r["r"])
            elif self.fence_tok is not None:
                deps.add(self.fence_tok)
        s["n"] += 1
        tok = ("d", stream, s["n"])
        self.ops[eng].append({"fn": fn, "deps": deps, "dma": (stream, s["n"]),
                              "tag": tag})
        for k in reads:
            if k in self.res:
                self.res[k]["r"].append(tok)
        for k in writes:
            self.res[k] = {"w": tok, "r": []}
        return tok

    def fence(self, tag=""):
        alldeps = set()
        for r in self.res.values():
            alldeps.add(r["w"])
            alldeps.update(r["r"])
        toks = []
        for e in self.ENGINES:
            if not self.ops[e]:
                continue
            idx = len(self.ops[e])
            toks.append(("e", e, idx))
            self.ops[e].append({"fn": lambda h: None, "deps": set(alldeps),
                                "dma": None, "tag": f"fence_{tag}"})
        master = ("e", "sp", len(self.ops["sp"]))
        self.ops["sp"].append({"fn": lambda h: None, "deps": set(toks),
                               "dma": None, "tag": f"fencem_{tag}"})
        for k in self.res:
            self.res[k] = {"w": master, "r": []}
        self.fence_tok = master

    NSLOT = 8

    def emit(self):
        nc = self.nc
        pub = {e: [False] * len(self.ops[e]) for e in self.ENGINES}
        for e in self.ENGINES:
            for o in self.ops[e]:
                for d in o["deps"]:
                    if d[0] == "e" and not (d[1] == e == "pe"):
                        pub[d[1]][d[2]] = True
        cum = {}
        for e in self.ENGINES:
            c, lst = 0, []
            for f in pub[e]:
                c += int(f)
                lst.append(c)
            cum[e] = lst

        ctxs, sems, ssems = [], {}, {}
        for e in self.ENGINES:
            if any(pub[e]):
                cm = nc.semaphore(f"sem_{e}")
                sems[e] = cm.__enter__()
                ctxs.append(cm)
        for sname in self.streams:
            lst = []
            for sl in range(self.NSLOT):
                cm = nc.semaphore(f"dma_{sname}{sl}")
                lst.append(cm.__enter__())
                ctxs.append(cm)
            ssems[sname] = lst
        blk_cm = nc.Block()
        block = blk_cm.__enter__()
        ctxs.append(blk_cm)

        eng_attr = {"pe": "tensor", "act": "scalar", "dve": "vector",
                    "sp": "sync", "pool": "gpsimd"}

        def slotcyc(n):
            return (n - 1) % self.NSLOT, (n - 1) // self.NSLOT + 1

        def make_body(e):
            def body(handle):
                waited = {}
                for i, o in enumerate(self.ops[e]):
                    deps = []
                    for d in sorted(o["deps"], key=repr):
                        if d[0] == "e":
                            if d[1] == e == "pe":
                                continue
                            deps.append((sems[d[1]], cum[d[1]][d[2]]))
                        else:
                            sl, cyc = slotcyc(d[2])
                            deps.append((ssems[d[1]][sl], 16 * cyc))
                    if o["dma"] is not None:
                        sname, n = o["dma"]
                        sl, cyc = slotcyc(n)
                        if cyc > 1:
                            deps.append((ssems[sname][sl], 16 * (cyc - 1)))
                    for sem, val in deps:
                        if val > 0 and waited.get(sem.name, -1) < val:
                            handle.wait_ge(sem, val)
                            waited[sem.name] = val
                    inst = o["fn"](handle)
                    if o["dma"] is not None:
                        sname, n = o["dma"]
                        sl, cyc = slotcyc(n)
                        inst.then_inc(ssems[sname][sl], 16)
                    elif pub[e][i]:
                        if inst is None:
                            inst = handle.nop()
                        inst.then_inc(sems[e], 1)
            return body

        for e in self.ENGINES:
            if self.ops[e]:
                getattr(block, eng_attr[e])(make_body(e))
        for cm in reversed(ctxs):
            cm.__exit__(None, None, None)


def build_nc():
    nc = bass.Bass("TRN2", target_bir_lowering=False)
    p = Prog(nc)

    def din(name, shape):
        return nc.dram_tensor(name, shape, F32, kind="ExternalInput").ap()

    x_loc = din("x_loc", [SC, D])
    x_rem = din("x_rem", [SC, D])
    ln1w_h = din("ln1w", [128, 8])
    ln1b_h = din("ln1b", [128, 8])
    ln2w_h = din("ln2w", [128, 8])
    ln2b_h = din("ln2b", [128, 8])
    w_h = {n: din(n, [D, D]) for n in ("wq", "wk", "wv", "wg", "wo")}
    w_in_h = din("w_in", [D, WIN])
    w_out_h = din("w_out", [HID, D])
    b_in_a_h = din("b_in_a", [128, 22])
    b_in_b_h = din("b_in_b", [128, 22])
    b_out_h = din("b_out_bc", [128, D])
    wgsh_h = din("wg_shard", [FFD, 128])
    wush_h = din("wu_shard", [FFD, 128])
    wdsh_h = din("wd_shard", [D, 3 * 128])
    agin_wg = nc.dram_tensor("agin_wg", [128, FFD], F32, kind="Internal")
    agin_wu = nc.dram_tensor("agin_wu", [128, FFD], F32, kind="Internal")
    agin_wd = nc.dram_tensor("agin_wd", [384, D], F32, kind="Internal")
    agout_wg = nc.dram_tensor("agout_wg", [1024, FFD], F32, kind="Internal")
    agout_wu = nc.dram_tensor("agout_wu", [1024, FFD], F32, kind="Internal")
    agout_wd = nc.dram_tensor("agout_wd", [3072, D], F32, kind="Internal")
    gqhl_h = [din(f"gq_{s}", [128, NP * 128]) for s in ("hi", "lo")]
    gkhl_h = [din(f"gk_{s}", [128, NP * 128]) for s in ("hi", "lo")]
    gknhl_h = [din(f"gkn_{s}", [128, D]) for s in ("hi", "lo")]
    gchl_h = [din(f"gc_{s}", [128, NP]) for s in ("hi", "lo")]
    mskT_h = din("mskT", [128, 128])
    ident_h = din("ident", [128, 128])
    flag_h = din("flag", [128, 1])

    y = nc.dram_tensor("y", [SC, D], F32, kind="ExternalOutput").ap()
    spill_x1 = nc.dram_tensor("spill_x1", [SC, D], BF16, kind="Internal").ap()
    spill_xa = nc.dram_tensor("spill_xa", [SC, D], BF16, kind="Internal").ap()
    spill_z1 = nc.dram_tensor("spill_z1", [SC, D], BF16, kind="Internal").ap()

    sb = nc.alloc_sbuf_tensor
    ident = sb("ident_sb", [128, 128], BF16)
    gc = sb("gc_sb", [128, NP], F32)
    ln1w = sb("ln1w_s", [128, 8], F32)
    ln1b = sb("ln1b_s", [128, 8], F32)
    ln2w = sb("ln2w_s", [128, 8], F32)
    ln2b = sb("ln2b_s", [128, 8], F32)
    bina = sb("bina_s", [128, 22], F32)
    binb = sb("binb_s", [128, 22], F32)
    bout = sb("bout_s", [128, D], BF16)
    flag = sb("flag_s", [128, 1], F32)
    WMAX = 3072
    scrf = sb("scrf", [128, WMAX], F32)
    scrg = sb("scrg", [128, WMAX], F32)
    scrh = sb("scrh", [128, WMAX], F32)
    scrp = sb("scrp", [128, WMAX], BF16)
    st = sb("st_all", [128, 14 * 16], F32)
    NB = 192
    qm = sb("qm_all", [128, 2 * NB], F32)

    ARENA = 164352
    base = SBUF_TOP - ARENA
    assert nc.sbuf_bytes_remaining >= ARENA + 64, nc.sbuf_bytes_remaining

    def at(name, off, shape, dtype):
        return nc.alloc_sbuf_tensor_at(name, shape, dtype, offset=base + off)

    ws = {}
    for i, n in enumerate(("wq", "wk", "wv", "wg", "wo")):
        ws[n] = at(n + "_s", i * 16384, [128, 8 * D], BF16)
    x0 = at("x0", 81920, [128, 2 * D], BF16)
    xrm = at("xrm", 86016, [128, 2 * D], BF16)
    G = 256
    xcT = at("xcT", 90112, [128, 8 * G], BF16)
    qhT = at("qhT", 94208, [128, NP * G], BF16)
    khT = at("khT", 98304, [128, NP * G], BF16)
    khn = at("khn", 102400, [128, 2 * D], BF16)
    vno = at("vno", 106496, [128, 2 * D], BF16)
    sg = at("sg", 110592, [128, 2 * D], BF16)
    xr = at("xr", 114688, [128, 2 * D], BF16)
    xrT = at("xrT", 118784, [128, 2 * D], BF16)
    zzb = at("zzb", 122880, [128, 2 * D], BF16)
    aT = at("aT", 126976, [128, 2 * 128], BF16)
    ro = at("ro", 127488, [128, D], F32)
    rcur = at("rcur", 131584, [128, NP * HD], F32)
    rnxt = at("rnxt", 133632, [128, NP * HD], F32)
    rbf = at("rbf", 135680, [128, NP * HD], BF16)
    x1b = at("x1b", 136704, [128, 2 * D], BF16)
    xab = at("xab", 140800, [128, 2 * D], BF16)

    w_in_s = at("w_in_s", 0, [128, 8 * WIN], BF16)
    w_out_s = at("w_out_s", 87360, [128, 22 * D], BF16)
    GB = 256
    xc2T = at("xc2T", 132416, [128, 8 * GB], BF16)
    hT = at("hT", 136512, [128, 22 * GB], BF16)
    abA = at("abA", 147776, [128, GB], BF16)
    abB = at("abB", 148288, [128, GB], BF16)
    x1g = at("x1g", 148800, [128, 2 * D], BF16)
    xag = at("xag", 152896, [128, 2 * D], BF16)
    z1b = at("z1b", 156992, [128, 2 * D], BF16)

    wgq = at("wgq", 0, [128, 8 * FFD], BF16)
    wuq = at("wuq", 45056, [128, 8 * FFD], BF16)
    wdq = at("wdq", 90112, [128, 22 * D], BF16)
    hqT = at("hqT", 135168, [128, 22 * 128], BF16)
    stage = at("prepstage", 144896, [128, 3 * D], F32)
    x0B = at("x0B", 144896, [128, 2 * D], BF16)
    xrB = at("xrB", 148992, [128, 2 * D], BF16)
    sqB = at("sqB", 153088, [128, D], F32)
    gq = at("gq_sb", 157184, [128, NP * 128], BF16)
    gk = at("gk_sb", 159232, [128, NP * 128], BF16)
    gkn = at("gkn_sb", 161280, [128, D], BF16)
    mskT = at("mskT_sb", 163328, [128, 128], BF16)
    identf = at("identf_sb", 163840, [128, 128], F32)
    xq = at("xq", 140800, [128, D], BF16)
    xqT = at("xqT", 142848, [128, D], BF16)
    sgt = at("sgt", 144896, [128, FFD], BF16)
    sut = at("sut", 150528, [128, FFD], BF16)
    z1g = at("z1g", 156160, [128, 2 * D], BF16)
    outf = at("outf", 160256, [128, D], F32)

    PS = [nc.alloc_psum_tensor(f"PS{i}", [128, 512], F32) for i in range(6)]
    PT = [nc.alloc_psum_tensor(f"PT{i}", [128, 512], BF16) for i in range(2)]
    PSK = [("PS", i) for i in range(6)]
    PTK = [("PT", i) for i in range(2)]
    p.op("sp", lambda h: None, writes=PSK + PTK, tag="psinit")

    def stv(j, cs=slice(0, 1), so=0):
        return st[:, 16 * (so + j):16 * (so + j + 1)][:, cs]
    ST = ("st", 0)

    def qmv(j, nb):
        return qm[:, NB * j:NB * j + nb]
    QM = ("qm",)
    REGS = [0, 88, 96, 176]
    FKS = [("scrf", r) for r in REGS]
    GKS = [("scrg", r) for r in REGS]
    HKS = [("scrh", r) for r in REGS]
    QMS = [("qm", r) for r in REGS]
    QOFF = 3072
    scrfb = scrf.bitcast(BF16)
    scrgb = scrg.bitcast(BF16)
    scrhb = scrh.bitcast(BF16)
    QREGS = [200, 288, 296, 376]
    HKS_ALL = HKS + [("scrh", r) for r in QREGS]

    def ld(stream, eng, dst_ap, src_ap, writes, reads=(), tag="ld"):
        def fn(h, dst=dst_ap, src=src_ap):
            return h.dma_start(out=dst, in_=src)
        p.dma(stream, eng, fn, reads=reads, writes=writes, tag=tag)

    for dst, src, key in [
        (mskT[:], mskT_h[:], ("mskT",)), (ident[:], ident_h[:], ("ident",)),
        (identf[:], ident_h[:], ("identf",)),
        (bout[:], b_out_h[:], ("bout",)),
    ]:
        ld("w", "pool", dst, src, [key])
    for i, (hl, dst, key, w_) in enumerate([
        (gqhl_h, gq, ("gq",), NP * 128), (gkhl_h, gk, ("gk",), NP * 128),
        (gknhl_h, gkn, ("gkn",), D), (gchl_h, gc, ("gc",), NP),
    ]):
        ld("w", "pool", scrf[:, 0:w_], hl[0][:], FKS)
        ld("w", "pool", scrg[:, 0:w_], hl[1][:], GKS)
        p.op("dve", lambda h, dst=dst, w_=w_: h.tensor_tensor(
            dst[:, 0:w_], scrf[:, 0:w_], scrg[:, 0:w_], ALU.add),
            reads=FKS + GKS, writes=[key], tag=f"ghl{i}")
    for dst, src, key in [
        (ln1w[:], ln1w_h[:], ("ln1w",)), (ln1b[:], ln1b_h[:], ("ln1b",)),
        (ln2w[:], ln2w_h[:], ("ln2w",)), (ln2b[:], ln2b_h[:], ("ln2b",)),
        (bina[:], b_in_a_h[:], ("bina",)), (binb[:], b_in_b_h[:], ("binb",)),
        (flag[:], flag_h[:], ("flag",)),
    ]:
        ld("c", "sp", dst, src, [key])
    for nm in ("wq", "wk", "wv", "wg", "wo"):
        for kt in range(8):
            ld("w", "pool", ws[nm][:, kt * D:(kt + 1) * D],
               w_h[nm][kt * 128:(kt + 1) * 128, :], [(nm, kt)])

    def ln_stats(src_ap, W, eps, src_keys, nst=1, tag="", sq_ap=None,
                 sq_keys=None, so=0):
        if sq_ap is None:
            sq_ap, sq_keys = scrf[:, 0:W], FKS
        ST = ("st", so)
        cs = slice(0, nst)
        if nst == 1:
            p.op("dve", lambda h, s=src_ap: h.reduce_sum(
                stv(0, cs, so), s, axis=AX), reads=src_keys, writes=[ST],
                tag="lnrs" + tag)
            p.op("act", lambda h, s=src_ap: h.activation(
                sq_ap, s, AF.Square, accum_out=stv(1, cs, so)),
                reads=list(src_keys) + [ST], writes=[ST] + list(sq_keys),
                tag="lnsq" + tag)
        else:
            v3 = src_ap.rearrange("p (g w) -> p g w", w=W)
            p.op("dve", lambda h, s=v3: h.reduce_sum(
                stv(0, cs, so), s, axis=AX), reads=src_keys, writes=[ST],
                tag="lnrs" + tag)
            p.op("act", lambda h, s=src_ap: h.activation(
                scrf[:, 0:nst * W], s, AF.Square),
                reads=src_keys, writes=FKS, tag="lnsq" + tag)
            p.op("dve", lambda h: h.reduce_sum(
                stv(1, cs, so), scrf[:, 0:nst * W].rearrange(
                    "p (g w) -> p g w", w=W), axis=AX),
                reads=FKS + [ST], writes=[ST], tag="lnss" + tag)
        p.op("dve", lambda h: h.tensor_scalar(
            stv(2, cs, so), stv(0, cs, so), 1.0 / W, None, ALU.mult),
            reads=[ST], writes=[ST], tag="lnmu" + tag)
        p.op("dve", lambda h: h.tensor_tensor(
            stv(3, cs, so), stv(2, cs, so), stv(2, cs, so), ALU.mult),
            reads=[ST], writes=[ST], tag="lnm2" + tag)
        p.op("dve", lambda h: h.scalar_tensor_tensor(
            stv(4, cs, so), stv(1, cs, so), 1.0 / W, stv(3, cs, so),
            ALU.mult, ALU.subtract),
            reads=[ST], writes=[ST], tag="lnva" + tag)
        p.op("dve", lambda h: h.tensor_scalar(
            stv(4, cs, so), stv(4, cs, so), float(eps), None, ALU.add),
            reads=[ST], writes=[ST], tag="lnve" + tag)
        p.op("act", lambda h: h.activation(
            stv(5, cs, so), stv(4, cs, so), AF.Sqrt),
            reads=[ST], writes=[ST], tag="lnsd" + tag)
        p.op("dve", lambda h: h.reciprocal(stv(6, cs, so), stv(5, cs, so)),
            reads=[ST], writes=[ST], tag="lnrd" + tag)
        return cs

    def transpose_tile(src_ap, src_keys, dstT_fn, dst_wkeys, wb=None, tag=""):
        for half in range(2):
            bank, bkey = PT[half], PTK[half]
            for j in range(4):
                kt = half * 4 + j
                p.op("pe", lambda h, kt=kt, j=j, bank=bank, s=src_ap: h.transpose(
                    bank[:, j * 128:(j + 1) * 128],
                    s[:, kt * 128:(kt + 1) * 128], ident[:]),
                    reads=list(src_keys) + [("ident",)],
                    writes=[bkey], tag="tr" + tag)
            for j in range(4):
                kt = half * 4 + j
                kw, rd, func = {}, [bkey], AF.Copy
                if wb is not None:
                    kw["scale"] = wb[0][:, kt:kt + 1]
                    kw["bias"] = wb[2][:, kt:kt + 1]
                    rd += [wb[1], wb[3]]
                    func = AF.Identity
                p.op("act", lambda h, kt=kt, j=j, bank=bank, kw=kw,
                     func=func: h.activation(
                    dstT_fn(kt), bank[:, j * 128:(j + 1) * 128], func, **kw),
                    reads=rd, writes=[dst_wkeys(kt)], tag="trev" + tag)

    def ln_chunk(src_ap, src_keys, eps, wb, xc_ap, xc_key, dstT_fn, dstT_wkeys,
                 tag="", so=0, sq_ap=None, sq_keys=None):
        cs = ln_stats(src_ap, D, eps, src_keys, tag=tag, so=so,
                      sq_ap=sq_ap, sq_keys=sq_keys)
        p.op("dve", lambda h, s=src_ap, cs=cs, so=so: h.tensor_scalar(
            xc_ap, s, stv(2, cs, so), stv(6, cs, so),
            ALU.subtract, ALU.mult),
            reads=list(src_keys) + [("st", so)], writes=[xc_key],
            tag="lnap" + tag)
        transpose_tile(xc_ap, [xc_key], dstT_fn, dstT_wkeys, wb=wb, tag=tag)

    def proj(ps_ap, bank_key, lhsT_fn, rhs_fn, nk, rd_fn, tag=""):
        for kt in range(nk):
            p.op("pe", lambda h, kt=kt: h.matmul(
                ps_ap, lhsT_fn(kt), rhs_fn(kt),
                start=(kt == 0), stop=(kt == nk - 1)),
                reads=rd_fn(kt) + ([bank_key] if kt else []),
                writes=[bank_key], tag=tag)

    def state_update(khn_ap, v_ap, kv_keys, tag=""):
        PD, PDK = PS[5], PSK[5]
        p.op("dve", lambda h: h.tensor_tensor(
            scrg[:, 0:NP * HD].rearrange("p (g w) -> p g w", w=HD),
            rcur[:].rearrange("p (g w) -> p g w", w=HD),
            gc[:, :, None].broadcast_to((128, NP, HD)), ALU.mult),
            reads=[("rcur",), ("gc",)], writes=GKS, tag="rsc" + tag)
        for pr in range(NP):
            c0 = pr * HD
            p.op("pe", lambda h, pr=pr: h.matmul(
                PD[0:64, 0:64], khn_ap[:, pr * 128:pr * 128 + 64],
                v_ap[:, pr * 128:pr * 128 + 64], start=True, stop=True,
                tile_position=(0, 0)),
                reads=kv_keys, writes=[PDK], tag="sdA" + tag)
            p.op("pe", lambda h, pr=pr: h.matmul(
                PD[64:128, 0:64], khn_ap[:, pr * 128 + 64:pr * 128 + 128],
                v_ap[:, pr * 128 + 64:pr * 128 + 128], start=True, stop=True,
                tile_position=(0, 64)),
                reads=kv_keys + [PDK], writes=[PDK], tag="sdB" + tag)
            p.op("dve", lambda h, pr=pr, c0=c0: h.scalar_tensor_tensor(
                rnxt[:, c0:c0 + HD], PD[:, 0:64], gc[:, pr:pr + 1],
                scrg[:, c0:c0 + HD], ALU.mult, ALU.add),
                reads=[PDK, ("gc",)] + GKS, writes=[("rnxt", pr)],
                tag="rup" + tag)
        p.op("dve", lambda h: h.tensor_copy(rcur[:], rnxt[:]),
            reads=[("rnxt", pr) for pr in range(NP)], writes=[("rcur",)],
            tag="rsw" + tag)

    def quant_tile(x_ap, x_keys, W, out_ap, out_wkey, rows=128, tag="",
                   soff=0, nsl=2, sseq=False, out_wkeys_fn=None):
        ws = W // nsl
        assert ws % 16 == 0
        SL = []
        for j in range(nsl):
            cx = j * ws
            c0 = soff if sseq else soff + cx
            nb, b0 = ws // 16, c0 // 16
            ki = 200 + c0 // 16
            x3 = x_ap[:, cx:cx + ws].rearrange("p (b k) -> p b k", k=16)
            m0 = qmv(0, NB)[0:rows, b0:b0 + nb]
            m1 = qmv(1, NB)[0:rows, b0:b0 + nb]
            SL.append((c0 + QOFF, ws, x3, m0, m1, nb, cx, ki))
        ORD = {s[7]: j for j, s in enumerate(SL)}

        STAGES = []

        def sl_op(eng, fn, rd, wr, tag_):
            STAGES.append((eng, fn, rd, wr, tag_))

        XK = list(x_keys)
        sl_op("dve", lambda c0, w, x3, m0, m1, nb, cx, ki:
              (lambda h: h.reduce_max(m0, x3, axis=AX,
                                      apply_absolute_value=True)),
              lambda i: XK, lambda i: [("qm", i)], "qbm" + tag)
        sl_op("act", lambda c0, w, x3, m0, m1, nb, cx, ki:
              (lambda h: h.activation(scrgb[0:rows, c0:c0 + w],
                                      x_ap[:, cx:cx + w], AF.Abs)),
              lambda i: XK, lambda i: [("scrg", i)], "qab" + tag)
        sl_op("dve", lambda c0, w, x3, m0, m1, nb, cx, ki:
              (lambda h: h.tensor_scalar(m1, m0, 1.0 / 6.0, 1e-12,
                                         ALU.mult, ALU.max)),
              lambda i: [("qm", i)], lambda i: [("qm", i)], "qs6")
        sl_op("dve", lambda c0, w, x3, m0, m1, nb, cx, ki:
              (lambda h: h.reciprocal(m0, m1)),
              lambda i: [("qm", i)], lambda i: [("qm", i)], "qre")
        sl_op("pool", lambda c0, w, x3, m0, m1, nb, cx, ki:
              (lambda h: h.tensor_tensor(
                  scrfb[0:rows, c0:c0 + w].rearrange("p (b k) -> p b k", k=16),
                  scrgb[0:rows, c0:c0 + w].rearrange("p (b k) -> p b k", k=16),
                  m0[:, :, None].broadcast_to((rows, nb, 16)), ALU.mult)),
              lambda i: [("scrg", i), ("qm", i)],
              lambda i: [("scrf", i)], "qt_" + tag)
        sl_op("dve", lambda c0, w, x3, m0, m1, nb, cx, ki:
              (lambda h: h.tensor_scalar(scrhb[0:rows, c0:c0 + w],
                                         scrfb[0:rows, c0:c0 + w], 0.25, 1.0,
                                         ALU.is_gt, ALU.mult)),
              lambda i: [("scrf", i)], lambda i: [("scrh", i)], "qt0")
        for t in (0.75, 1.25, 1.75):
            sl_op("dve", lambda c0, w, x3, m0, m1, nb, cx, ki, t=t:
                  (lambda h: h.scalar_tensor_tensor(
                      scrhb[0:rows, c0:c0 + w], scrfb[0:rows, c0:c0 + w], t,
                      scrhb[0:rows, c0:c0 + w], ALU.is_gt, ALU.add)),
                  lambda i: [("scrf", i), ("scrh", i)],
                  lambda i: [("scrh", i)], "qtl")
        sl_op("dve", lambda c0, w, x3, m0, m1, nb, cx, ki:
              (lambda h: h.tensor_scalar(scrp[0:rows, c0 - QOFF:c0 - QOFF + w],
                                         scrfb[0:rows, c0:c0 + w], 2.5, 1.0,
                                         ALU.is_gt, ALU.mult)),
              lambda i: [("scrf", i)], lambda i: [("scrp", i)], "qb0")
        for t in (3.5, 5.0, 5.0):
            sl_op("dve", lambda c0, w, x3, m0, m1, nb, cx, ki, t=t:
                  (lambda h: h.scalar_tensor_tensor(
                      scrp[0:rows, c0 - QOFF:c0 - QOFF + w],
                      scrfb[0:rows, c0:c0 + w], t,
                      scrp[0:rows, c0 - QOFF:c0 - QOFF + w],
                      ALU.is_gt, ALU.add)),
                  lambda i: [("scrf", i), ("scrp", i)],
                  lambda i: [("scrp", i)], "qbl")
        sl_op("act", lambda c0, w, x3, m0, m1, nb, cx, ki:
              (lambda h: h.activation(scrgb[0:rows, c0:c0 + w],
                                      x_ap[:, cx:cx + w], AF.Sign)),
              lambda i: XK + [("scrf", i)], lambda i: [("scrg", i)],
              "qsg" + tag)
        sl_op("dve", lambda c0, w, x3, m0, m1, nb, cx, ki:
              (lambda h: h.scalar_tensor_tensor(
                  scrhb[0:rows, c0:c0 + w], scrhb[0:rows, c0:c0 + w], 0.5,
                  scrp[0:rows, c0 - QOFF:c0 - QOFF + w], ALU.mult, ALU.add)),
              lambda i: [("scrh", i), ("scrp", i)],
              lambda i: [("scrh", i)], "qcm")
        sl_op("pool", lambda c0, w, x3, m0, m1, nb, cx, ki:
              (lambda h: h.tensor_tensor(
                  scrhb[0:rows, c0:c0 + w].rearrange("p (b k) -> p b k", k=16),
                  scrhb[0:rows, c0:c0 + w].rearrange("p (b k) -> p b k", k=16),
                  m1[:, :, None].broadcast_to((rows, nb, 16)), ALU.mult)),
              lambda i: [("scrh", i), ("qm", i)],
              lambda i: [("scrh", i)], "qds")
        sl_op("pool", lambda c0, w, x3, m0, m1, nb, cx, ki:
              (lambda h: h.tensor_tensor(
                  out_ap[:, cx:cx + w], scrhb[0:rows, c0:c0 + w],
                  scrgb[0:rows, c0:c0 + w], ALU.mult)),
              lambda i: [("scrh", i), ("scrg", i)],
              (lambda i: [out_wkey] if out_wkeys_fn is None
               else [out_wkey] + out_wkeys_fn(ORD[i])), "qo" + tag)

        if sseq:
            for s in SL:
                for eng, fn, rd, wr, tag_ in STAGES:
                    p.op(eng, fn(*s), reads=rd(s[7]), writes=wr(s[7]),
                         tag=tag_)
        else:
            for eng, fn, rd, wr, tag_ in STAGES:
                for s in SL:
                    p.op(eng, fn(*s), reads=rd(s[7]), writes=wr(s[7]),
                         tag=tag_)

    def prep_shard(src_h, nt, kc, agin_t, agkey, tag):
        W = nt * kc
        ld("w", "pool",
           stage[:, 0:W].rearrange("p (t k) -> p t k", k=kc),
           src_h.rearrange("(t p) k -> p t k", p=128),
           [("prepstage",)], tag="psh" + tag)
        quant_tile(stage[:, 0:W], [("prepstage",)], W,
                   stage[:, 0:W], ("prepstage",), tag="pq" + tag)
        ntile = W // 128
        for grp in range((ntile + 3) // 4):
            tls = list(range(grp * 4, min(ntile, grp * 4 + 4)))
            bank, bkey = PS[2 + grp % 2], PSK[2 + grp % 2]
            for j, tl in enumerate(tls):
                p.op("pe", lambda h, j=j, tl=tl, bank=bank: h.transpose(
                    bank[:, j * 128:(j + 1) * 128],
                    stage[:, tl * 128:(tl + 1) * 128], identf[:]),
                    reads=[("prepstage",), ("identf",)],
                    writes=[bkey], tag="ptr" + tag)
            for j, tl in enumerate(tls):
                nkc = kc // 128
                t_, jj = tl // nkc, tl % nkc
                dcol = jj * (nt * 128) + t_ * 128
                p.op("act", lambda h, j=j, dcol=dcol, bank=bank: h.activation(
                    scrh[:, dcol:dcol + 128], bank[:, j * 128:(j + 1) * 128],
                    AF.Copy),
                    reads=[bkey], writes=HKS_ALL, tag="pev" + tag)
        p.dma("sp1", "sp",
              lambda h, agin_t=agin_t, nt=nt, kc=kc: h.dma_start(
                  out=agin_t.ap().rearrange("(jj p) n -> p jj n", p=128),
                  in_=scrh[:, 0:W].rearrange("p (jj n) -> p jj n",
                                             n=nt * 128)),
              reads=HKS_ALL, writes=[agkey], tag="pst" + tag)

    prep_shard(wgsh_h, 22, 128, agin_wg, ("agin", "wg"), "g")
    prep_shard(wush_h, 22, 128, agin_wu, ("agin", "wu"), "u")
    prep_shard(wdsh_h, 8, 384, agin_wd, ("agin", "wd"), "d")

    p.op("dve", lambda h: h.memset(rcur[:], 0.0), writes=[("rcur",)], tag="ri")

    for rc in range(NT):
        par = rc % 2
        xs = xrm[:, par * D:(par + 1) * D]
        XB = ("xrmb", par)
        ld("w", "pool", xs, x_rem[rc * 128:(rc + 1) * 128, :], [XB])
        ln_chunk(xs, [XB], EPS1, (ln1w, ("ln1w",), ln1b, ("ln1b",)),
                 xr[:, par * D:(par + 1) * D], ("xrb", 0, par),
                 lambda kt, par=par: xcT[:, kt * G + par * 128:kt * G + par * 128 + 128],
                 lambda kt, par=par: ("xcTb", par, kt), tag=f"a{rc}")
        xcT_keys = [("xcTb", par, kt) for kt in range(8)]
        for half in range(2):
            ps, pk = PS[half], PSK[half]
            proj(ps[:, 0:512], pk,
                 lambda kt, par=par: xcT[:, kt * G + par * 128:kt * G + par * 128 + 128],
                 lambda kt, half=half: ws["wk"][:, kt * D + half * 512:kt * D + (half + 1) * 512],
                 8, lambda kt: xcT_keys + [("wk", kt)], tag=f"a0k{rc}")
            p.op("dve", lambda h, ps=ps, half=half, par=par: h.tensor_tensor(
                khn[:, par * D + half * 512:par * D + (half + 1) * 512],
                ps[:, 0:512], gkn[:, half * 512:(half + 1) * 512], ALU.mult),
                reads=[pk, ("gkn",)], writes=[("khnb", par, half)], tag="a0ek")
            ps2, pk2 = PS[2 + half], PSK[2 + half]
            proj(ps2[:, 0:512], pk2,
                 lambda kt, par=par: xcT[:, kt * G + par * 128:kt * G + par * 128 + 128],
                 lambda kt, half=half: ws["wv"][:, kt * D + half * 512:kt * D + (half + 1) * 512],
                 8, lambda kt: xcT_keys + [("wv", kt)], tag=f"a0v{rc}")
            p.op("act", lambda h, ps2=ps2, half=half, par=par: h.activation(
                vno[:, par * D + half * 512:par * D + (half + 1) * 512],
                ps2[:, 0:512], AF.Copy),
                reads=[pk2], writes=[("vnob", par, half)], tag="a0ev")
        state_update(khn[:, par * D:(par + 1) * D], vno[:, par * D:(par + 1) * D],
                     [("khnb", par, 0), ("khnb", par, 1),
                      ("vnob", par, 0), ("vnob", par, 1)], tag=f"a0{rc}")

    p.op("dve", lambda h: h.tensor_scalar(
        rcur[:], rcur[:], flag[:, 0:1], None, ALU.mult),
        reads=[("rcur",), ("flag",)], writes=[("rcur",)], tag="rflag")
    p.op("dve", lambda h: h.tensor_copy(rbf[:], rcur[:]),
        reads=[("rcur",)], writes=[("rbf",)], tag="rbf0")

    RG8 = [[0, 1, 2, 3, 4, 5, 6, 7]]

    def emit_ags():
        for tg, ain, aout in [("wg", agin_wg, agout_wg),
                              ("wu", agin_wu, agout_wu),
                              ("wd", agin_wd, agout_wd)]:
            p.op("pool", lambda h, ain=ain, aout=aout: h.collective_compute(
                "AllGather", ALU.bypass, replica_groups=RG8,
                ins=[ain.ap().opt()], outs=[aout.ap().opt()]),
                reads=[("agin", tg)], writes=[("ag", tg)], tag="ag" + tg)

    def emit_ln1(g, part):
        gp = g % 2
        xb = x0 if gp == 0 else x0B
        xrb = xr if gp == 0 else xrB
        so = 0 if gp == 0 else 7
        sqa = None if gp == 0 else sqB[:, 0:D]
        sqk = None if gp == 0 else [("sqB",)]
        if g == 1 and part == 1:
            p.op("sp", lambda h: None, reads=[("prepstage",)],
                 writes=[("x0b", 1, 0), ("x0b", 1, 1), ("xrb", 1, 0),
                         ("xrb", 1, 1), ("sqB",)], tag="claimB")
        for par in range(2):
            ci = 2 * g + par
            XK = ("x0b", gp, par)
            if part == 1:
                ld("w", "pool", xb[:, par * D:(par + 1) * D],
                   x_loc[ci * 128:(ci + 1) * 128, :], [XK])
                cs = ln_stats(xb[:, par * D:(par + 1) * D], D, EPS1, [XK],
                              tag=f"A{ci}", so=so, sq_ap=sqa, sq_keys=sqk)
                p.op("dve", lambda h, par=par, cs=cs, so=so, xb=xb,
                     xrb=xrb: h.tensor_scalar(
                         xrb[:, par * D:(par + 1) * D],
                         xb[:, par * D:(par + 1) * D],
                         stv(2, cs, so), stv(6, cs, so),
                         ALU.subtract, ALU.mult),
                     reads=[XK, ("st", so)], writes=[("xrb", gp, par)],
                     tag=f"lnapA{ci}")
            else:
                transpose_tile(
                    xrb[:, par * D:(par + 1) * D], [("xrb", gp, par)],
                    lambda kt, par=par: xcT[:, kt * G + par * 128:kt * G + par * 128 + 128],
                    lambda kt, par=par: ("xcTb", par, kt),
                    wb=(ln1w, ("ln1w",), ln1b, ("ln1b",)), tag=f"A{ci}")

    emit_ln1(0, 1)
    emit_ln1(0, 2)
    for g in range(NT // 2):
        gp = g % 2
        xb = x0 if gp == 0 else x0B
        xrb = xr if gp == 0 else xrB
        if g + 1 < NT // 2:
            emit_ln1(g + 1, 1)
        if g == 6:
            emit_ags()
        ck0 = 2 * g
        xcT_keys = [("xcTb", pp, kt) for pp in range(2) for kt in range(8)]

        for pr in range(NP):
            ps, pk = PS[pr % 2], PSK[pr % 2]
            proj(ps[:, 0:G], pk,
                 lambda kt, pr=pr: ws["wq"][:, kt * D + pr * 128:kt * D + (pr + 1) * 128],
                 lambda kt: xcT[:, kt * G:(kt + 1) * G],
                 8, lambda kt: xcT_keys + [("wq", kt)], tag=f"pq{pr}")
            for cc in range(2):
                p.op("dve", lambda h, ps=ps, pr=pr, cc=cc: h.tensor_tensor(
                    qhT[:, pr * G + cc * 128:pr * G + (cc + 1) * 128],
                    ps[:, cc * 128:(cc + 1) * 128],
                    gq[:, pr * 128:(pr + 1) * 128], ALU.mult),
                    reads=[pk, ("gq",)], writes=[("qhTb", pr, cc)], tag="evq")
            ps2, pk2 = PS[2 + pr % 2], PSK[2 + pr % 2]
            proj(ps2[:, 0:G], pk2,
                 lambda kt, pr=pr: ws["wk"][:, kt * D + pr * 128:kt * D + (pr + 1) * 128],
                 lambda kt: xcT[:, kt * G:(kt + 1) * G],
                 8, lambda kt: xcT_keys + [("wk", kt)], tag=f"pkT{pr}")
            for cc in range(2):
                p.op("dve", lambda h, ps2=ps2, pr=pr, cc=cc: h.tensor_tensor(
                    khT[:, pr * G + cc * 128:pr * G + (cc + 1) * 128],
                    ps2[:, cc * 128:(cc + 1) * 128],
                    gk[:, pr * 128:(pr + 1) * 128], ALU.mult),
                    reads=[pk2, ("gk",)], writes=[("khTb", pr, cc)], tag="evk")

        for par in range(2):
            ci = ck0 + par
            for pr in range(NP):
                bank, bkey = PT[pr % 2], PTK[pr % 2]
                p.op("pe", lambda h, pr=pr, par=par, bank=bank: h.transpose(
                    bank[:, 0:128],
                    khT[:, pr * G + par * 128:pr * G + par * 128 + 128],
                    ident[:]),
                    reads=[("khTb", pr, par), ("ident",)], writes=[bkey],
                    tag="ktr")
                p.op("act", lambda h, pr=pr, par=par, bank=bank: h.activation(
                    khn[:, par * D + pr * 128:par * D + (pr + 1) * 128],
                    bank[:, 0:128], AF.Copy),
                    reads=[bkey], writes=[("khnb", par, pr // 4)], tag="kte")
            for half in range(2):
                ps2, pk2 = PS[5], PSK[5]
                proj(ps2[:, 0:512], pk2,
                     lambda kt, par=par: xcT[:, kt * G + par * 128:kt * G + par * 128 + 128],
                     lambda kt, half=half: ws["wv"][:, kt * D + half * 512:kt * D + (half + 1) * 512],
                     8, lambda kt: xcT_keys + [("wv", kt)], tag=f"nv{ci}")
                p.op("act", lambda h, ps2=ps2, half=half, par=par: h.activation(
                    vno[:, par * D + half * 512:par * D + (half + 1) * 512],
                    ps2[:, 0:512], AF.Copy),
                    reads=[pk2], writes=[("vnob", par, half)], tag="nev")
                ps3, pk3 = PS[half], PSK[half]
                proj(ps3[:, 0:512], pk3,
                     lambda kt, par=par: xcT[:, kt * G + par * 128:kt * G + par * 128 + 128],
                     lambda kt, half=half: ws["wg"][:, kt * D + half * 512:kt * D + (half + 1) * 512],
                     8, lambda kt: xcT_keys + [("wg", kt)], tag=f"ng{ci}")
                p.op("act", lambda h, ps3=ps3, half=half, par=par: h.activation(
                    sg[:, par * D + half * 512:par * D + half * 512 + 512],
                    ps3[:, 0:512], AF.Silu),
                    reads=[pk3], writes=[("sgb", par, half)], tag="neg")

        if g + 1 < NT // 2:
            emit_ln1(g + 1, 2)

        for par in range(2):
            ci = ck0 + par
            v_c = vno[:, par * D:(par + 1) * D]
            kc_keys = [("khnb", par, 0), ("khnb", par, 1)]
            vc_keys = [("vnob", par, 0), ("vnob", par, 1)]
            PO, POK = PS[4], PSK[4]
            for pr in range(NP):
                qsl = qhT[:, pr * G + par * 128:pr * G + par * 128 + 128]
                qkey = ("qhTb", pr, par)
                ksl = khT[:, pr * G + par * 128:pr * G + par * 128 + 128]
                kkey = ("khTb", pr, par)
                for hh in range(2):
                    PA, PAK = PS[hh], PSK[hh]
                    p.op("pe", lambda h, PA=PA, hh=hh, qsl=qsl, ksl=ksl: h.matmul(
                        PA[:, 0:128], ksl[64 * hh:64 * (hh + 1), :],
                        qsl[64 * hh:64 * (hh + 1), :], start=True, stop=True,
                        tile_position=(64 * hh, 0)),
                        reads=[qkey, kkey], writes=[PAK], tag="mmA")
                    p.op("dve", lambda h, PA=PA, hh=hh: h.tensor_tensor(
                        aT[:, hh * 128:(hh + 1) * 128], PA[:, 0:128],
                        mskT[:], ALU.mult),
                        reads=[PAK, ("mskT",)], writes=[("aTb", hh)], tag="mask")
                for hh in range(2):
                    hcol = (2 * pr + hh) * HD
                    p.op("pe", lambda h, PO=PO, hh=hh, hcol=hcol, v_c=v_c: h.matmul(
                        PO[:, hh * 64:(hh + 1) * 64],
                        aT[:, hh * 128:(hh + 1) * 128],
                        v_c[:, hcol:hcol + HD], start=(hh == 0), stop=False),
                        reads=[("aTb", hh)] + vc_keys + [POK], writes=[POK],
                        tag="mmI")
                    p.op("pe", lambda h, PO=PO, hh=hh, pr=pr, qsl=qsl: h.matmul(
                        PO[:, hh * 64:(hh + 1) * 64],
                        qsl[64 * hh:64 * (hh + 1), :],
                        rbf[64 * hh:64 * (hh + 1), pr * HD:(pr + 1) * HD],
                        start=False, stop=(hh == 1), tile_position=(64 * hh, 0)),
                        reads=[qkey, ("rbf",), POK], writes=[POK], tag="mmX")
                p.op("act", lambda h, PO=PO, pr=pr: h.activation(
                    ro[:, pr * 128:(pr + 1) * 128], PO[:, 0:128], AF.Copy),
                    reads=[POK], writes=[("rob", pr)], tag="evo")
            state_update(khn[:, par * D:(par + 1) * D], v_c,
                         kc_keys + vc_keys, tag=f"A{ci}")
            p.op("dve", lambda h: h.tensor_copy(rbf[:], rcur[:]),
                reads=[("rcur",)], writes=[("rbf",)], tag="rbfA")

            ro_keys = [("rob", pr) for pr in range(NP)]
            cs = ln_stats(ro[:], HD, EPSH, ro_keys, nst=16, tag=f"hl{ci}")
            p.op("dve", lambda h, cs=cs: h.tensor_tensor(
                scrg[:, 0:D].rearrange("p (g w) -> p g w", w=HD),
                ro[:].rearrange("p (g w) -> p g w", w=HD),
                stv(2, cs)[:, :, None].broadcast_to((128, 16, HD)),
                ALU.subtract),
                reads=ro_keys + [ST], writes=GKS, tag="gnt1")
            p.op("dve", lambda h, par=par, cs=cs: h.tensor_tensor(
                zzb[:, par * D:(par + 1) * D].rearrange("p (g w) -> p g w", w=HD),
                sg[:, par * D:(par + 1) * D].rearrange("p (g w) -> p g w", w=HD),
                stv(6, cs)[:, :, None].broadcast_to((128, 16, HD)), ALU.mult),
                reads=[("sgb", par, 0), ("sgb", par, 1), ST],
                writes=[("zzbb", par)], tag="gnz")
            p.op("dve", lambda h, par=par, xrb=xrb: h.tensor_tensor(
                xrb[:, par * D:(par + 1) * D], scrg[:, 0:D],
                zzb[:, par * D:(par + 1) * D], ALU.mult),
                reads=GKS + [("zzbb", par)], writes=[("xrb", gp, par)],
                tag="gnxr")
            transpose_tile(xrb[:, par * D:(par + 1) * D], [("xrb", gp, par)],
                           lambda kt, par=par: xrT[:, par * D + kt * 128:par * D + (kt + 1) * 128],
                           lambda kt, par=par: ("xrTb", par, kt), tag=f"xr{ci}")
            p.op("dve", lambda h, par=par, xb=xb: h.scalar_tensor_tensor(
                scrh[:, 0:D], xb[:, par * D:(par + 1) * D], 2.0, bout[:],
                ALU.mult, ALU.add),
                reads=[("x0b", gp, par), ("bout",)], writes=HKS, tag="x0b2")
            for half in range(2):
                ps, pk = PS[half], PSK[half]
                proj(ps[:, 0:512], pk,
                     lambda kt, par=par: xrT[:, par * D + kt * 128:par * D + (kt + 1) * 128],
                     lambda kt, half=half: ws["wo"][:, kt * D + half * 512:kt * D + (half + 1) * 512],
                     8, lambda kt, par=par: [("xrTb", par, kt), ("wo", kt)],
                     tag=f"po{ci}")
                sl = slice(half * 512, (half + 1) * 512)
                p.op("dve", lambda h, ps=ps, sl=sl, par=par, xb=xb: h.tensor_tensor(
                    x1b[:, par * D:(par + 1) * D][:, sl], ps[:, 0:512],
                    xb[:, par * D:(par + 1) * D][:, sl], ALU.add),
                    reads=[pk, ("x0b", gp, par)], writes=[("x1bb", par, half)],
                    tag="evx1")
                p.op("dve", lambda h, ps=ps, sl=sl, par=par: h.tensor_tensor(
                    xab[:, par * D:(par + 1) * D][:, sl], ps[:, 0:512],
                    scrh[:, 0:D][:, sl], ALU.add),
                    reads=[pk] + HKS, writes=[("xabb", par, half)],
                    tag="evxa")
            p.dma("sp1", "sp",
                  lambda h, ci=ci, par=par: h.dma_start(
                      out=spill_x1[ci * 128:(ci + 1) * 128, :],
                      in_=x1b[:, par * D:(par + 1) * D]),
                  reads=[("x1bb", par, 0), ("x1bb", par, 1)],
                  writes=[("sx1", ci)], tag="spx1")
            p.dma("sp1", "sp",
                  lambda h, ci=ci, par=par: h.dma_start(
                      out=spill_xa[ci * 128:(ci + 1) * 128, :],
                      in_=xab[:, par * D:(par + 1) * D]),
                  reads=[("xabb", par, 0), ("xabb", par, 1)],
                  writes=[("sxa", ci)], tag="spxa")

    p.fence("AB")
    for kt in range(8):
        ld("w", "pool", w_in_s[:, kt * WIN:(kt + 1) * WIN],
           w_in_h[kt * 128:(kt + 1) * 128, :], [("w_in", kt)])
    for kt in range(22):
        c0, cw = ACT_TILES[kt]
        ld("w", "pool", w_out_s[0:cw, kt * D:(kt + 1) * D],
           w_out_h[c0:c0 + cw, :], [("w_out", kt)])

    for g in range(NT // 2):
        ck0 = 2 * g
        for par in range(2):
            ci = ck0 + par
            ld("w", "pool", x1g[:, par * D:(par + 1) * D],
               spill_x1[ci * 128:(ci + 1) * 128, :], [("x1gb", par)],
               reads=[("sx1", ci)])
            ld("w", "pool", xag[:, par * D:(par + 1) * D],
               spill_xa[ci * 128:(ci + 1) * 128, :], [("xagb", par)],
               reads=[("sxa", ci)])
            ln_chunk(x1g[:, par * D:(par + 1) * D], [("x1gb", par)], EPS1,
                     (ln2w, ("ln2w",), ln2b, ("ln2b",)),
                     z1b[:, par * D:(par + 1) * D], ("z1bb", par),
                     lambda kt, par=par: xc2T[:, kt * GB + par * 128:kt * GB + par * 128 + 128],
                     lambda kt, par=par: ("x2Tb", par, kt), tag=f"B{ci}")
        x2T_keys = [("x2Tb", pp, kt) for pp in range(2) for kt in range(8)]

        for ct in range(22):
            c0a, cw = ACT_TILES[ct]
            psA, pkA = PS[ct % 2], PSK[ct % 2]
            psB, pkB = PS[2 + ct % 2], PSK[2 + ct % 2]
            proj(psA[0:cw, 0:GB], pkA,
                 lambda kt, c0a=c0a, cw=cw: w_in_s[:, kt * WIN + c0a:kt * WIN + c0a + cw],
                 lambda kt: xc2T[:, kt * GB:(kt + 1) * GB],
                 8, lambda kt: x2T_keys + [("w_in", kt)], tag=f"pA{ct}")
            proj(psB[0:cw, 0:GB], pkB,
                 lambda kt, c0a=c0a, cw=cw: w_in_s[:, kt * WIN + HID + c0a:kt * WIN + HID + c0a + cw],
                 lambda kt: xc2T[:, kt * GB:(kt + 1) * GB],
                 8, lambda kt: x2T_keys + [("w_in", kt)], tag=f"pB{ct}")
            p.op("act", lambda h, psA=psA, cw=cw, ct=ct: h.activation(
                abA[0:cw, :], psA[0:cw, 0:GB], AF.Identity,
                bias=bina[0:cw, ct:ct + 1]),
                reads=[pkA, ("bina",)], writes=[("abA",)], tag="evA")
            p.op("act", lambda h, psB=psB, cw=cw, ct=ct: h.activation(
                abB[0:cw, :], psB[0:cw, 0:GB], AF.Silu,
                bias=binb[0:cw, ct:ct + 1]),
                reads=[pkB, ("binb",)], writes=[("abB",)], tag="evB")
            p.op("dve", lambda h, cw=cw, ct=ct: h.tensor_tensor(
                hT[0:cw, ct * GB:(ct + 1) * GB], abA[0:cw, :], abB[0:cw, :],
                ALU.mult),
                reads=[("abA",), ("abB",)], writes=[("hTb", ct)], tag="hTm")

        for par in range(2):
            ci = ck0 + par
            for half in range(2):
                ps, pk = PS[4 + half], PSK[4 + half]
                def lf(kt, par=par):
                    _, cw = ACT_TILES[kt]
                    return hT[0:cw, kt * GB + par * 128:kt * GB + par * 128 + 128]
                def rf(kt, half=half):
                    _, cw = ACT_TILES[kt]
                    return w_out_s[0:cw, kt * D + half * 512:kt * D + (half + 1) * 512]
                proj(ps[:, 0:512], pk, lf, rf, 22,
                     lambda kt: [("hTb", kt), ("w_out", kt)], tag=f"pw2{ci}")
                sl = slice(half * 512, (half + 1) * 512)
                p.op("dve", lambda h, ps=ps, sl=sl, par=par: h.tensor_tensor(
                    scrg[:, 0:D][:, sl], ps[:, 0:512],
                    xag[:, par * D:(par + 1) * D][:, sl], ALU.add),
                    reads=[pk, ("xagb", par)], writes=GKS, tag="z1in")
            cs = ln_stats(scrg[:, 0:D], D, EPS1, GKS, tag=f"p1{ci}")
            p.op("dve", lambda h, cs=cs, par=par: h.tensor_scalar(
                z1b[:, par * D:(par + 1) * D], scrg[:, 0:D],
                stv(2, cs), stv(6, cs), ALU.subtract, ALU.mult),
                reads=GKS + [ST], writes=[("z1bb", par)], tag="z1ap")
            p.dma("sp1", "sp",
                  lambda h, ci=ci, par=par: h.dma_start(
                      out=spill_z1[ci * 128:(ci + 1) * 128, :],
                      in_=z1b[:, par * D:(par + 1) * D]),
                  reads=[("z1bb", par)], writes=[("sz1", ci)], tag="spz1")

    p.fence("BD")
    for kt in range(8):
        ld("w", "pool", wgq[:, kt * FFD:(kt + 1) * FFD],
           agout_wg[kt * 128:(kt + 1) * 128, :], [("wgq", kt)],
           reads=[("ag", "wg")], tag="lwgq")
        ld("w", "pool", wuq[:, kt * FFD:(kt + 1) * FFD],
           agout_wu[kt * 128:(kt + 1) * 128, :], [("wuq", kt)],
           reads=[("ag", "wu")], tag="lwuq")
    for kt in range(22):
        r_, pos = (kt // 3, kt % 3) if kt <= 20 else (7, 2)
        srow = r_ * 384 + pos * 128
        ld("w", "pool", wdq[:, kt * D:(kt + 1) * D],
           agout_wd[srow:srow + 128, :], [("wdq", kt)],
           reads=[("ag", "wd")], tag="lwdq")
    def hq_tiles(sl_i):
        return list(range(11)) if sl_i == 0 else list(range(11, 22))

    def stage_x(ci):
        par = ci % 2
        zsl = z1g[:, par * D:(par + 1) * D]
        ld("w", "pool", zsl, spill_z1[ci * 128:(ci + 1) * 128, :],
           [("z1gb", par)], reads=[("sz1", ci)])
        quant_tile(zsl, [("z1gb", par)], D, xq[:, 0:D], ("xq",), tag="x",
                   nsl=1)

    def stage_a(ci):
        transpose_tile(xq[:, 0:D], [("xq",)],
                       lambda kt: xqT[:, kt * 128:(kt + 1) * 128],
                       lambda kt: ("xqT",), tag=f"xq{ci}")
        for nm, wbuf, wname, obuf, obkey, func, pb in [
            ("gm", wgq, "wgq", sgt, ("sgt",), AF.Silu, 0),
            ("um", wuq, "wuq", sut, ("sut",), AF.Copy, 2),
        ]:
            for nc_i in range(6):
                n0 = nc_i * 512
                nw = min(512, FFD - n0)
                ps, pk = PS[pb + nc_i % 2], PSK[pb + nc_i % 2]
                proj(ps[:, 0:nw], pk,
                     lambda kt: xqT[:, kt * 128:(kt + 1) * 128],
                     lambda kt, n0=n0, nw=nw, wbuf=wbuf: wbuf[:, kt * FFD + n0:kt * FFD + n0 + nw],
                     8, lambda kt, wname=wname: [("xqT",), (wname, kt)],
                     tag=f"{nm}{nc_i}")
                p.op("act", lambda h, ps=ps, n0=n0, nw=nw, obuf=obuf,
                     func=func: h.activation(
                         obuf[:, n0:n0 + nw], ps[:, 0:nw], func),
                     reads=[pk], writes=[obkey], tag=nm + "ev")

    def stage_b(ci):
        p.op("dve", lambda h: h.tensor_tensor(
            sgt[:, 0:FFD], sgt[:, 0:FFD], sut[:, 0:FFD], ALU.mult),
            reads=[("sgt",), ("sut",)], writes=[("sgt",)], tag="hff")
        quant_tile(sgt[:, 0:FFD], [("sgt",)], FFD, hqT[:, 0:FFD],
                   ("hqTp",), tag="h",
                   out_wkeys_fn=lambda sl_i: [("hqT", t)
                                              for t in hq_tiles(sl_i)])

    def stage_c1(ci):
        for grp in range(6):
            nts = list(range(grp * 4, min(22, grp * 4 + 4)))
            bank, bkey = PT[grp % 2], PTK[grp % 2]
            for j, ntl in enumerate(nts):
                p.op("pe", lambda h, j=j, ntl=ntl, bank=bank: h.transpose(
                    bank[:, j * 128:(j + 1) * 128],
                    hqT[:, ntl * 128:(ntl + 1) * 128], ident[:]),
                    reads=[("hqT", ntl), ("ident",)], writes=[bkey],
                    tag="trh")
            for j, ntl in enumerate(nts):
                p.op("act", lambda h, j=j, ntl=ntl, bank=bank: h.activation(
                    hqT[:, ntl * 128:(ntl + 1) * 128],
                    bank[:, j * 128:(j + 1) * 128], AF.Copy),
                    reads=[bkey], writes=[("hqT", ntl)], tag="trhev")
        for half in range(2):
            ps, pk = PS[4 + half], PSK[4 + half]
            proj(ps[:, 0:512], pk,
                 lambda kt: hqT[:, kt * 128:(kt + 1) * 128],
                 lambda kt, half=half: wdq[:, kt * D + half * 512:kt * D + (half + 1) * 512],
                 22, lambda kt: [("hqT", kt), ("wdq", kt)], tag=f"dm{half}")

    def stage_c2(ci):
        par = ci % 2
        zsl = z1g[:, par * D:(par + 1) * D]
        for half in range(2):
            ps, pk = PS[4 + half], PSK[4 + half]
            sl = slice(half * 512, (half + 1) * 512)
            p.op("dve", lambda h, ps=ps, sl=sl, zsl=zsl: h.tensor_tensor(
                outf[:, sl], ps[:, 0:512], zsl[:, sl], ALU.add),
                reads=[pk, ("z1gb", par)], writes=[("outfb",)], tag="z2in")
        cs = ln_stats(outf[:], D, EPS1, [("outfb",)], tag=f"p2{ci}",
                      sq_ap=xqT[:, 0:D], sq_keys=[("xqT",)])
        p.op("dve", lambda h, cs=cs: h.tensor_scalar(
            outf[:], outf[:], stv(2, cs), stv(6, cs),
            ALU.subtract, ALU.mult),
            reads=[("outfb",), ST], writes=[("outfb",)], tag="outap")
        p.dma("out", "sp",
              lambda h, ci=ci: h.dma_start(
                  out=y[ci * 128:(ci + 1) * 128, :], in_=outf[:]),
              reads=[("outfb",)], writes=[("y", ci)], tag="sty")

    stage_x(0)
    stage_a(0)
    stage_x(1)
    stage_b(0)
    for ci in range(NT):
        if ci + 1 < NT:
            stage_a(ci + 1)
        stage_c1(ci)
        if ci + 1 < NT:
            stage_b(ci + 1)
        stage_c2(ci)
        if ci + 2 < NT:
            stage_x(ci + 2)

    p.op("sp", lambda h: None, reads=[("y", ci) for ci in range(NT)], tag="fin")
    p.emit()
    return nc


_NC_CACHE = [None]


def _consts():
    g = np.array(GAMMA, dtype=np.float64)
    idx = np.arange(128, dtype=np.float64)
    gq = np.zeros((128, NP * 128))
    gk = np.zeros((128, NP * 128))
    gcv = np.zeros((128, NP))
    for pr in range(NP):
        for r in range(128):
            hh = 2 * pr + (1 if r >= 64 else 0)
            gq[r, pr * 128:(pr + 1) * 128] = 0.125 * g[hh] ** idx
            gk[r, pr * 128:(pr + 1) * 128] = g[hh] ** (-idx)
            gcv[r, pr] = g[hh] ** 128
    gkn = np.zeros((128, D))
    for col in range(D):
        gkn[:, col] = g[col // HD] ** (-idx)
    mskT = (idx[None, :] >= idx[:, None]).astype(np.float64)
    ident = np.eye(128)
    out = {k: np.ascontiguousarray(v, dtype=np.float32) for k, v in
           dict(mskT=mskT, ident=ident).items()}
    for name, v in dict(gq=gq, gk=gk, gkn=gkn, gc=gcv).items():
        hi = v.astype(np.float32)
        lo = (v - hi.astype(np.float64)).astype(np.float32)
        out[name + "_hi"] = np.ascontiguousarray(hi)
        out[name + "_lo"] = np.ascontiguousarray(lo)
    return out


def kernel(**inputs):
    f = np.float32
    x = np.ascontiguousarray(np.asarray(inputs["L0_attn_x"], f))
    B, S, _ = x.shape

    def cols8(v):
        return np.ascontiguousarray(np.asarray(v, f).reshape(8, 128).T)

    b_in = np.asarray(inputs["L0_attn_b_in"], f)
    b_in_a = np.zeros((22 * 128,), f)
    b_in_a[:HID] = b_in[:HID]
    b_in_b = np.zeros((22 * 128,), f)
    b_in_b[:HID] = b_in[HID:]
    common = {
        "ln1w": cols8(inputs["L0_attn_ln1_w"]),
        "ln1b": cols8(inputs["L0_attn_ln1_b"]),
        "ln2w": cols8(inputs["L0_attn_ln2_w"]),
        "ln2b": cols8(inputs["L0_attn_ln2_b"]),
        "wq": np.ascontiguousarray(np.asarray(inputs["L0_attn_wq"], f)),
        "wk": np.ascontiguousarray(np.asarray(inputs["L0_attn_wk"], f)),
        "wv": np.ascontiguousarray(np.asarray(inputs["L0_attn_wv"], f)),
        "wg": np.ascontiguousarray(np.asarray(inputs["L0_attn_wg"], f)),
        "wo": np.ascontiguousarray(np.asarray(inputs["L0_attn_wo"], f)),
        "w_in": np.ascontiguousarray(np.asarray(inputs["L0_attn_w_in"], f)),
        "w_out": np.ascontiguousarray(np.asarray(inputs["L0_attn_w_out"], f)),
        "b_in_a": np.ascontiguousarray(b_in_a.reshape(22, 128).T),
        "b_in_b": np.ascontiguousarray(b_in_b.reshape(22, 128).T),
        "b_out_bc": np.ascontiguousarray(np.broadcast_to(
            np.asarray(inputs["L0_attn_b_out"], f), (128, D))),
        **_consts(),
    }
    w_gate = np.asarray(inputs["L0_ffn_w_gate"], f)
    w_up = np.asarray(inputs["L0_ffn_w_up"], f)
    w_down = np.asarray(inputs["L0_ffn_w_down"], f)

    if _NC_CACHE[0] is None:
        _NC_CACHE[0] = build_nc()
    nc = _NC_CACHE[0]

    in_maps = []
    for c in range(8):
        b, hf = c // 2, c % 2
        m = dict(common)
        m["x_loc"] = np.ascontiguousarray(x[b, hf * SC:(hf + 1) * SC, :])
        m["x_rem"] = np.ascontiguousarray(x[b, 0:SC, :])
        m["flag"] = np.full((128, 1), float(hf), f)
        m["wg_shard"] = np.ascontiguousarray(w_gate[:, 128 * c:128 * (c + 1)])
        m["wu_shard"] = np.ascontiguousarray(w_up[:, 128 * c:128 * (c + 1)])
        k0 = 384 * c if c < 7 else 2816 - 384
        m["wd_shard"] = np.ascontiguousarray(w_down[:, k0:k0 + 384])
        in_maps.append(m)

    res = run_bass_kernel_spmd(nc, in_maps, core_ids=list(range(8)))
    out = np.empty((B, S, D), f)
    for c in range(8):
        b, hf = c // 2, c % 2
        out[b, hf * SC:(hf + 1) * SC, :] = res.results[c]["y"]
    return out


# version 15
# speedup vs baseline: 1.0080x; 1.0080x over previous
import numpy as np
import ml_dtypes
import concourse.bass as bass
import concourse.mybir as mybir
from concourse.bass_utils import run_bass_kernel_spmd

F32 = mybir.dt.float32
BF16 = mybir.dt.bfloat16
AF = mybir.ActivationFunctionType
ALU = mybir.AluOpType
AX = mybir.AxisListType.X

D = 1024
SC = 2048
NT = SC // 128
NH = 16
HD = 64
NP = 8
HID = 2730
WIN = 2 * HID
FFD = 2816
GAMMA = [float(1.0 - 2.0 ** (-5.0 - h)) for h in range(NH)]
EPS1 = 1e-5
EPSH = 1e-6
ACT_TILES = [(ct * 128, min(128, HID - ct * 128)) for ct in range(22)]
SBUF_TOP = 229344


class Prog:
    ENGINES = ("pe", "act", "dve", "sp", "pool")

    def __init__(self, nc):
        self.nc = nc
        self.ops = {e: [] for e in self.ENGINES}
        self.res = {}
        self.streams = {}
        self.fence_tok = None

    def op(self, eng, fn, reads=(), writes=(), tag=""):
        deps = set()
        for k in reads:
            r = self.res.get(k)
            if r is None:
                raise KeyError(f"read before write: {k!r} (op {tag})")
            deps.add(r["w"])
        for k in writes:
            r = self.res.get(k)
            if r is not None:
                deps.add(r["w"])
                deps.update(r["r"])
            elif self.fence_tok is not None:
                deps.add(self.fence_tok)
        idx = len(self.ops[eng])
        tok = ("e", eng, idx)
        self.ops[eng].append({"fn": fn, "deps": deps, "dma": None, "tag": tag})
        for k in reads:
            self.res[k]["r"].append(tok)
        for k in writes:
            self.res[k] = {"w": tok, "r": []}
        return tok

    def dma(self, stream, eng, fn, reads=(), writes=(), tag=""):
        s = self.streams.setdefault(stream, {"engine": eng, "n": 0})
        assert s["engine"] == eng
        deps = set()
        for k in reads:
            r = self.res.get(k)
            if r is not None:
                deps.add(r["w"])
        for k in writes:
            r = self.res.get(k)
            if r is not None:
                deps.add(r["w"])
                deps.update(r["r"])
            elif self.fence_tok is not None:
                deps.add(self.fence_tok)
        s["n"] += 1
        tok = ("d", stream, s["n"])
        self.ops[eng].append({"fn": fn, "deps": deps, "dma": (stream, s["n"]),
                              "tag": tag})
        for k in reads:
            if k in self.res:
                self.res[k]["r"].append(tok)
        for k in writes:
            self.res[k] = {"w": tok, "r": []}
        return tok

    def fence(self, tag=""):
        alldeps = set()
        for r in self.res.values():
            alldeps.add(r["w"])
            alldeps.update(r["r"])
        toks = []
        for e in self.ENGINES:
            if not self.ops[e]:
                continue
            idx = len(self.ops[e])
            toks.append(("e", e, idx))
            self.ops[e].append({"fn": lambda h: None, "deps": set(alldeps),
                                "dma": None, "tag": f"fence_{tag}"})
        master = ("e", "sp", len(self.ops["sp"]))
        self.ops["sp"].append({"fn": lambda h: None, "deps": set(toks),
                               "dma": None, "tag": f"fencem_{tag}"})
        for k in self.res:
            self.res[k] = {"w": master, "r": []}
        self.fence_tok = master

    NSLOT = 8

    def emit(self):
        nc = self.nc
        pub = {e: [False] * len(self.ops[e]) for e in self.ENGINES}
        for e in self.ENGINES:
            for o in self.ops[e]:
                for d in o["deps"]:
                    if d[0] == "e" and not (d[1] == e == "pe"):
                        pub[d[1]][d[2]] = True
        cum = {}
        for e in self.ENGINES:
            c, lst = 0, []
            for f in pub[e]:
                c += int(f)
                lst.append(c)
            cum[e] = lst

        ctxs, sems, ssems = [], {}, {}
        for e in self.ENGINES:
            if any(pub[e]):
                cm = nc.semaphore(f"sem_{e}")
                sems[e] = cm.__enter__()
                ctxs.append(cm)
        for sname in self.streams:
            lst = []
            for sl in range(self.NSLOT):
                cm = nc.semaphore(f"dma_{sname}{sl}")
                lst.append(cm.__enter__())
                ctxs.append(cm)
            ssems[sname] = lst
        blk_cm = nc.Block()
        block = blk_cm.__enter__()
        ctxs.append(blk_cm)

        eng_attr = {"pe": "tensor", "act": "scalar", "dve": "vector",
                    "sp": "sync", "pool": "gpsimd"}

        def slotcyc(n):
            return (n - 1) % self.NSLOT, (n - 1) // self.NSLOT + 1

        def make_body(e):
            def body(handle):
                waited = {}
                for i, o in enumerate(self.ops[e]):
                    deps = []
                    for d in sorted(o["deps"], key=repr):
                        if d[0] == "e":
                            if d[1] == e == "pe":
                                continue
                            deps.append((sems[d[1]], cum[d[1]][d[2]]))
                        else:
                            sl, cyc = slotcyc(d[2])
                            deps.append((ssems[d[1]][sl], 16 * cyc))
                    if o["dma"] is not None:
                        sname, n = o["dma"]
                        sl, cyc = slotcyc(n)
                        if cyc > 1:
                            deps.append((ssems[sname][sl], 16 * (cyc - 1)))
                    for sem, val in deps:
                        if val > 0 and waited.get(sem.name, -1) < val:
                            handle.wait_ge(sem, val)
                            waited[sem.name] = val
                    inst = o["fn"](handle)
                    if o["dma"] is not None:
                        sname, n = o["dma"]
                        sl, cyc = slotcyc(n)
                        inst.then_inc(ssems[sname][sl], 16)
                    elif pub[e][i]:
                        if inst is None:
                            inst = handle.nop()
                        inst.then_inc(sems[e], 1)
            return body

        for e in self.ENGINES:
            if self.ops[e]:
                getattr(block, eng_attr[e])(make_body(e))
        for cm in reversed(ctxs):
            cm.__exit__(None, None, None)


def build_nc():
    nc = bass.Bass("TRN2", target_bir_lowering=False)
    p = Prog(nc)

    def din(name, shape):
        return nc.dram_tensor(name, shape, F32, kind="ExternalInput").ap()

    x_loc = din("x_loc", [SC, D])
    x_rem = din("x_rem", [SC, D])
    ln1w_h = din("ln1w", [128, 8])
    ln1b_h = din("ln1b", [128, 8])
    ln2w_h = din("ln2w", [128, 8])
    ln2b_h = din("ln2b", [128, 8])
    w_h = {n: din(n, [D, D]) for n in ("wq", "wk", "wv", "wg", "wo")}
    w_in_h = din("w_in", [D, WIN])
    w_out_h = din("w_out", [HID, D])
    b_in_a_h = din("b_in_a", [128, 22])
    b_in_b_h = din("b_in_b", [128, 22])
    b_out_h = din("b_out_bc", [128, D])
    wgsh_h = din("wg_shard", [FFD, 128])
    wush_h = din("wu_shard", [FFD, 128])
    wdsh_h = din("wd_shard", [D, 3 * 128])
    agin_wg = nc.dram_tensor("agin_wg", [128, FFD], F32, kind="Internal")
    agin_wu = nc.dram_tensor("agin_wu", [128, FFD], F32, kind="Internal")
    agin_wd = nc.dram_tensor("agin_wd", [384, D], F32, kind="Internal")
    agout_wg = nc.dram_tensor("agout_wg", [1024, FFD], F32, kind="Internal")
    agout_wu = nc.dram_tensor("agout_wu", [1024, FFD], F32, kind="Internal")
    agout_wd = nc.dram_tensor("agout_wd", [3072, D], F32, kind="Internal")
    gqhl_h = [din(f"gq_{s}", [128, NP * 128]) for s in ("hi", "lo")]
    gkhl_h = [din(f"gk_{s}", [128, NP * 128]) for s in ("hi", "lo")]
    gknhl_h = [din(f"gkn_{s}", [128, D]) for s in ("hi", "lo")]
    gchl_h = [din(f"gc_{s}", [128, NP]) for s in ("hi", "lo")]
    mskT_h = din("mskT", [128, 128])
    ident_h = din("ident", [128, 128])
    flag_h = din("flag", [128, 1])

    y = nc.dram_tensor("y", [SC, D], F32, kind="ExternalOutput").ap()
    spill_x1 = nc.dram_tensor("spill_x1", [SC, D], BF16, kind="Internal").ap()
    spill_xa = nc.dram_tensor("spill_xa", [SC, D], BF16, kind="Internal").ap()
    spill_z1 = nc.dram_tensor("spill_z1", [SC, D], BF16, kind="Internal").ap()

    sb = nc.alloc_sbuf_tensor
    ident = sb("ident_sb", [128, 128], BF16)
    gc = sb("gc_sb", [128, NP], F32)
    ln1w = sb("ln1w_s", [128, 8], F32)
    ln1b = sb("ln1b_s", [128, 8], F32)
    ln2w = sb("ln2w_s", [128, 8], F32)
    ln2b = sb("ln2b_s", [128, 8], F32)
    bina = sb("bina_s", [128, 22], F32)
    binb = sb("binb_s", [128, 22], F32)
    bout = sb("bout_s", [128, D], BF16)
    flag = sb("flag_s", [128, 1], F32)
    WMAX = 3072
    scrf = sb("scrf", [128, WMAX], F32)
    scrg = sb("scrg", [128, WMAX], F32)
    scrh = sb("scrh", [128, WMAX], F32)
    scrp = sb("scrp", [128, WMAX], BF16)
    st = sb("st_all", [128, 14 * 16], F32)
    NB = 192
    qm = sb("qm_all", [128, 2 * NB], F32)

    ARENA = 164352
    base = SBUF_TOP - ARENA
    assert nc.sbuf_bytes_remaining >= ARENA + 64, nc.sbuf_bytes_remaining

    def at(name, off, shape, dtype):
        return nc.alloc_sbuf_tensor_at(name, shape, dtype, offset=base + off)

    ws = {}
    for i, n in enumerate(("wq", "wk", "wv", "wg", "wo")):
        ws[n] = at(n + "_s", i * 16384, [128, 8 * D], BF16)
    x0 = at("x0", 81920, [128, 2 * D], BF16)
    xrm = at("xrm", 86016, [128, 2 * D], BF16)
    G = 256
    xcT = at("xcT", 90112, [128, 8 * G], BF16)
    qhT = at("qhT", 94208, [128, NP * G], BF16)
    khT = at("khT", 98304, [128, NP * G], BF16)
    khn = at("khn", 102400, [128, 2 * D], BF16)
    vno = at("vno", 106496, [128, 2 * D], BF16)
    sg = at("sg", 110592, [128, 2 * D], BF16)
    xr = at("xr", 114688, [128, 2 * D], BF16)
    xrT = at("xrT", 118784, [128, 2 * D], BF16)
    zzb = at("zzb", 122880, [128, 2 * D], BF16)
    aT = at("aT", 126976, [128, 2 * 128], BF16)
    ro = at("ro", 127488, [128, D], F32)
    rcur = at("rcur", 131584, [128, NP * HD], F32)
    rnxt = at("rnxt", 133632, [128, NP * HD], F32)
    rbf = at("rbf", 135680, [128, NP * HD], BF16)
    x1b = at("x1b", 136704, [128, 2 * D], BF16)
    xab = at("xab", 140800, [128, 2 * D], BF16)

    w_in_s = at("w_in_s", 0, [128, 8 * WIN], BF16)
    w_out_s = at("w_out_s", 87360, [128, 22 * D], BF16)
    GB = 256
    xc2T = at("xc2T", 132416, [128, 8 * GB], BF16)
    hT = at("hT", 136512, [128, 22 * GB], BF16)
    abA = at("abA", 147776, [128, GB], BF16)
    abB = at("abB", 148288, [128, GB], BF16)
    x1g = at("x1g", 148800, [128, 2 * D], BF16)
    xag = at("xag", 152896, [128, 2 * D], BF16)
    z1b = at("z1b", 156992, [128, 2 * D], BF16)

    wgq = at("wgq", 0, [128, 8 * FFD], BF16)
    wuq = at("wuq", 45056, [128, 8 * FFD], BF16)
    wdq = at("wdq", 90112, [128, 22 * D], BF16)
    hqT = at("hqT", 135168, [128, 22 * 128], BF16)
    stage = at("prepstage", 144896, [128, 3 * D], F32)
    x0B = at("x0B", 144896, [128, 2 * D], BF16)
    xrB = at("xrB", 148992, [128, 2 * D], BF16)
    sqB = at("sqB", 153088, [128, D], F32)
    gq = at("gq_sb", 157184, [128, NP * 128], BF16)
    gk = at("gk_sb", 159232, [128, NP * 128], BF16)
    gkn = at("gkn_sb", 161280, [128, D], BF16)
    mskT = at("mskT_sb", 163328, [128, 128], BF16)
    identf = at("identf_sb", 163840, [128, 128], F32)
    xq = at("xq", 140800, [128, D], BF16)
    xqT = at("xqT", 142848, [128, D], BF16)
    sgt = at("sgt", 144896, [128, FFD], BF16)
    sut = at("sut", 150528, [128, FFD], BF16)
    z1g = at("z1g", 156160, [128, 2 * D], BF16)
    outf = at("outf", 160256, [128, D], F32)

    PS = [nc.alloc_psum_tensor(f"PS{i}", [128, 512], F32) for i in range(6)]
    PT = [nc.alloc_psum_tensor(f"PT{i}", [128, 512], BF16) for i in range(2)]
    PSK = [("PS", i) for i in range(6)]
    PTK = [("PT", i) for i in range(2)]
    p.op("sp", lambda h: None, writes=PSK + PTK, tag="psinit")

    def stv(j, cs=slice(0, 1), so=0):
        return st[:, 16 * (so + j):16 * (so + j + 1)][:, cs]
    ST = ("st", 0)

    def qmv(j, nb):
        return qm[:, NB * j:NB * j + nb]
    QM = ("qm",)
    REGS = [0, 88, 96, 176]
    FKS = [("scrf", r) for r in REGS]
    GKS = [("scrg", r) for r in REGS]
    HKS = [("scrh", r) for r in REGS]
    QMS = [("qm", r) for r in REGS]
    QOFF = 3072
    scrfb = scrf.bitcast(BF16)
    scrgb = scrg.bitcast(BF16)
    scrhb = scrh.bitcast(BF16)
    QREGS = [200, 288, 296, 376]
    HKS_ALL = HKS + [("scrh", r) for r in QREGS]

    def ld(stream, eng, dst_ap, src_ap, writes, reads=(), tag="ld"):
        def fn(h, dst=dst_ap, src=src_ap):
            return h.dma_start(out=dst, in_=src)
        p.dma(stream, eng, fn, reads=reads, writes=writes, tag=tag)

    for dst, src, key in [
        (mskT[:], mskT_h[:], ("mskT",)), (ident[:], ident_h[:], ("ident",)),
        (identf[:], ident_h[:], ("identf",)),
        (bout[:], b_out_h[:], ("bout",)),
    ]:
        ld("w", "pool", dst, src, [key])
    for i, (hl, dst, key, w_) in enumerate([
        (gqhl_h, gq, ("gq",), NP * 128), (gkhl_h, gk, ("gk",), NP * 128),
        (gknhl_h, gkn, ("gkn",), D), (gchl_h, gc, ("gc",), NP),
    ]):
        ld("w", "pool", scrf[:, 0:w_], hl[0][:], FKS)
        ld("w", "pool", scrg[:, 0:w_], hl[1][:], GKS)
        p.op("dve", lambda h, dst=dst, w_=w_: h.tensor_tensor(
            dst[:, 0:w_], scrf[:, 0:w_], scrg[:, 0:w_], ALU.add),
            reads=FKS + GKS, writes=[key], tag=f"ghl{i}")
    for dst, src, key in [
        (ln1w[:], ln1w_h[:], ("ln1w",)), (ln1b[:], ln1b_h[:], ("ln1b",)),
        (ln2w[:], ln2w_h[:], ("ln2w",)), (ln2b[:], ln2b_h[:], ("ln2b",)),
        (bina[:], b_in_a_h[:], ("bina",)), (binb[:], b_in_b_h[:], ("binb",)),
        (flag[:], flag_h[:], ("flag",)),
    ]:
        ld("c", "sp", dst, src, [key])
    for nm in ("wq", "wk", "wv", "wg", "wo"):
        for kt in range(8):
            ld("w", "pool", ws[nm][:, kt * D:(kt + 1) * D],
               w_h[nm][kt * 128:(kt + 1) * 128, :], [(nm, kt)])

    def ln_stats(src_ap, W, eps, src_keys, nst=1, tag="", sq_ap=None,
                 sq_keys=None, so=0):
        if sq_ap is None:
            sq_ap, sq_keys = scrf[:, 0:W], FKS
        ST = ("st", so)
        cs = slice(0, nst)
        if nst == 1:
            p.op("dve", lambda h, s=src_ap: h.reduce_sum(
                stv(0, cs, so), s, axis=AX), reads=src_keys, writes=[ST],
                tag="lnrs" + tag)
            p.op("act", lambda h, s=src_ap: h.activation(
                sq_ap, s, AF.Square, accum_out=stv(1, cs, so)),
                reads=list(src_keys) + [ST], writes=[ST] + list(sq_keys),
                tag="lnsq" + tag)
        else:
            v3 = src_ap.rearrange("p (g w) -> p g w", w=W)
            p.op("dve", lambda h, s=v3: h.reduce_sum(
                stv(0, cs, so), s, axis=AX), reads=src_keys, writes=[ST],
                tag="lnrs" + tag)
            p.op("act", lambda h, s=src_ap: h.activation(
                scrf[:, 0:nst * W], s, AF.Square),
                reads=src_keys, writes=FKS, tag="lnsq" + tag)
            p.op("dve", lambda h: h.reduce_sum(
                stv(1, cs, so), scrf[:, 0:nst * W].rearrange(
                    "p (g w) -> p g w", w=W), axis=AX),
                reads=FKS + [ST], writes=[ST], tag="lnss" + tag)
        p.op("dve", lambda h: h.tensor_scalar(
            stv(2, cs, so), stv(0, cs, so), 1.0 / W, None, ALU.mult),
            reads=[ST], writes=[ST], tag="lnmu" + tag)
        p.op("dve", lambda h: h.tensor_tensor(
            stv(3, cs, so), stv(2, cs, so), stv(2, cs, so), ALU.mult),
            reads=[ST], writes=[ST], tag="lnm2" + tag)
        p.op("dve", lambda h: h.scalar_tensor_tensor(
            stv(4, cs, so), stv(1, cs, so), 1.0 / W, stv(3, cs, so),
            ALU.mult, ALU.subtract),
            reads=[ST], writes=[ST], tag="lnva" + tag)
        p.op("dve", lambda h: h.tensor_scalar(
            stv(4, cs, so), stv(4, cs, so), float(eps), None, ALU.add),
            reads=[ST], writes=[ST], tag="lnve" + tag)
        p.op("act", lambda h: h.activation(
            stv(5, cs, so), stv(4, cs, so), AF.Sqrt),
            reads=[ST], writes=[ST], tag="lnsd" + tag)
        p.op("dve", lambda h: h.reciprocal(stv(6, cs, so), stv(5, cs, so)),
            reads=[ST], writes=[ST], tag="lnrd" + tag)
        return cs

    def transpose_tile(src_ap, src_keys, dstT_fn, dst_wkeys, wb=None, tag=""):
        for half in range(2):
            bank, bkey = PT[half], PTK[half]
            for j in range(4):
                kt = half * 4 + j
                p.op("pe", lambda h, kt=kt, j=j, bank=bank, s=src_ap: h.transpose(
                    bank[:, j * 128:(j + 1) * 128],
                    s[:, kt * 128:(kt + 1) * 128], ident[:]),
                    reads=list(src_keys) + [("ident",)],
                    writes=[bkey], tag="tr" + tag)
            for j in range(4):
                kt = half * 4 + j
                kw, rd, func = {}, [bkey], AF.Copy
                if wb is not None:
                    kw["scale"] = wb[0][:, kt:kt + 1]
                    kw["bias"] = wb[2][:, kt:kt + 1]
                    rd += [wb[1], wb[3]]
                    func = AF.Identity
                p.op("act", lambda h, kt=kt, j=j, bank=bank, kw=kw,
                     func=func: h.activation(
                    dstT_fn(kt), bank[:, j * 128:(j + 1) * 128], func, **kw),
                    reads=rd, writes=[dst_wkeys(kt)], tag="trev" + tag)

    def ln_chunk(src_ap, src_keys, eps, wb, xc_ap, xc_key, dstT_fn, dstT_wkeys,
                 tag="", so=0, sq_ap=None, sq_keys=None):
        cs = ln_stats(src_ap, D, eps, src_keys, tag=tag, so=so,
                      sq_ap=sq_ap, sq_keys=sq_keys)
        p.op("dve", lambda h, s=src_ap, cs=cs, so=so: h.tensor_scalar(
            xc_ap, s, stv(2, cs, so), stv(6, cs, so),
            ALU.subtract, ALU.mult),
            reads=list(src_keys) + [("st", so)], writes=[xc_key],
            tag="lnap" + tag)
        transpose_tile(xc_ap, [xc_key], dstT_fn, dstT_wkeys, wb=wb, tag=tag)

    def proj(ps_ap, bank_key, lhsT_fn, rhs_fn, nk, rd_fn, tag=""):
        for kt in range(nk):
            p.op("pe", lambda h, kt=kt: h.matmul(
                ps_ap, lhsT_fn(kt), rhs_fn(kt),
                start=(kt == 0), stop=(kt == nk - 1)),
                reads=rd_fn(kt) + ([bank_key] if kt else []),
                writes=[bank_key], tag=tag)

    def state_update(khn_ap, v_ap, kv_keys, tag=""):
        PD, PDK = PS[5], PSK[5]
        p.op("dve", lambda h: h.tensor_tensor(
            scrg[:, 0:NP * HD].rearrange("p (g w) -> p g w", w=HD),
            rcur[:].rearrange("p (g w) -> p g w", w=HD),
            gc[:, :, None].broadcast_to((128, NP, HD)), ALU.mult),
            reads=[("rcur",), ("gc",)], writes=GKS, tag="rsc" + tag)
        for pr in range(NP):
            c0 = pr * HD
            p.op("pe", lambda h, pr=pr: h.matmul(
                PD[0:64, 0:64], khn_ap[:, pr * 128:pr * 128 + 64],
                v_ap[:, pr * 128:pr * 128 + 64], start=True, stop=True,
                tile_position=(0, 0)),
                reads=kv_keys, writes=[PDK], tag="sdA" + tag)
            p.op("pe", lambda h, pr=pr: h.matmul(
                PD[64:128, 0:64], khn_ap[:, pr * 128 + 64:pr * 128 + 128],
                v_ap[:, pr * 128 + 64:pr * 128 + 128], start=True, stop=True,
                tile_position=(0, 64)),
                reads=kv_keys + [PDK], writes=[PDK], tag="sdB" + tag)
            p.op("dve", lambda h, pr=pr, c0=c0: h.scalar_tensor_tensor(
                rnxt[:, c0:c0 + HD], PD[:, 0:64], gc[:, pr:pr + 1],
                scrg[:, c0:c0 + HD], ALU.mult, ALU.add),
                reads=[PDK, ("gc",)] + GKS, writes=[("rnxt", pr)],
                tag="rup" + tag)
        p.op("dve", lambda h: h.tensor_copy(rcur[:], rnxt[:]),
            reads=[("rnxt", pr) for pr in range(NP)], writes=[("rcur",)],
            tag="rsw" + tag)

    def quant_tile(x_ap, x_keys, W, out_ap, out_wkey, rows=128, tag="",
                   soff=0, nsl=2, sseq=False, out_wkeys_fn=None):
        ws = W // nsl
        assert ws % 16 == 0
        SL = []
        for j in range(nsl):
            cx = j * ws
            c0 = soff if sseq else soff + cx
            nb, b0 = ws // 16, c0 // 16
            ki = 200 + c0 // 16
            x3 = x_ap[:, cx:cx + ws].rearrange("p (b k) -> p b k", k=16)
            m0 = qmv(0, NB)[0:rows, b0:b0 + nb]
            m1 = qmv(1, NB)[0:rows, b0:b0 + nb]
            SL.append((c0 + QOFF, ws, x3, m0, m1, nb, cx, ki))
        ORD = {s[7]: j for j, s in enumerate(SL)}

        STAGES = []

        def sl_op(eng, fn, rd, wr, tag_):
            STAGES.append((eng, fn, rd, wr, tag_))

        XK = list(x_keys)
        sl_op("dve", lambda c0, w, x3, m0, m1, nb, cx, ki:
              (lambda h: h.reduce_max(m0, x3, axis=AX,
                                      apply_absolute_value=True)),
              lambda i: XK, lambda i: [("qm", i)], "qbm" + tag)
        sl_op("act", lambda c0, w, x3, m0, m1, nb, cx, ki:
              (lambda h: h.activation(scrgb[0:rows, c0:c0 + w],
                                      x_ap[:, cx:cx + w], AF.Abs)),
              lambda i: XK, lambda i: [("scrg", i)], "qab" + tag)
        sl_op("dve", lambda c0, w, x3, m0, m1, nb, cx, ki:
              (lambda h: h.tensor_scalar(m1, m0, 1.0 / 6.0, 1e-12,
                                         ALU.mult, ALU.max)),
              lambda i: [("qm", i)], lambda i: [("qm", i)], "qs6")
        sl_op("dve", lambda c0, w, x3, m0, m1, nb, cx, ki:
              (lambda h: h.reciprocal(m0, m1)),
              lambda i: [("qm", i)], lambda i: [("qm", i)], "qre")
        sl_op("pool", lambda c0, w, x3, m0, m1, nb, cx, ki:
              (lambda h: h.tensor_tensor(
                  scrfb[0:rows, c0:c0 + w].rearrange("p (b k) -> p b k", k=16),
                  scrgb[0:rows, c0:c0 + w].rearrange("p (b k) -> p b k", k=16),
                  m0[:, :, None].broadcast_to((rows, nb, 16)), ALU.mult)),
              lambda i: [("scrg", i), ("qm", i)],
              lambda i: [("scrf", i)], "qt_" + tag)
        sl_op("dve", lambda c0, w, x3, m0, m1, nb, cx, ki:
              (lambda h: h.tensor_scalar(scrhb[0:rows, c0:c0 + w],
                                         scrfb[0:rows, c0:c0 + w], 0.25, 1.0,
                                         ALU.is_gt, ALU.mult)),
              lambda i: [("scrf", i)], lambda i: [("scrh", i)], "qt0")
        for t in (0.75, 1.25, 1.75):
            sl_op("dve", lambda c0, w, x3, m0, m1, nb, cx, ki, t=t:
                  (lambda h: h.scalar_tensor_tensor(
                      scrhb[0:rows, c0:c0 + w], scrfb[0:rows, c0:c0 + w], t,
                      scrhb[0:rows, c0:c0 + w], ALU.is_gt, ALU.add)),
                  lambda i: [("scrf", i), ("scrh", i)],
                  lambda i: [("scrh", i)], "qtl")
        sl_op("dve", lambda c0, w, x3, m0, m1, nb, cx, ki:
              (lambda h: h.tensor_scalar(scrp[0:rows, c0 - QOFF:c0 - QOFF + w],
                                         scrfb[0:rows, c0:c0 + w], 2.5, 1.0,
                                         ALU.is_gt, ALU.mult)),
              lambda i: [("scrf", i)], lambda i: [("scrp", i)], "qb0")
        for t in (3.5, 5.0, 5.0):
            sl_op("dve", lambda c0, w, x3, m0, m1, nb, cx, ki, t=t:
                  (lambda h: h.scalar_tensor_tensor(
                      scrp[0:rows, c0 - QOFF:c0 - QOFF + w],
                      scrfb[0:rows, c0:c0 + w], t,
                      scrp[0:rows, c0 - QOFF:c0 - QOFF + w],
                      ALU.is_gt, ALU.add)),
                  lambda i: [("scrf", i), ("scrp", i)],
                  lambda i: [("scrp", i)], "qbl")
        sl_op("act", lambda c0, w, x3, m0, m1, nb, cx, ki:
              (lambda h: h.activation(scrgb[0:rows, c0:c0 + w],
                                      x_ap[:, cx:cx + w], AF.Sign)),
              lambda i: XK + [("scrf", i)], lambda i: [("scrg", i)],
              "qsg" + tag)
        sl_op("dve", lambda c0, w, x3, m0, m1, nb, cx, ki:
              (lambda h: h.scalar_tensor_tensor(
                  scrhb[0:rows, c0:c0 + w], scrhb[0:rows, c0:c0 + w], 0.5,
                  scrp[0:rows, c0 - QOFF:c0 - QOFF + w], ALU.mult, ALU.add)),
              lambda i: [("scrh", i), ("scrp", i)],
              lambda i: [("scrh", i)], "qcm")
        sl_op("pool", lambda c0, w, x3, m0, m1, nb, cx, ki:
              (lambda h: h.tensor_tensor(
                  scrhb[0:rows, c0:c0 + w].rearrange("p (b k) -> p b k", k=16),
                  scrhb[0:rows, c0:c0 + w].rearrange("p (b k) -> p b k", k=16),
                  m1[:, :, None].broadcast_to((rows, nb, 16)), ALU.mult)),
              lambda i: [("scrh", i), ("qm", i)],
              lambda i: [("scrh", i)], "qds")
        sl_op("pool", lambda c0, w, x3, m0, m1, nb, cx, ki:
              (lambda h: h.tensor_tensor(
                  out_ap[:, cx:cx + w], scrhb[0:rows, c0:c0 + w],
                  scrgb[0:rows, c0:c0 + w], ALU.mult)),
              lambda i: [("scrh", i), ("scrg", i)],
              (lambda i: [out_wkey] if out_wkeys_fn is None
               else [out_wkey] + out_wkeys_fn(ORD[i])), "qo" + tag)

        if sseq:
            for s in SL:
                for eng, fn, rd, wr, tag_ in STAGES:
                    p.op(eng, fn(*s), reads=rd(s[7]), writes=wr(s[7]),
                         tag=tag_)
        else:
            for eng, fn, rd, wr, tag_ in STAGES:
                for s in SL:
                    p.op(eng, fn(*s), reads=rd(s[7]), writes=wr(s[7]),
                         tag=tag_)

    def prep_shard(src_h, nt, kc, agin_t, agkey, tag):
        W = nt * kc
        ld("w", "pool",
           stage[:, 0:W].rearrange("p (t k) -> p t k", k=kc),
           src_h.rearrange("(t p) k -> p t k", p=128),
           [("prepstage",)], tag="psh" + tag)
        quant_tile(stage[:, 0:W], [("prepstage",)], W,
                   stage[:, 0:W], ("prepstage",), tag="pq" + tag)
        ntile = W // 128
        for grp in range((ntile + 3) // 4):
            tls = list(range(grp * 4, min(ntile, grp * 4 + 4)))
            bank, bkey = PS[2 + grp % 2], PSK[2 + grp % 2]
            for j, tl in enumerate(tls):
                p.op("pe", lambda h, j=j, tl=tl, bank=bank: h.transpose(
                    bank[:, j * 128:(j + 1) * 128],
                    stage[:, tl * 128:(tl + 1) * 128], identf[:]),
                    reads=[("prepstage",), ("identf",)],
                    writes=[bkey], tag="ptr" + tag)
            for j, tl in enumerate(tls):
                nkc = kc // 128
                t_, jj = tl // nkc, tl % nkc
                dcol = jj * (nt * 128) + t_ * 128
                p.op("act", lambda h, j=j, dcol=dcol, bank=bank: h.activation(
                    scrh[:, dcol:dcol + 128], bank[:, j * 128:(j + 1) * 128],
                    AF.Copy),
                    reads=[bkey], writes=HKS_ALL, tag="pev" + tag)
        p.dma("sp1", "sp",
              lambda h, agin_t=agin_t, nt=nt, kc=kc: h.dma_start(
                  out=agin_t.ap().rearrange("(jj p) n -> p jj n", p=128),
                  in_=scrh[:, 0:W].rearrange("p (jj n) -> p jj n",
                                             n=nt * 128)),
              reads=HKS_ALL, writes=[agkey], tag="pst" + tag)

    prep_shard(wgsh_h, 22, 128, agin_wg, ("agin", "wg"), "g")
    prep_shard(wush_h, 22, 128, agin_wu, ("agin", "wu"), "u")
    prep_shard(wdsh_h, 8, 384, agin_wd, ("agin", "wd"), "d")

    p.op("dve", lambda h: h.memset(rcur[:], 0.0), writes=[("rcur",)], tag="ri")

    for rc in range(NT):
        par = rc % 2
        xs = xrm[:, par * D:(par + 1) * D]
        XB = ("xrmb", par)
        ld("w", "pool", xs, x_rem[rc * 128:(rc + 1) * 128, :], [XB])
        ln_chunk(xs, [XB], EPS1, (ln1w, ("ln1w",), ln1b, ("ln1b",)),
                 xr[:, par * D:(par + 1) * D], ("xrb", 0, par),
                 lambda kt, par=par: xcT[:, kt * G + par * 128:kt * G + par * 128 + 128],
                 lambda kt, par=par: ("xcTb", par, kt), tag=f"a{rc}")
        xcT_keys = [("xcTb", par, kt) for kt in range(8)]
        for half in range(2):
            ps, pk = PS[half], PSK[half]
            proj(ps[:, 0:512], pk,
                 lambda kt, par=par: xcT[:, kt * G + par * 128:kt * G + par * 128 + 128],
                 lambda kt, half=half: ws["wk"][:, kt * D + half * 512:kt * D + (half + 1) * 512],
                 8, lambda kt: xcT_keys + [("wk", kt)], tag=f"a0k{rc}")
            p.op("dve", lambda h, ps=ps, half=half, par=par: h.tensor_tensor(
                khn[:, par * D + half * 512:par * D + (half + 1) * 512],
                ps[:, 0:512], gkn[:, half * 512:(half + 1) * 512], ALU.mult),
                reads=[pk, ("gkn",)], writes=[("khnb", par, half)], tag="a0ek")
            ps2, pk2 = PS[2 + half], PSK[2 + half]
            proj(ps2[:, 0:512], pk2,
                 lambda kt, par=par: xcT[:, kt * G + par * 128:kt * G + par * 128 + 128],
                 lambda kt, half=half: ws["wv"][:, kt * D + half * 512:kt * D + (half + 1) * 512],
                 8, lambda kt: xcT_keys + [("wv", kt)], tag=f"a0v{rc}")
            p.op("act", lambda h, ps2=ps2, half=half, par=par: h.activation(
                vno[:, par * D + half * 512:par * D + (half + 1) * 512],
                ps2[:, 0:512], AF.Copy),
                reads=[pk2], writes=[("vnob", par, half)], tag="a0ev")
        state_update(khn[:, par * D:(par + 1) * D], vno[:, par * D:(par + 1) * D],
                     [("khnb", par, 0), ("khnb", par, 1),
                      ("vnob", par, 0), ("vnob", par, 1)], tag=f"a0{rc}")

    p.op("dve", lambda h: h.tensor_scalar(
        rcur[:], rcur[:], flag[:, 0:1], None, ALU.mult),
        reads=[("rcur",), ("flag",)], writes=[("rcur",)], tag="rflag")
    p.op("dve", lambda h: h.tensor_copy(rbf[:], rcur[:]),
        reads=[("rcur",)], writes=[("rbf",)], tag="rbf0")

    RG8 = [[0, 1, 2, 3, 4, 5, 6, 7]]

    def emit_ags():
        for tg, ain, aout in [("wg", agin_wg, agout_wg),
                              ("wu", agin_wu, agout_wu),
                              ("wd", agin_wd, agout_wd)]:
            p.op("pool", lambda h, ain=ain, aout=aout: h.collective_compute(
                "AllGather", ALU.bypass, replica_groups=RG8,
                ins=[ain.ap().opt()], outs=[aout.ap().opt()]),
                reads=[("agin", tg)], writes=[("ag", tg)], tag="ag" + tg)

    def emit_ln1(g, part):
        gp = g % 2
        xb = x0 if gp == 0 else x0B
        xrb = xr if gp == 0 else xrB
        so = 0 if gp == 0 else 7
        sqa = None if gp == 0 else sqB[:, 0:D]
        sqk = None if gp == 0 else [("sqB",)]
        if g == 1 and part == 1:
            p.op("sp", lambda h: None, reads=[("prepstage",)],
                 writes=[("x0b", 1, 0), ("x0b", 1, 1), ("xrb", 1, 0),
                         ("xrb", 1, 1), ("sqB",)], tag="claimB")
        for par in range(2):
            ci = 2 * g + par
            XK = ("x0b", gp, par)
            if part == 1:
                ld("w", "pool", xb[:, par * D:(par + 1) * D],
                   x_loc[ci * 128:(ci + 1) * 128, :], [XK])
                cs = ln_stats(xb[:, par * D:(par + 1) * D], D, EPS1, [XK],
                              tag=f"A{ci}", so=so, sq_ap=sqa, sq_keys=sqk)
                p.op("dve", lambda h, par=par, cs=cs, so=so, xb=xb,
                     xrb=xrb: h.tensor_scalar(
                         xrb[:, par * D:(par + 1) * D],
                         xb[:, par * D:(par + 1) * D],
                         stv(2, cs, so), stv(6, cs, so),
                         ALU.subtract, ALU.mult),
                     reads=[XK, ("st", so)], writes=[("xrb", gp, par)],
                     tag=f"lnapA{ci}")
            else:
                transpose_tile(
                    xrb[:, par * D:(par + 1) * D], [("xrb", gp, par)],
                    lambda kt, par=par: xcT[:, kt * G + par * 128:kt * G + par * 128 + 128],
                    lambda kt, par=par: ("xcTb", par, kt),
                    wb=(ln1w, ("ln1w",), ln1b, ("ln1b",)), tag=f"A{ci}")

    emit_ln1(0, 1)
    emit_ln1(0, 2)
    for g in range(NT // 2):
        gp = g % 2
        xb = x0 if gp == 0 else x0B
        xrb = xr if gp == 0 else xrB
        if g == 4:
            emit_ags()
        ck0 = 2 * g
        xcT_keys = [("xcTb", pp, kt) for pp in range(2) for kt in range(8)]

        for pr in range(NP):
            ps, pk = PS[pr % 2], PSK[pr % 2]
            proj(ps[:, 0:G], pk,
                 lambda kt, pr=pr: ws["wq"][:, kt * D + pr * 128:kt * D + (pr + 1) * 128],
                 lambda kt: xcT[:, kt * G:(kt + 1) * G],
                 8, lambda kt: xcT_keys + [("wq", kt)], tag=f"pq{pr}")
            for cc in range(2):
                p.op("dve", lambda h, ps=ps, pr=pr, cc=cc: h.tensor_tensor(
                    qhT[:, pr * G + cc * 128:pr * G + (cc + 1) * 128],
                    ps[:, cc * 128:(cc + 1) * 128],
                    gq[:, pr * 128:(pr + 1) * 128], ALU.mult),
                    reads=[pk, ("gq",)], writes=[("qhTb", pr, cc)], tag="evq")
            ps2, pk2 = PS[2 + pr % 2], PSK[2 + pr % 2]
            proj(ps2[:, 0:G], pk2,
                 lambda kt, pr=pr: ws["wk"][:, kt * D + pr * 128:kt * D + (pr + 1) * 128],
                 lambda kt: xcT[:, kt * G:(kt + 1) * G],
                 8, lambda kt: xcT_keys + [("wk", kt)], tag=f"pkT{pr}")
            for cc in range(2):
                p.op("dve", lambda h, ps2=ps2, pr=pr, cc=cc: h.tensor_tensor(
                    khT[:, pr * G + cc * 128:pr * G + (cc + 1) * 128],
                    ps2[:, cc * 128:(cc + 1) * 128],
                    gk[:, pr * 128:(pr + 1) * 128], ALU.mult),
                    reads=[pk2, ("gk",)], writes=[("khTb", pr, cc)], tag="evk")

        for par in range(2):
            ci = ck0 + par
            for pr in range(NP):
                bank, bkey = PT[pr % 2], PTK[pr % 2]
                p.op("pe", lambda h, pr=pr, par=par, bank=bank: h.transpose(
                    bank[:, 0:128],
                    khT[:, pr * G + par * 128:pr * G + par * 128 + 128],
                    ident[:]),
                    reads=[("khTb", pr, par), ("ident",)], writes=[bkey],
                    tag="ktr")
                p.op("act", lambda h, pr=pr, par=par, bank=bank: h.activation(
                    khn[:, par * D + pr * 128:par * D + (pr + 1) * 128],
                    bank[:, 0:128], AF.Copy),
                    reads=[bkey], writes=[("khnb", par, pr // 4)], tag="kte")
            for half in range(2):
                ps2, pk2 = PS[5], PSK[5]
                proj(ps2[:, 0:512], pk2,
                     lambda kt, par=par: xcT[:, kt * G + par * 128:kt * G + par * 128 + 128],
                     lambda kt, half=half: ws["wv"][:, kt * D + half * 512:kt * D + (half + 1) * 512],
                     8, lambda kt: xcT_keys + [("wv", kt)], tag=f"nv{ci}")
                p.op("act", lambda h, ps2=ps2, half=half, par=par: h.activation(
                    vno[:, par * D + half * 512:par * D + (half + 1) * 512],
                    ps2[:, 0:512], AF.Copy),
                    reads=[pk2], writes=[("vnob", par, half)], tag="nev")
                ps3, pk3 = PS[half], PSK[half]
                proj(ps3[:, 0:512], pk3,
                     lambda kt, par=par: xcT[:, kt * G + par * 128:kt * G + par * 128 + 128],
                     lambda kt, half=half: ws["wg"][:, kt * D + half * 512:kt * D + (half + 1) * 512],
                     8, lambda kt: xcT_keys + [("wg", kt)], tag=f"ng{ci}")
                p.op("act", lambda h, ps3=ps3, half=half, par=par: h.activation(
                    sg[:, par * D + half * 512:par * D + half * 512 + 512],
                    ps3[:, 0:512], AF.Silu),
                    reads=[pk3], writes=[("sgb", par, half)], tag="neg")

        if g + 1 < NT // 2:
            emit_ln1(g + 1, 1)
            emit_ln1(g + 1, 2)

        for par in range(2):
            ci = ck0 + par
            v_c = vno[:, par * D:(par + 1) * D]
            kc_keys = [("khnb", par, 0), ("khnb", par, 1)]
            vc_keys = [("vnob", par, 0), ("vnob", par, 1)]
            PO, POK = PS[4], PSK[4]
            for pr in range(NP):
                qsl = qhT[:, pr * G + par * 128:pr * G + par * 128 + 128]
                qkey = ("qhTb", pr, par)
                ksl = khT[:, pr * G + par * 128:pr * G + par * 128 + 128]
                kkey = ("khTb", pr, par)
                for hh in range(2):
                    PA, PAK = PS[hh], PSK[hh]
                    p.op("pe", lambda h, PA=PA, hh=hh, qsl=qsl, ksl=ksl: h.matmul(
                        PA[:, 0:128], ksl[64 * hh:64 * (hh + 1), :],
                        qsl[64 * hh:64 * (hh + 1), :], start=True, stop=True,
                        tile_position=(64 * hh, 0)),
                        reads=[qkey, kkey], writes=[PAK], tag="mmA")
                    p.op("dve", lambda h, PA=PA, hh=hh: h.tensor_tensor(
                        aT[:, hh * 128:(hh + 1) * 128], PA[:, 0:128],
                        mskT[:], ALU.mult),
                        reads=[PAK, ("mskT",)], writes=[("aTb", hh)], tag="mask")
                for hh in range(2):
                    hcol = (2 * pr + hh) * HD
                    p.op("pe", lambda h, PO=PO, hh=hh, hcol=hcol, v_c=v_c: h.matmul(
                        PO[:, hh * 64:(hh + 1) * 64],
                        aT[:, hh * 128:(hh + 1) * 128],
                        v_c[:, hcol:hcol + HD], start=(hh == 0), stop=False),
                        reads=[("aTb", hh)] + vc_keys + [POK], writes=[POK],
                        tag="mmI")
                    p.op("pe", lambda h, PO=PO, hh=hh, pr=pr, qsl=qsl: h.matmul(
                        PO[:, hh * 64:(hh + 1) * 64],
                        qsl[64 * hh:64 * (hh + 1), :],
                        rbf[64 * hh:64 * (hh + 1), pr * HD:(pr + 1) * HD],
                        start=False, stop=(hh == 1), tile_position=(64 * hh, 0)),
                        reads=[qkey, ("rbf",), POK], writes=[POK], tag="mmX")
                p.op("act", lambda h, PO=PO, pr=pr: h.activation(
                    ro[:, pr * 128:(pr + 1) * 128], PO[:, 0:128], AF.Copy),
                    reads=[POK], writes=[("rob", pr)], tag="evo")
            state_update(khn[:, par * D:(par + 1) * D], v_c,
                         kc_keys + vc_keys, tag=f"A{ci}")
            p.op("dve", lambda h: h.tensor_copy(rbf[:], rcur[:]),
                reads=[("rcur",)], writes=[("rbf",)], tag="rbfA")

            ro_keys = [("rob", pr) for pr in range(NP)]
            cs = ln_stats(ro[:], HD, EPSH, ro_keys, nst=16, tag=f"hl{ci}")
            p.op("dve", lambda h, cs=cs: h.tensor_tensor(
                scrg[:, 0:D].rearrange("p (g w) -> p g w", w=HD),
                ro[:].rearrange("p (g w) -> p g w", w=HD),
                stv(2, cs)[:, :, None].broadcast_to((128, 16, HD)),
                ALU.subtract),
                reads=ro_keys + [ST], writes=GKS, tag="gnt1")
            p.op("dve", lambda h, par=par, cs=cs: h.tensor_tensor(
                zzb[:, par * D:(par + 1) * D].rearrange("p (g w) -> p g w", w=HD),
                sg[:, par * D:(par + 1) * D].rearrange("p (g w) -> p g w", w=HD),
                stv(6, cs)[:, :, None].broadcast_to((128, 16, HD)), ALU.mult),
                reads=[("sgb", par, 0), ("sgb", par, 1), ST],
                writes=[("zzbb", par)], tag="gnz")
            p.op("dve", lambda h, par=par, xrb=xrb: h.tensor_tensor(
                xrb[:, par * D:(par + 1) * D], scrg[:, 0:D],
                zzb[:, par * D:(par + 1) * D], ALU.mult),
                reads=GKS + [("zzbb", par)], writes=[("xrb", gp, par)],
                tag="gnxr")
            transpose_tile(xrb[:, par * D:(par + 1) * D], [("xrb", gp, par)],
                           lambda kt, par=par: xrT[:, par * D + kt * 128:par * D + (kt + 1) * 128],
                           lambda kt, par=par: ("xrTb", par, kt), tag=f"xr{ci}")
            p.op("dve", lambda h, par=par, xb=xb: h.scalar_tensor_tensor(
                scrh[:, 0:D], xb[:, par * D:(par + 1) * D], 2.0, bout[:],
                ALU.mult, ALU.add),
                reads=[("x0b", gp, par), ("bout",)], writes=HKS, tag="x0b2")
            for half in range(2):
                ps, pk = PS[half], PSK[half]
                proj(ps[:, 0:512], pk,
                     lambda kt, par=par: xrT[:, par * D + kt * 128:par * D + (kt + 1) * 128],
                     lambda kt, half=half: ws["wo"][:, kt * D + half * 512:kt * D + (half + 1) * 512],
                     8, lambda kt, par=par: [("xrTb", par, kt), ("wo", kt)],
                     tag=f"po{ci}")
                sl = slice(half * 512, (half + 1) * 512)
                p.op("dve", lambda h, ps=ps, sl=sl, par=par, xb=xb: h.tensor_tensor(
                    x1b[:, par * D:(par + 1) * D][:, sl], ps[:, 0:512],
                    xb[:, par * D:(par + 1) * D][:, sl], ALU.add),
                    reads=[pk, ("x0b", gp, par)], writes=[("x1bb", par, half)],
                    tag="evx1")
                p.op("dve", lambda h, ps=ps, sl=sl, par=par: h.tensor_tensor(
                    xab[:, par * D:(par + 1) * D][:, sl], ps[:, 0:512],
                    scrh[:, 0:D][:, sl], ALU.add),
                    reads=[pk] + HKS, writes=[("xabb", par, half)],
                    tag="evxa")
            p.dma("sp1", "sp",
                  lambda h, ci=ci, par=par: h.dma_start(
                      out=spill_x1[ci * 128:(ci + 1) * 128, :],
                      in_=x1b[:, par * D:(par + 1) * D]),
                  reads=[("x1bb", par, 0), ("x1bb", par, 1)],
                  writes=[("sx1", ci)], tag="spx1")
            p.dma("sp1", "sp",
                  lambda h, ci=ci, par=par: h.dma_start(
                      out=spill_xa[ci * 128:(ci + 1) * 128, :],
                      in_=xab[:, par * D:(par + 1) * D]),
                  reads=[("xabb", par, 0), ("xabb", par, 1)],
                  writes=[("sxa", ci)], tag="spxa")

    p.fence("AB")
    for kt in range(8):
        ld("w", "pool", w_in_s[:, kt * WIN:(kt + 1) * WIN],
           w_in_h[kt * 128:(kt + 1) * 128, :], [("w_in", kt)])
    for kt in range(22):
        c0, cw = ACT_TILES[kt]
        ld("w", "pool", w_out_s[0:cw, kt * D:(kt + 1) * D],
           w_out_h[c0:c0 + cw, :], [("w_out", kt)])

    for g in range(NT // 2):
        ck0 = 2 * g
        for par in range(2):
            ci = ck0 + par
            ld("w", "pool", x1g[:, par * D:(par + 1) * D],
               spill_x1[ci * 128:(ci + 1) * 128, :], [("x1gb", par)],
               reads=[("sx1", ci)])
            ld("w", "pool", xag[:, par * D:(par + 1) * D],
               spill_xa[ci * 128:(ci + 1) * 128, :], [("xagb", par)],
               reads=[("sxa", ci)])
            ln_chunk(x1g[:, par * D:(par + 1) * D], [("x1gb", par)], EPS1,
                     (ln2w, ("ln2w",), ln2b, ("ln2b",)),
                     z1b[:, par * D:(par + 1) * D], ("z1bb", par),
                     lambda kt, par=par: xc2T[:, kt * GB + par * 128:kt * GB + par * 128 + 128],
                     lambda kt, par=par: ("x2Tb", par, kt), tag=f"B{ci}")
        x2T_keys = [("x2Tb", pp, kt) for pp in range(2) for kt in range(8)]

        for ct in range(22):
            c0a, cw = ACT_TILES[ct]
            psA, pkA = PS[ct % 2], PSK[ct % 2]
            psB, pkB = PS[2 + ct % 2], PSK[2 + ct % 2]
            proj(psA[0:cw, 0:GB], pkA,
                 lambda kt, c0a=c0a, cw=cw: w_in_s[:, kt * WIN + c0a:kt * WIN + c0a + cw],
                 lambda kt: xc2T[:, kt * GB:(kt + 1) * GB],
                 8, lambda kt: x2T_keys + [("w_in", kt)], tag=f"pA{ct}")
            proj(psB[0:cw, 0:GB], pkB,
                 lambda kt, c0a=c0a, cw=cw: w_in_s[:, kt * WIN + HID + c0a:kt * WIN + HID + c0a + cw],
                 lambda kt: xc2T[:, kt * GB:(kt + 1) * GB],
                 8, lambda kt: x2T_keys + [("w_in", kt)], tag=f"pB{ct}")
            p.op("act", lambda h, psA=psA, cw=cw, ct=ct: h.activation(
                abA[0:cw, :], psA[0:cw, 0:GB], AF.Identity,
                bias=bina[0:cw, ct:ct + 1]),
                reads=[pkA, ("bina",)], writes=[("abA",)], tag="evA")
            p.op("act", lambda h, psB=psB, cw=cw, ct=ct: h.activation(
                abB[0:cw, :], psB[0:cw, 0:GB], AF.Silu,
                bias=binb[0:cw, ct:ct + 1]),
                reads=[pkB, ("binb",)], writes=[("abB",)], tag="evB")
            p.op("dve", lambda h, cw=cw, ct=ct: h.tensor_tensor(
                hT[0:cw, ct * GB:(ct + 1) * GB], abA[0:cw, :], abB[0:cw, :],
                ALU.mult),
                reads=[("abA",), ("abB",)], writes=[("hTb", ct)], tag="hTm")

        for par in range(2):
            ci = ck0 + par
            for half in range(2):
                ps, pk = PS[4 + half], PSK[4 + half]
                def lf(kt, par=par):
                    _, cw = ACT_TILES[kt]
                    return hT[0:cw, kt * GB + par * 128:kt * GB + par * 128 + 128]
                def rf(kt, half=half):
                    _, cw = ACT_TILES[kt]
                    return w_out_s[0:cw, kt * D + half * 512:kt * D + (half + 1) * 512]
                proj(ps[:, 0:512], pk, lf, rf, 22,
                     lambda kt: [("hTb", kt), ("w_out", kt)], tag=f"pw2{ci}")
                sl = slice(half * 512, (half + 1) * 512)
                p.op("dve", lambda h, ps=ps, sl=sl, par=par: h.tensor_tensor(
                    scrg[:, 0:D][:, sl], ps[:, 0:512],
                    xag[:, par * D:(par + 1) * D][:, sl], ALU.add),
                    reads=[pk, ("xagb", par)], writes=GKS, tag="z1in")
            cs = ln_stats(scrg[:, 0:D], D, EPS1, GKS, tag=f"p1{ci}")
            p.op("dve", lambda h, cs=cs, par=par: h.tensor_scalar(
                z1b[:, par * D:(par + 1) * D], scrg[:, 0:D],
                stv(2, cs), stv(6, cs), ALU.subtract, ALU.mult),
                reads=GKS + [ST], writes=[("z1bb", par)], tag="z1ap")
            p.dma("sp1", "sp",
                  lambda h, ci=ci, par=par: h.dma_start(
                      out=spill_z1[ci * 128:(ci + 1) * 128, :],
                      in_=z1b[:, par * D:(par + 1) * D]),
                  reads=[("z1bb", par)], writes=[("sz1", ci)], tag="spz1")

    p.fence("BD")
    for kt in range(8):
        ld("w", "pool", wgq[:, kt * FFD:(kt + 1) * FFD],
           agout_wg[kt * 128:(kt + 1) * 128, :], [("wgq", kt)],
           reads=[("ag", "wg")], tag="lwgq")
        ld("w", "pool", wuq[:, kt * FFD:(kt + 1) * FFD],
           agout_wu[kt * 128:(kt + 1) * 128, :], [("wuq", kt)],
           reads=[("ag", "wu")], tag="lwuq")
    for kt in range(22):
        r_, pos = (kt // 3, kt % 3) if kt <= 20 else (7, 2)
        srow = r_ * 384 + pos * 128
        ld("w", "pool", wdq[:, kt * D:(kt + 1) * D],
           agout_wd[srow:srow + 128, :], [("wdq", kt)],
           reads=[("ag", "wd")], tag="lwdq")
    def hq_tiles(sl_i):
        return list(range(11)) if sl_i == 0 else list(range(11, 22))

    def stage_x(ci):
        par = ci % 2
        zsl = z1g[:, par * D:(par + 1) * D]
        ld("w", "pool", zsl, spill_z1[ci * 128:(ci + 1) * 128, :],
           [("z1gb", par)], reads=[("sz1", ci)])
        quant_tile(zsl, [("z1gb", par)], D, xq[:, 0:D], ("xq",), tag="x",
                   nsl=1)

    def stage_a(ci):
        transpose_tile(xq[:, 0:D], [("xq",)],
                       lambda kt: xqT[:, kt * 128:(kt + 1) * 128],
                       lambda kt: ("xqT",), tag=f"xq{ci}")
        for nm, wbuf, wname, obuf, obkey, func, pb in [
            ("gm", wgq, "wgq", sgt, ("sgt",), AF.Silu, 0),
            ("um", wuq, "wuq", sut, ("sut",), AF.Copy, 2),
        ]:
            for nc_i in range(6):
                n0 = nc_i * 512
                nw = min(512, FFD - n0)
                ps, pk = PS[pb + nc_i % 2], PSK[pb + nc_i % 2]
                proj(ps[:, 0:nw], pk,
                     lambda kt: xqT[:, kt * 128:(kt + 1) * 128],
                     lambda kt, n0=n0, nw=nw, wbuf=wbuf: wbuf[:, kt * FFD + n0:kt * FFD + n0 + nw],
                     8, lambda kt, wname=wname: [("xqT",), (wname, kt)],
                     tag=f"{nm}{nc_i}")
                p.op("act", lambda h, ps=ps, n0=n0, nw=nw, obuf=obuf,
                     func=func: h.activation(
                         obuf[:, n0:n0 + nw], ps[:, 0:nw], func),
                     reads=[pk], writes=[obkey], tag=nm + "ev")

    def stage_b(ci):
        p.op("dve", lambda h: h.tensor_tensor(
            sgt[:, 0:FFD], sgt[:, 0:FFD], sut[:, 0:FFD], ALU.mult),
            reads=[("sgt",), ("sut",)], writes=[("sgt",)], tag="hff")
        quant_tile(sgt[:, 0:FFD], [("sgt",)], FFD, hqT[:, 0:FFD],
                   ("hqTp",), tag="h",
                   out_wkeys_fn=lambda sl_i: [("hqT", t)
                                              for t in hq_tiles(sl_i)])

    def stage_c1(ci):
        for grp in range(6):
            nts = list(range(grp * 4, min(22, grp * 4 + 4)))
            bank, bkey = PT[grp % 2], PTK[grp % 2]
            for j, ntl in enumerate(nts):
                p.op("pe", lambda h, j=j, ntl=ntl, bank=bank: h.transpose(
                    bank[:, j * 128:(j + 1) * 128],
                    hqT[:, ntl * 128:(ntl + 1) * 128], ident[:]),
                    reads=[("hqT", ntl), ("ident",)], writes=[bkey],
                    tag="trh")
            for j, ntl in enumerate(nts):
                p.op("act", lambda h, j=j, ntl=ntl, bank=bank: h.activation(
                    hqT[:, ntl * 128:(ntl + 1) * 128],
                    bank[:, j * 128:(j + 1) * 128], AF.Copy),
                    reads=[bkey], writes=[("hqT", ntl)], tag="trhev")
        for half in range(2):
            ps, pk = PS[4 + half], PSK[4 + half]
            proj(ps[:, 0:512], pk,
                 lambda kt: hqT[:, kt * 128:(kt + 1) * 128],
                 lambda kt, half=half: wdq[:, kt * D + half * 512:kt * D + (half + 1) * 512],
                 22, lambda kt: [("hqT", kt), ("wdq", kt)], tag=f"dm{half}")

    def stage_c2(ci):
        par = ci % 2
        zsl = z1g[:, par * D:(par + 1) * D]
        for half in range(2):
            ps, pk = PS[4 + half], PSK[4 + half]
            sl = slice(half * 512, (half + 1) * 512)
            p.op("dve", lambda h, ps=ps, sl=sl, zsl=zsl: h.tensor_tensor(
                outf[:, sl], ps[:, 0:512], zsl[:, sl], ALU.add),
                reads=[pk, ("z1gb", par)], writes=[("outfb",)], tag="z2in")
        cs = ln_stats(outf[:], D, EPS1, [("outfb",)], tag=f"p2{ci}",
                      sq_ap=xqT[:, 0:D], sq_keys=[("xqT",)])
        p.op("dve", lambda h, cs=cs: h.tensor_scalar(
            outf[:], outf[:], stv(2, cs), stv(6, cs),
            ALU.subtract, ALU.mult),
            reads=[("outfb",), ST], writes=[("outfb",)], tag="outap")
        p.dma("out", "sp",
              lambda h, ci=ci: h.dma_start(
                  out=y[ci * 128:(ci + 1) * 128, :], in_=outf[:]),
              reads=[("outfb",)], writes=[("y", ci)], tag="sty")

    stage_x(0)
    stage_a(0)
    stage_x(1)
    stage_b(0)
    for ci in range(NT):
        if ci + 1 < NT:
            stage_a(ci + 1)
        stage_c1(ci)
        if ci + 1 < NT:
            stage_b(ci + 1)
        stage_c2(ci)
        if ci + 2 < NT:
            stage_x(ci + 2)

    p.op("sp", lambda h: None, reads=[("y", ci) for ci in range(NT)], tag="fin")
    p.emit()
    return nc


_NC_CACHE = [None]


def _consts():
    g = np.array(GAMMA, dtype=np.float64)
    idx = np.arange(128, dtype=np.float64)
    gq = np.zeros((128, NP * 128))
    gk = np.zeros((128, NP * 128))
    gcv = np.zeros((128, NP))
    for pr in range(NP):
        for r in range(128):
            hh = 2 * pr + (1 if r >= 64 else 0)
            gq[r, pr * 128:(pr + 1) * 128] = 0.125 * g[hh] ** idx
            gk[r, pr * 128:(pr + 1) * 128] = g[hh] ** (-idx)
            gcv[r, pr] = g[hh] ** 128
    gkn = np.zeros((128, D))
    for col in range(D):
        gkn[:, col] = g[col // HD] ** (-idx)
    mskT = (idx[None, :] >= idx[:, None]).astype(np.float64)
    ident = np.eye(128)
    out = {k: np.ascontiguousarray(v, dtype=np.float32) for k, v in
           dict(mskT=mskT, ident=ident).items()}
    for name, v in dict(gq=gq, gk=gk, gkn=gkn, gc=gcv).items():
        hi = v.astype(np.float32)
        lo = (v - hi.astype(np.float64)).astype(np.float32)
        out[name + "_hi"] = np.ascontiguousarray(hi)
        out[name + "_lo"] = np.ascontiguousarray(lo)
    return out


def kernel(**inputs):
    f = np.float32
    x = np.ascontiguousarray(np.asarray(inputs["L0_attn_x"], f))
    B, S, _ = x.shape

    def cols8(v):
        return np.ascontiguousarray(np.asarray(v, f).reshape(8, 128).T)

    b_in = np.asarray(inputs["L0_attn_b_in"], f)
    b_in_a = np.zeros((22 * 128,), f)
    b_in_a[:HID] = b_in[:HID]
    b_in_b = np.zeros((22 * 128,), f)
    b_in_b[:HID] = b_in[HID:]
    common = {
        "ln1w": cols8(inputs["L0_attn_ln1_w"]),
        "ln1b": cols8(inputs["L0_attn_ln1_b"]),
        "ln2w": cols8(inputs["L0_attn_ln2_w"]),
        "ln2b": cols8(inputs["L0_attn_ln2_b"]),
        "wq": np.ascontiguousarray(np.asarray(inputs["L0_attn_wq"], f)),
        "wk": np.ascontiguousarray(np.asarray(inputs["L0_attn_wk"], f)),
        "wv": np.ascontiguousarray(np.asarray(inputs["L0_attn_wv"], f)),
        "wg": np.ascontiguousarray(np.asarray(inputs["L0_attn_wg"], f)),
        "wo": np.ascontiguousarray(np.asarray(inputs["L0_attn_wo"], f)),
        "w_in": np.ascontiguousarray(np.asarray(inputs["L0_attn_w_in"], f)),
        "w_out": np.ascontiguousarray(np.asarray(inputs["L0_attn_w_out"], f)),
        "b_in_a": np.ascontiguousarray(b_in_a.reshape(22, 128).T),
        "b_in_b": np.ascontiguousarray(b_in_b.reshape(22, 128).T),
        "b_out_bc": np.ascontiguousarray(np.broadcast_to(
            np.asarray(inputs["L0_attn_b_out"], f), (128, D))),
        **_consts(),
    }
    w_gate = np.asarray(inputs["L0_ffn_w_gate"], f)
    w_up = np.asarray(inputs["L0_ffn_w_up"], f)
    w_down = np.asarray(inputs["L0_ffn_w_down"], f)

    if _NC_CACHE[0] is None:
        _NC_CACHE[0] = build_nc()
    nc = _NC_CACHE[0]

    in_maps = []
    for c in range(8):
        b, hf = c // 2, c % 2
        m = dict(common)
        m["x_loc"] = np.ascontiguousarray(x[b, hf * SC:(hf + 1) * SC, :])
        m["x_rem"] = np.ascontiguousarray(x[b, 0:SC, :])
        m["flag"] = np.full((128, 1), float(hf), f)
        m["wg_shard"] = np.ascontiguousarray(w_gate[:, 128 * c:128 * (c + 1)])
        m["wu_shard"] = np.ascontiguousarray(w_up[:, 128 * c:128 * (c + 1)])
        k0 = 384 * c if c < 7 else 2816 - 384
        m["wd_shard"] = np.ascontiguousarray(w_down[:, k0:k0 + 384])
        in_maps.append(m)

    res = run_bass_kernel_spmd(nc, in_maps, core_ids=list(range(8)))
    out = np.empty((B, S, D), f)
    for c in range(8):
        b, hf = c // 2, c % 2
        out[b, hf * SC:(hf + 1) * SC, :] = res.results[c]["y"]
    return out


# version 18
# speedup vs baseline: 1.0462x; 1.0379x over previous
import numpy as np
import ml_dtypes
import concourse.bass as bass
import concourse.mybir as mybir
from concourse.bass_utils import run_bass_kernel_spmd

F32 = mybir.dt.float32
BF16 = mybir.dt.bfloat16
AF = mybir.ActivationFunctionType
ALU = mybir.AluOpType
AX = mybir.AxisListType.X

D = 1024
SC = 2048
NT = SC // 128
NH = 16
HD = 64
NP = 8
HID = 2730
WIN = 2 * HID
FFD = 2816
GAMMA = [float(1.0 - 2.0 ** (-5.0 - h)) for h in range(NH)]
EPS1 = 1e-5
EPSH = 1e-6
ACT_TILES = [(ct * 128, min(128, HID - ct * 128)) for ct in range(22)]
SBUF_TOP = 229344


class Prog:
    ENGINES = ("pe", "act", "dve", "sp", "pool")

    def __init__(self, nc):
        self.nc = nc
        self.ops = {e: [] for e in self.ENGINES}
        self.res = {}
        self.streams = {}
        self.fence_tok = None

    def op(self, eng, fn, reads=(), writes=(), tag=""):
        deps = set()
        for k in reads:
            r = self.res.get(k)
            if r is None:
                raise KeyError(f"read before write: {k!r} (op {tag})")
            deps.add(r["w"])
        for k in writes:
            r = self.res.get(k)
            if r is not None:
                deps.add(r["w"])
                deps.update(r["r"])
            elif self.fence_tok is not None:
                deps.add(self.fence_tok)
        idx = len(self.ops[eng])
        tok = ("e", eng, idx)
        self.ops[eng].append({"fn": fn, "deps": deps, "dma": None, "tag": tag})
        for k in reads:
            self.res[k]["r"].append(tok)
        for k in writes:
            self.res[k] = {"w": tok, "r": []}
        return tok

    def dma(self, stream, eng, fn, reads=(), writes=(), tag=""):
        s = self.streams.setdefault(stream, {"engine": eng, "n": 0})
        assert s["engine"] == eng
        deps = set()
        for k in reads:
            r = self.res.get(k)
            if r is not None:
                deps.add(r["w"])
        for k in writes:
            r = self.res.get(k)
            if r is not None:
                deps.add(r["w"])
                deps.update(r["r"])
            elif self.fence_tok is not None:
                deps.add(self.fence_tok)
        s["n"] += 1
        tok = ("d", stream, s["n"])
        self.ops[eng].append({"fn": fn, "deps": deps, "dma": (stream, s["n"]),
                              "tag": tag})
        for k in reads:
            if k in self.res:
                self.res[k]["r"].append(tok)
        for k in writes:
            self.res[k] = {"w": tok, "r": []}
        return tok

    def fence(self, tag=""):
        alldeps = set()
        for r in self.res.values():
            alldeps.add(r["w"])
            alldeps.update(r["r"])
        toks = []
        for e in self.ENGINES:
            if not self.ops[e]:
                continue
            idx = len(self.ops[e])
            toks.append(("e", e, idx))
            self.ops[e].append({"fn": lambda h: None, "deps": set(alldeps),
                                "dma": None, "tag": f"fence_{tag}"})
        master = ("e", "sp", len(self.ops["sp"]))
        self.ops["sp"].append({"fn": lambda h: None, "deps": set(toks),
                               "dma": None, "tag": f"fencem_{tag}"})
        for k in self.res:
            self.res[k] = {"w": master, "r": []}
        self.fence_tok = master

    NSLOT = 8

    def emit(self):
        nc = self.nc
        pub = {e: [False] * len(self.ops[e]) for e in self.ENGINES}
        for e in self.ENGINES:
            for o in self.ops[e]:
                for d in o["deps"]:
                    if d[0] == "e" and not (d[1] == e == "pe"):
                        pub[d[1]][d[2]] = True
        cum = {}
        for e in self.ENGINES:
            c, lst = 0, []
            for f in pub[e]:
                c += int(f)
                lst.append(c)
            cum[e] = lst

        ctxs, sems, ssems = [], {}, {}
        for e in self.ENGINES:
            if any(pub[e]):
                cm = nc.semaphore(f"sem_{e}")
                sems[e] = cm.__enter__()
                ctxs.append(cm)
        for sname in self.streams:
            lst = []
            for sl in range(self.NSLOT):
                cm = nc.semaphore(f"dma_{sname}{sl}")
                lst.append(cm.__enter__())
                ctxs.append(cm)
            ssems[sname] = lst
        blk_cm = nc.Block()
        block = blk_cm.__enter__()
        ctxs.append(blk_cm)

        eng_attr = {"pe": "tensor", "act": "scalar", "dve": "vector",
                    "sp": "sync", "pool": "gpsimd"}

        def slotcyc(n):
            return (n - 1) % self.NSLOT, (n - 1) // self.NSLOT + 1

        def make_body(e):
            def body(handle):
                waited = {}
                for i, o in enumerate(self.ops[e]):
                    deps = []
                    for d in sorted(o["deps"], key=repr):
                        if d[0] == "e":
                            if d[1] == e == "pe":
                                continue
                            deps.append((sems[d[1]], cum[d[1]][d[2]]))
                        else:
                            sl, cyc = slotcyc(d[2])
                            deps.append((ssems[d[1]][sl], 16 * cyc))
                    if o["dma"] is not None:
                        sname, n = o["dma"]
                        sl, cyc = slotcyc(n)
                        if cyc > 1:
                            deps.append((ssems[sname][sl], 16 * (cyc - 1)))
                    for sem, val in deps:
                        if val > 0 and waited.get(sem.name, -1) < val:
                            handle.wait_ge(sem, val)
                            waited[sem.name] = val
                    inst = o["fn"](handle)
                    if o["dma"] is not None:
                        sname, n = o["dma"]
                        sl, cyc = slotcyc(n)
                        inst.then_inc(ssems[sname][sl], 16)
                    elif pub[e][i]:
                        if inst is None:
                            inst = handle.nop()
                        inst.then_inc(sems[e], 1)
            return body

        for e in self.ENGINES:
            if self.ops[e]:
                getattr(block, eng_attr[e])(make_body(e))
        for cm in reversed(ctxs):
            cm.__exit__(None, None, None)


def build_nc():
    nc = bass.Bass("TRN2", target_bir_lowering=False)
    p = Prog(nc)

    def din(name, shape):
        return nc.dram_tensor(name, shape, F32, kind="ExternalInput").ap()

    x_loc = din("x_loc", [SC, D])
    x_rem = din("x_rem", [SC, D])
    ln1w_h = din("ln1w", [128, 8])
    ln1b_h = din("ln1b", [128, 8])
    ln2w_h = din("ln2w", [128, 8])
    ln2b_h = din("ln2b", [128, 8])
    w_h = {n: din(n, [D, D]) for n in ("wq", "wk", "wv", "wg", "wo")}
    w_in_h = din("w_in", [D, WIN])
    w_out_h = din("w_out", [HID, D])
    b_in_a_h = din("b_in_a", [128, 22])
    b_in_b_h = din("b_in_b", [128, 22])
    b_out_h = din("b_out_bc", [128, D])
    wgsh_h = din("wg_shard", [FFD, 128])
    wush_h = din("wu_shard", [FFD, 128])
    wdsh_h = din("wd_shard", [D, 3 * 128])
    agin_wg = nc.dram_tensor("agin_wg", [128, FFD], BF16, kind="Internal")
    agin_wu = nc.dram_tensor("agin_wu", [128, FFD], BF16, kind="Internal")
    agin_wd = nc.dram_tensor("agin_wd", [384, D], BF16, kind="Internal")
    agout_wg = nc.dram_tensor("agout_wg", [1024, FFD], BF16, kind="Internal")
    agout_wu = nc.dram_tensor("agout_wu", [1024, FFD], BF16, kind="Internal")
    agout_wd = nc.dram_tensor("agout_wd", [3072, D], BF16, kind="Internal")
    gqhl_h = [din(f"gq_{s}", [128, NP * 128]) for s in ("hi", "lo")]
    gkhl_h = [din(f"gk_{s}", [128, NP * 128]) for s in ("hi", "lo")]
    gknhl_h = [din(f"gkn_{s}", [128, D]) for s in ("hi", "lo")]
    gchl_h = [din(f"gc_{s}", [128, NP]) for s in ("hi", "lo")]
    mskT_h = din("mskT", [128, 128])
    ident_h = din("ident", [128, 128])
    flag_h = din("flag", [128, 1])

    y = nc.dram_tensor("y", [SC, D], F32, kind="ExternalOutput").ap()
    spill_x1 = nc.dram_tensor("spill_x1", [SC, D], BF16, kind="Internal").ap()
    spill_xa = nc.dram_tensor("spill_xa", [SC, D], BF16, kind="Internal").ap()
    spill_z1 = nc.dram_tensor("spill_z1", [SC, D], BF16, kind="Internal").ap()

    sb = nc.alloc_sbuf_tensor
    ident = sb("ident_sb", [128, 128], BF16)
    gc = sb("gc_sb", [128, NP], F32)
    ln1w = sb("ln1w_s", [128, 8], F32)
    ln1b = sb("ln1b_s", [128, 8], F32)
    ln2w = sb("ln2w_s", [128, 8], F32)
    ln2b = sb("ln2b_s", [128, 8], F32)
    bina = sb("bina_s", [128, 22], F32)
    binb = sb("binb_s", [128, 22], F32)
    bout = sb("bout_s", [128, D], BF16)
    flag = sb("flag_s", [128, 1], F32)
    WMAX = 3072
    scrf = sb("scrf", [128, WMAX], F32)
    scrg = sb("scrg", [128, WMAX], F32)
    scrh = sb("scrh", [128, WMAX], F32)
    scrp = sb("scrp", [128, WMAX], BF16)
    st = sb("st_all", [128, 14 * 16], F32)
    NB = 192
    qm = sb("qm_all", [128, 2 * NB], F32)

    ARENA = 164352
    base = SBUF_TOP - ARENA
    assert nc.sbuf_bytes_remaining >= ARENA + 64, nc.sbuf_bytes_remaining

    def at(name, off, shape, dtype):
        return nc.alloc_sbuf_tensor_at(name, shape, dtype, offset=base + off)

    ws = {}
    for i, n in enumerate(("wq", "wk", "wv", "wg", "wo")):
        ws[n] = at(n + "_s", i * 16384, [128, 8 * D], BF16)
    x0 = at("x0", 81920, [128, 2 * D], BF16)
    xrm = at("xrm", 86016, [128, 2 * D], BF16)
    G = 256
    xcT = at("xcT", 90112, [128, 8 * G], BF16)
    qhT = at("qhT", 94208, [128, NP * G], BF16)
    khT = at("khT", 98304, [128, NP * G], BF16)
    khn = at("khn", 102400, [128, 2 * D], BF16)
    vno = at("vno", 106496, [128, 2 * D], BF16)
    sg = at("sg", 110592, [128, 2 * D], BF16)
    xr = at("xr", 114688, [128, 2 * D], BF16)
    xrT = at("xrT", 118784, [128, 2 * D], BF16)
    zzb = at("zzb", 122880, [128, 2 * D], BF16)
    aT = at("aT", 126976, [128, 2 * 128], BF16)
    ro = at("ro", 127488, [128, D], F32)
    rcur = at("rcur", 131584, [128, NP * HD], F32)
    rnxt = at("rnxt", 133632, [128, NP * HD], F32)
    rbf = at("rbf", 135680, [128, NP * HD], BF16)
    x1b = at("x1b", 136704, [128, 2 * D], BF16)
    xab = at("xab", 140800, [128, 2 * D], BF16)

    w_in_s = at("w_in_s", 0, [128, 8 * WIN], BF16)
    w_out_s = at("w_out_s", 87360, [128, 22 * D], BF16)
    GB = 256
    xc2T = at("xc2T", 132416, [128, 8 * GB], BF16)
    hT = at("hT", 136512, [128, 22 * GB], BF16)
    abA = at("abA", 147776, [128, GB], BF16)
    abB = at("abB", 148288, [128, GB], BF16)
    x1g = at("x1g", 148800, [128, 2 * D], BF16)
    xag = at("xag", 152896, [128, 2 * D], BF16)
    z1b = at("z1b", 156992, [128, 2 * D], BF16)

    wgq = at("wgq", 0, [128, 8 * FFD], BF16)
    wuq = at("wuq", 45056, [128, 8 * FFD], BF16)
    wdq = at("wdq", 90112, [128, 22 * D], BF16)
    hqT = at("hqT", 135168, [128, 22 * 128], BF16)
    stage = at("prepstage", 144896, [128, 3 * D], F32)
    x0B = at("x0B", 144896, [128, 2 * D], BF16)
    xrB = at("xrB", 148992, [128, 2 * D], BF16)
    sqB = at("sqB", 153088, [128, D], F32)
    gq = at("gq_sb", 157184, [128, NP * 128], BF16)
    gk = at("gk_sb", 159232, [128, NP * 128], BF16)
    gkn = at("gkn_sb", 161280, [128, D], BF16)
    mskT = at("mskT_sb", 163328, [128, 128], BF16)
    identf = at("identf_sb", 163840, [128, 128], F32)
    xq = at("xq", 140800, [128, D], BF16)
    xqT = at("xqT", 142848, [128, D], BF16)
    sgt = at("sgt", 144896, [128, FFD], BF16)
    sut = at("sut", 150528, [128, FFD], BF16)
    z1g = at("z1g", 156160, [128, 2 * D], BF16)
    outf = at("outf", 160256, [128, D], F32)

    PS = [nc.alloc_psum_tensor(f"PS{i}", [128, 512], F32) for i in range(6)]
    PT = [nc.alloc_psum_tensor(f"PT{i}", [128, 512], BF16) for i in range(2)]
    PSK = [("PS", i) for i in range(6)]
    PTK = [("PT", i) for i in range(2)]
    p.op("sp", lambda h: None, writes=PSK + PTK, tag="psinit")

    def stv(j, cs=slice(0, 1), so=0):
        return st[:, 16 * (so + j):16 * (so + j + 1)][:, cs]
    ST = ("st", 0)

    def qmv(j, nb):
        return qm[:, NB * j:NB * j + nb]
    QM = ("qm",)
    REGS = [0, 88, 96, 176]
    FKS = [("scrf", r) for r in REGS]
    GKS = [("scrg", r) for r in REGS]
    HKS = [("scrh", r) for r in REGS]
    QMS = [("qm", r) for r in REGS]
    QOFF = 3072
    scrfb = scrf.bitcast(BF16)
    scrgb = scrg.bitcast(BF16)
    scrhb = scrh.bitcast(BF16)
    QREGS = [200, 288, 296, 376]
    HKS_ALL = HKS + [("scrh", r) for r in QREGS]
    PPK_ALL = [("scrp", r) for r in QREGS]

    def ld(stream, eng, dst_ap, src_ap, writes, reads=(), tag="ld"):
        def fn(h, dst=dst_ap, src=src_ap):
            return h.dma_start(out=dst, in_=src)
        p.dma(stream, eng, fn, reads=reads, writes=writes, tag=tag)

    for dst, src, key in [
        (mskT[:], mskT_h[:], ("mskT",)), (ident[:], ident_h[:], ("ident",)),
        (identf[:], ident_h[:], ("identf",)),
        (bout[:], b_out_h[:], ("bout",)),
    ]:
        ld("w", "pool", dst, src, [key])
    for i, (hl, dst, key, w_) in enumerate([
        (gqhl_h, gq, ("gq",), NP * 128), (gkhl_h, gk, ("gk",), NP * 128),
        (gknhl_h, gkn, ("gkn",), D), (gchl_h, gc, ("gc",), NP),
    ]):
        ld("w", "pool", scrf[:, 0:w_], hl[0][:], FKS)
        ld("w", "pool", scrg[:, 0:w_], hl[1][:], GKS)
        p.op("dve", lambda h, dst=dst, w_=w_: h.tensor_tensor(
            dst[:, 0:w_], scrf[:, 0:w_], scrg[:, 0:w_], ALU.add),
            reads=FKS + GKS, writes=[key], tag=f"ghl{i}")
    for dst, src, key in [
        (ln1w[:], ln1w_h[:], ("ln1w",)), (ln1b[:], ln1b_h[:], ("ln1b",)),
        (ln2w[:], ln2w_h[:], ("ln2w",)), (ln2b[:], ln2b_h[:], ("ln2b",)),
        (bina[:], b_in_a_h[:], ("bina",)), (binb[:], b_in_b_h[:], ("binb",)),
        (flag[:], flag_h[:], ("flag",)),
    ]:
        ld("c", "sp", dst, src, [key])
    for nm in ("wq", "wk", "wv", "wg", "wo"):
        for kt in range(8):
            ld("w", "pool", ws[nm][:, kt * D:(kt + 1) * D],
               w_h[nm][kt * 128:(kt + 1) * 128, :], [(nm, kt)])

    def ln_stats(src_ap, W, eps, src_keys, nst=1, tag="", sq_ap=None,
                 sq_keys=None, so=0):
        if sq_ap is None:
            sq_ap, sq_keys = scrf[:, 0:W], FKS
        ST = ("st", so)
        cs = slice(0, nst)
        if nst == 1:
            p.op("dve", lambda h, s=src_ap: h.reduce_sum(
                stv(0, cs, so), s, axis=AX), reads=src_keys, writes=[ST],
                tag="lnrs" + tag)
            p.op("act", lambda h, s=src_ap: h.activation(
                sq_ap, s, AF.Square, accum_out=stv(1, cs, so)),
                reads=list(src_keys) + [ST], writes=[ST] + list(sq_keys),
                tag="lnsq" + tag)
        else:
            v3 = src_ap.rearrange("p (g w) -> p g w", w=W)
            p.op("dve", lambda h, s=v3: h.reduce_sum(
                stv(0, cs, so), s, axis=AX), reads=src_keys, writes=[ST],
                tag="lnrs" + tag)
            p.op("act", lambda h, s=src_ap: h.activation(
                scrf[:, 0:nst * W], s, AF.Square),
                reads=src_keys, writes=FKS, tag="lnsq" + tag)
            p.op("dve", lambda h: h.reduce_sum(
                stv(1, cs, so), scrf[:, 0:nst * W].rearrange(
                    "p (g w) -> p g w", w=W), axis=AX),
                reads=FKS + [ST], writes=[ST], tag="lnss" + tag)
        p.op("dve", lambda h: h.tensor_scalar(
            stv(2, cs, so), stv(0, cs, so), 1.0 / W, None, ALU.mult),
            reads=[ST], writes=[ST], tag="lnmu" + tag)
        p.op("dve", lambda h: h.tensor_tensor(
            stv(3, cs, so), stv(2, cs, so), stv(2, cs, so), ALU.mult),
            reads=[ST], writes=[ST], tag="lnm2" + tag)
        p.op("dve", lambda h: h.scalar_tensor_tensor(
            stv(4, cs, so), stv(1, cs, so), 1.0 / W, stv(3, cs, so),
            ALU.mult, ALU.subtract),
            reads=[ST], writes=[ST], tag="lnva" + tag)
        p.op("dve", lambda h: h.tensor_scalar(
            stv(4, cs, so), stv(4, cs, so), float(eps), None, ALU.add),
            reads=[ST], writes=[ST], tag="lnve" + tag)
        p.op("act", lambda h: h.activation(
            stv(5, cs, so), stv(4, cs, so), AF.Sqrt),
            reads=[ST], writes=[ST], tag="lnsd" + tag)
        p.op("dve", lambda h: h.reciprocal(stv(6, cs, so), stv(5, cs, so)),
            reads=[ST], writes=[ST], tag="lnrd" + tag)
        return cs

    def transpose_tile(src_ap, src_keys, dstT_fn, dst_wkeys, wb=None, tag=""):
        for half in range(2):
            bank, bkey = PT[half], PTK[half]
            for j in range(4):
                kt = half * 4 + j
                p.op("pe", lambda h, kt=kt, j=j, bank=bank, s=src_ap: h.transpose(
                    bank[:, j * 128:(j + 1) * 128],
                    s[:, kt * 128:(kt + 1) * 128], ident[:]),
                    reads=list(src_keys) + [("ident",)],
                    writes=[bkey], tag="tr" + tag)
            for j in range(4):
                kt = half * 4 + j
                kw, rd, func = {}, [bkey], AF.Copy
                if wb is not None:
                    kw["scale"] = wb[0][:, kt:kt + 1]
                    kw["bias"] = wb[2][:, kt:kt + 1]
                    rd += [wb[1], wb[3]]
                    func = AF.Identity
                p.op("act", lambda h, kt=kt, j=j, bank=bank, kw=kw,
                     func=func: h.activation(
                    dstT_fn(kt), bank[:, j * 128:(j + 1) * 128], func, **kw),
                    reads=rd, writes=[dst_wkeys(kt)], tag="trev" + tag)

    def ln_chunk(src_ap, src_keys, eps, wb, xc_ap, xc_key, dstT_fn, dstT_wkeys,
                 tag="", so=0, sq_ap=None, sq_keys=None):
        cs = ln_stats(src_ap, D, eps, src_keys, tag=tag, so=so,
                      sq_ap=sq_ap, sq_keys=sq_keys)
        p.op("dve", lambda h, s=src_ap, cs=cs, so=so: h.tensor_scalar(
            xc_ap, s, stv(2, cs, so), stv(6, cs, so),
            ALU.subtract, ALU.mult),
            reads=list(src_keys) + [("st", so)], writes=[xc_key],
            tag="lnap" + tag)
        transpose_tile(xc_ap, [xc_key], dstT_fn, dstT_wkeys, wb=wb, tag=tag)

    def proj(ps_ap, bank_key, lhsT_fn, rhs_fn, nk, rd_fn, tag=""):
        for kt in range(nk):
            p.op("pe", lambda h, kt=kt: h.matmul(
                ps_ap, lhsT_fn(kt), rhs_fn(kt),
                start=(kt == 0), stop=(kt == nk - 1)),
                reads=rd_fn(kt) + ([bank_key] if kt else []),
                writes=[bank_key], tag=tag)

    def state_update(khn_ap, v_ap, kv_keys, tag=""):
        PD, PDK = PS[5], PSK[5]
        p.op("dve", lambda h: h.tensor_tensor(
            scrg[:, 0:NP * HD].rearrange("p (g w) -> p g w", w=HD),
            rcur[:].rearrange("p (g w) -> p g w", w=HD),
            gc[:, :, None].broadcast_to((128, NP, HD)), ALU.mult),
            reads=[("rcur",), ("gc",)], writes=GKS, tag="rsc" + tag)
        for pr in range(NP):
            c0 = pr * HD
            p.op("pe", lambda h, pr=pr: h.matmul(
                PD[0:64, 0:64], khn_ap[:, pr * 128:pr * 128 + 64],
                v_ap[:, pr * 128:pr * 128 + 64], start=True, stop=True,
                tile_position=(0, 0)),
                reads=kv_keys, writes=[PDK], tag="sdA" + tag)
            p.op("pe", lambda h, pr=pr: h.matmul(
                PD[64:128, 0:64], khn_ap[:, pr * 128 + 64:pr * 128 + 128],
                v_ap[:, pr * 128 + 64:pr * 128 + 128], start=True, stop=True,
                tile_position=(0, 64)),
                reads=kv_keys + [PDK], writes=[PDK], tag="sdB" + tag)
            p.op("dve", lambda h, pr=pr, c0=c0: h.scalar_tensor_tensor(
                rnxt[:, c0:c0 + HD], PD[:, 0:64], gc[:, pr:pr + 1],
                scrg[:, c0:c0 + HD], ALU.mult, ALU.add),
                reads=[PDK, ("gc",)] + GKS, writes=[("rnxt", pr)],
                tag="rup" + tag)
        p.op("dve", lambda h: h.tensor_copy(rcur[:], rnxt[:]),
            reads=[("rnxt", pr) for pr in range(NP)], writes=[("rcur",)],
            tag="rsw" + tag)

    def quant_tile(x_ap, x_keys, W, out_ap, out_wkey, rows=128, tag="",
                   soff=0, nsl=2, sseq=False, out_wkeys_fn=None):
        ws = W // nsl
        assert ws % 16 == 0
        SL = []
        for j in range(nsl):
            cx = j * ws
            c0 = soff if sseq else soff + cx
            nb, b0 = ws // 16, c0 // 16
            ki = 200 + c0 // 16
            x3 = x_ap[:, cx:cx + ws].rearrange("p (b k) -> p b k", k=16)
            m0 = qmv(0, NB)[0:rows, b0:b0 + nb]
            m1 = qmv(1, NB)[0:rows, b0:b0 + nb]
            SL.append((c0 + QOFF, ws, x3, m0, m1, nb, cx, ki))
        ORD = {s[7]: j for j, s in enumerate(SL)}

        STAGES = []

        def sl_op(eng, fn, rd, wr, tag_):
            STAGES.append((eng, fn, rd, wr, tag_))

        XK = list(x_keys)
        sl_op("dve", lambda c0, w, x3, m0, m1, nb, cx, ki:
              (lambda h: h.reduce_max(m0, x3, axis=AX,
                                      apply_absolute_value=True)),
              lambda i: XK, lambda i: [("qm", i)], "qbm" + tag)
        sl_op("act", lambda c0, w, x3, m0, m1, nb, cx, ki:
              (lambda h: h.activation(scrgb[0:rows, c0:c0 + w],
                                      x_ap[:, cx:cx + w], AF.Abs)),
              lambda i: XK, lambda i: [("scrg", i)], "qab" + tag)
        sl_op("dve", lambda c0, w, x3, m0, m1, nb, cx, ki:
              (lambda h: h.tensor_scalar(m1, m0, 1.0 / 6.0, 1e-12,
                                         ALU.mult, ALU.max)),
              lambda i: [("qm", i)], lambda i: [("qm", i)], "qs6")
        sl_op("dve", lambda c0, w, x3, m0, m1, nb, cx, ki:
              (lambda h: h.reciprocal(m0, m1)),
              lambda i: [("qm", i)], lambda i: [("qm", i)], "qre")
        sl_op("pool", lambda c0, w, x3, m0, m1, nb, cx, ki:
              (lambda h: h.tensor_tensor(
                  scrfb[0:rows, c0:c0 + w].rearrange("p (b k) -> p b k", k=16),
                  scrgb[0:rows, c0:c0 + w].rearrange("p (b k) -> p b k", k=16),
                  m0[:, :, None].broadcast_to((rows, nb, 16)), ALU.mult)),
              lambda i: [("scrg", i), ("qm", i)],
              lambda i: [("scrf", i)], "qt_" + tag)
        sl_op("dve", lambda c0, w, x3, m0, m1, nb, cx, ki:
              (lambda h: h.tensor_scalar(scrhb[0:rows, c0:c0 + w],
                                         scrfb[0:rows, c0:c0 + w], 0.25, 1.0,
                                         ALU.is_gt, ALU.mult)),
              lambda i: [("scrf", i)], lambda i: [("scrh", i)], "qt0")
        for t in (0.75, 1.25, 1.75):
            sl_op("dve", lambda c0, w, x3, m0, m1, nb, cx, ki, t=t:
                  (lambda h: h.scalar_tensor_tensor(
                      scrhb[0:rows, c0:c0 + w], scrfb[0:rows, c0:c0 + w], t,
                      scrhb[0:rows, c0:c0 + w], ALU.is_gt, ALU.add)),
                  lambda i: [("scrf", i), ("scrh", i)],
                  lambda i: [("scrh", i)], "qtl")
        sl_op("dve", lambda c0, w, x3, m0, m1, nb, cx, ki:
              (lambda h: h.tensor_scalar(scrp[0:rows, c0 - QOFF:c0 - QOFF + w],
                                         scrfb[0:rows, c0:c0 + w], 2.5, 1.0,
                                         ALU.is_gt, ALU.mult)),
              lambda i: [("scrf", i)], lambda i: [("scrp", i)], "qb0")
        for t in (3.5, 5.0, 5.0):
            sl_op("dve", lambda c0, w, x3, m0, m1, nb, cx, ki, t=t:
                  (lambda h: h.scalar_tensor_tensor(
                      scrp[0:rows, c0 - QOFF:c0 - QOFF + w],
                      scrfb[0:rows, c0:c0 + w], t,
                      scrp[0:rows, c0 - QOFF:c0 - QOFF + w],
                      ALU.is_gt, ALU.add)),
                  lambda i: [("scrf", i), ("scrp", i)],
                  lambda i: [("scrp", i)], "qbl")
        sl_op("act", lambda c0, w, x3, m0, m1, nb, cx, ki:
              (lambda h: h.activation(scrgb[0:rows, c0:c0 + w],
                                      x_ap[:, cx:cx + w], AF.Sign)),
              lambda i: XK + [("scrf", i)], lambda i: [("scrg", i)],
              "qsg" + tag)
        sl_op("dve", lambda c0, w, x3, m0, m1, nb, cx, ki:
              (lambda h: h.scalar_tensor_tensor(
                  scrhb[0:rows, c0:c0 + w], scrhb[0:rows, c0:c0 + w], 0.5,
                  scrp[0:rows, c0 - QOFF:c0 - QOFF + w], ALU.mult, ALU.add)),
              lambda i: [("scrh", i), ("scrp", i)],
              lambda i: [("scrh", i)], "qcm")
        sl_op("pool", lambda c0, w, x3, m0, m1, nb, cx, ki:
              (lambda h: h.tensor_tensor(
                  scrhb[0:rows, c0:c0 + w].rearrange("p (b k) -> p b k", k=16),
                  scrhb[0:rows, c0:c0 + w].rearrange("p (b k) -> p b k", k=16),
                  m1[:, :, None].broadcast_to((rows, nb, 16)), ALU.mult)),
              lambda i: [("scrh", i), ("qm", i)],
              lambda i: [("scrh", i)], "qds")
        sl_op("pool", lambda c0, w, x3, m0, m1, nb, cx, ki:
              (lambda h: h.tensor_tensor(
                  out_ap[:, cx:cx + w], scrhb[0:rows, c0:c0 + w],
                  scrgb[0:rows, c0:c0 + w], ALU.mult)),
              lambda i: [("scrh", i), ("scrg", i)],
              (lambda i: [out_wkey] if out_wkeys_fn is None
               else [out_wkey] + out_wkeys_fn(ORD[i])), "qo" + tag)

        if sseq:
            for s in SL:
                for eng, fn, rd, wr, tag_ in STAGES:
                    p.op(eng, fn(*s), reads=rd(s[7]), writes=wr(s[7]),
                         tag=tag_)
        else:
            for eng, fn, rd, wr, tag_ in STAGES:
                for s in SL:
                    p.op(eng, fn(*s), reads=rd(s[7]), writes=wr(s[7]),
                         tag=tag_)

    def prep_shard(src_h, nt, kc, agin_t, agkey, tag):
        W = nt * kc
        ld("w", "pool",
           stage[:, 0:W].rearrange("p (t k) -> p t k", k=kc),
           src_h.rearrange("(t p) k -> p t k", p=128),
           [("prepstage",)], tag="psh" + tag)
        quant_tile(stage[:, 0:W], [("prepstage",)], W,
                   stage[:, 0:W], ("prepstage",), tag="pq" + tag)
        ntile = W // 128
        for grp in range((ntile + 3) // 4):
            tls = list(range(grp * 4, min(ntile, grp * 4 + 4)))
            bank, bkey = PS[2 + grp % 2], PSK[2 + grp % 2]
            for j, tl in enumerate(tls):
                p.op("pe", lambda h, j=j, tl=tl, bank=bank: h.transpose(
                    bank[:, j * 128:(j + 1) * 128],
                    stage[:, tl * 128:(tl + 1) * 128], identf[:]),
                    reads=[("prepstage",), ("identf",)],
                    writes=[bkey], tag="ptr" + tag)
            for j, tl in enumerate(tls):
                nkc = kc // 128
                t_, jj = tl // nkc, tl % nkc
                dcol = jj * (nt * 128) + t_ * 128
                p.op("act", lambda h, j=j, dcol=dcol, bank=bank: h.activation(
                    scrp[:, dcol:dcol + 128], bank[:, j * 128:(j + 1) * 128],
                    AF.Copy),
                    reads=[bkey], writes=PPK_ALL, tag="pev" + tag)
        p.dma("sp1", "sp",
              lambda h, agin_t=agin_t, nt=nt, kc=kc: h.dma_start(
                  out=agin_t.ap().rearrange("(jj p) n -> p jj n", p=128),
                  in_=scrp[:, 0:W].rearrange("p (jj n) -> p jj n",
                                             n=nt * 128)),
              reads=PPK_ALL, writes=[agkey], tag="pst" + tag)

    prep_shard(wgsh_h, 22, 128, agin_wg, ("agin", "wg"), "g")
    prep_shard(wush_h, 22, 128, agin_wu, ("agin", "wu"), "u")
    prep_shard(wdsh_h, 8, 384, agin_wd, ("agin", "wd"), "d")

    p.op("dve", lambda h: h.memset(rcur[:], 0.0), writes=[("rcur",)], tag="ri")

    for rc in range(NT):
        par = rc % 2
        xs = xrm[:, par * D:(par + 1) * D]
        XB = ("xrmb", par)
        ld("w", "pool", xs, x_rem[rc * 128:(rc + 1) * 128, :], [XB])
        ln_chunk(xs, [XB], EPS1, (ln1w, ("ln1w",), ln1b, ("ln1b",)),
                 xr[:, par * D:(par + 1) * D], ("xrb", 0, par),
                 lambda kt, par=par: xcT[:, kt * G + par * 128:kt * G + par * 128 + 128],
                 lambda kt, par=par: ("xcTb", par, kt), tag=f"a{rc}")
        xcT_keys = [("xcTb", par, kt) for kt in range(8)]
        for half in range(2):
            ps, pk = PS[half], PSK[half]
            proj(ps[:, 0:512], pk,
                 lambda kt, par=par: xcT[:, kt * G + par * 128:kt * G + par * 128 + 128],
                 lambda kt, half=half: ws["wk"][:, kt * D + half * 512:kt * D + (half + 1) * 512],
                 8, lambda kt: xcT_keys + [("wk", kt)], tag=f"a0k{rc}")
            p.op("dve", lambda h, ps=ps, half=half, par=par: h.tensor_tensor(
                khn[:, par * D + half * 512:par * D + (half + 1) * 512],
                ps[:, 0:512], gkn[:, half * 512:(half + 1) * 512], ALU.mult),
                reads=[pk, ("gkn",)], writes=[("khnb", par, half)], tag="a0ek")
            ps2, pk2 = PS[2 + half], PSK[2 + half]
            proj(ps2[:, 0:512], pk2,
                 lambda kt, par=par: xcT[:, kt * G + par * 128:kt * G + par * 128 + 128],
                 lambda kt, half=half: ws["wv"][:, kt * D + half * 512:kt * D + (half + 1) * 512],
                 8, lambda kt: xcT_keys + [("wv", kt)], tag=f"a0v{rc}")
            p.op("act", lambda h, ps2=ps2, half=half, par=par: h.activation(
                vno[:, par * D + half * 512:par * D + (half + 1) * 512],
                ps2[:, 0:512], AF.Copy),
                reads=[pk2], writes=[("vnob", par, half)], tag="a0ev")
        state_update(khn[:, par * D:(par + 1) * D], vno[:, par * D:(par + 1) * D],
                     [("khnb", par, 0), ("khnb", par, 1),
                      ("vnob", par, 0), ("vnob", par, 1)], tag=f"a0{rc}")

    p.op("dve", lambda h: h.tensor_scalar(
        rcur[:], rcur[:], flag[:, 0:1], None, ALU.mult),
        reads=[("rcur",), ("flag",)], writes=[("rcur",)], tag="rflag")
    p.op("dve", lambda h: h.tensor_copy(rbf[:], rcur[:]),
        reads=[("rcur",)], writes=[("rbf",)], tag="rbf0")

    RG8 = [[0, 1, 2, 3, 4, 5, 6, 7]]

    def emit_ags():
        for tg, ain, aout in [("wg", agin_wg, agout_wg),
                              ("wu", agin_wu, agout_wu),
                              ("wd", agin_wd, agout_wd)]:
            p.op("pool", lambda h, ain=ain, aout=aout: h.collective_compute(
                "AllGather", ALU.bypass, replica_groups=RG8,
                ins=[ain.ap().opt()], outs=[aout.ap().opt()]),
                reads=[("agin", tg)], writes=[("ag", tg)], tag="ag" + tg)

    def emit_ln1(g, part):
        gp = g % 2
        xb = x0 if gp == 0 else x0B
        xrb = xr if gp == 0 else xrB
        so = 0 if gp == 0 else 7
        sqa = None if gp == 0 else sqB[:, 0:D]
        sqk = None if gp == 0 else [("sqB",)]
        if g == 1 and part == 1:
            p.op("sp", lambda h: None, reads=[("prepstage",)],
                 writes=[("x0b", 1, 0), ("x0b", 1, 1), ("xrb", 1, 0),
                         ("xrb", 1, 1), ("sqB",)], tag="claimB")
        for par in range(2):
            ci = 2 * g + par
            XK = ("x0b", gp, par)
            if part == 1:
                ld("w", "pool", xb[:, par * D:(par + 1) * D],
                   x_loc[ci * 128:(ci + 1) * 128, :], [XK])
                cs = ln_stats(xb[:, par * D:(par + 1) * D], D, EPS1, [XK],
                              tag=f"A{ci}", so=so, sq_ap=sqa, sq_keys=sqk)
                p.op("dve", lambda h, par=par, cs=cs, so=so, xb=xb,
                     xrb=xrb: h.tensor_scalar(
                         xrb[:, par * D:(par + 1) * D],
                         xb[:, par * D:(par + 1) * D],
                         stv(2, cs, so), stv(6, cs, so),
                         ALU.subtract, ALU.mult),
                     reads=[XK, ("st", so)], writes=[("xrb", gp, par)],
                     tag=f"lnapA{ci}")
            else:
                transpose_tile(
                    xrb[:, par * D:(par + 1) * D], [("xrb", gp, par)],
                    lambda kt, par=par: xcT[:, kt * G + par * 128:kt * G + par * 128 + 128],
                    lambda kt, par=par: ("xcTb", par, kt),
                    wb=(ln1w, ("ln1w",), ln1b, ("ln1b",)), tag=f"A{ci}")

    emit_ln1(0, 1)
    emit_ln1(0, 2)
    for g in range(NT // 2):
        gp = g % 2
        xb = x0 if gp == 0 else x0B
        xrb = xr if gp == 0 else xrB
        if g == 4:
            emit_ags()
        ck0 = 2 * g
        xcT_keys = [("xcTb", pp, kt) for pp in range(2) for kt in range(8)]

        for pr in range(NP):
            ps, pk = PS[pr % 2], PSK[pr % 2]
            proj(ps[:, 0:G], pk,
                 lambda kt, pr=pr: ws["wq"][:, kt * D + pr * 128:kt * D + (pr + 1) * 128],
                 lambda kt: xcT[:, kt * G:(kt + 1) * G],
                 8, lambda kt: xcT_keys + [("wq", kt)], tag=f"pq{pr}")
            for cc in range(2):
                p.op("dve", lambda h, ps=ps, pr=pr, cc=cc: h.tensor_tensor(
                    qhT[:, pr * G + cc * 128:pr * G + (cc + 1) * 128],
                    ps[:, cc * 128:(cc + 1) * 128],
                    gq[:, pr * 128:(pr + 1) * 128], ALU.mult),
                    reads=[pk, ("gq",)], writes=[("qhTb", pr, cc)], tag="evq")
            ps2, pk2 = PS[2 + pr % 2], PSK[2 + pr % 2]
            proj(ps2[:, 0:G], pk2,
                 lambda kt, pr=pr: ws["wk"][:, kt * D + pr * 128:kt * D + (pr + 1) * 128],
                 lambda kt: xcT[:, kt * G:(kt + 1) * G],
                 8, lambda kt: xcT_keys + [("wk", kt)], tag=f"pkT{pr}")
            for cc in range(2):
                p.op("dve", lambda h, ps2=ps2, pr=pr, cc=cc: h.tensor_tensor(
                    khT[:, pr * G + cc * 128:pr * G + (cc + 1) * 128],
                    ps2[:, cc * 128:(cc + 1) * 128],
                    gk[:, pr * 128:(pr + 1) * 128], ALU.mult),
                    reads=[pk2, ("gk",)], writes=[("khTb", pr, cc)], tag="evk")

        for par in range(2):
            ci = ck0 + par
            for pr in range(NP):
                bank, bkey = PT[pr % 2], PTK[pr % 2]
                p.op("pe", lambda h, pr=pr, par=par, bank=bank: h.transpose(
                    bank[:, 0:128],
                    khT[:, pr * G + par * 128:pr * G + par * 128 + 128],
                    ident[:]),
                    reads=[("khTb", pr, par), ("ident",)], writes=[bkey],
                    tag="ktr")
                p.op("act", lambda h, pr=pr, par=par, bank=bank: h.activation(
                    khn[:, par * D + pr * 128:par * D + (pr + 1) * 128],
                    bank[:, 0:128], AF.Copy),
                    reads=[bkey], writes=[("khnb", par, pr // 4)], tag="kte")
            for half in range(2):
                ps2, pk2 = PS[5], PSK[5]
                proj(ps2[:, 0:512], pk2,
                     lambda kt, par=par: xcT[:, kt * G + par * 128:kt * G + par * 128 + 128],
                     lambda kt, half=half: ws["wv"][:, kt * D + half * 512:kt * D + (half + 1) * 512],
                     8, lambda kt: xcT_keys + [("wv", kt)], tag=f"nv{ci}")
                p.op("act", lambda h, ps2=ps2, half=half, par=par: h.activation(
                    vno[:, par * D + half * 512:par * D + (half + 1) * 512],
                    ps2[:, 0:512], AF.Copy),
                    reads=[pk2], writes=[("vnob", par, half)], tag="nev")
                ps3, pk3 = PS[half], PSK[half]
                proj(ps3[:, 0:512], pk3,
                     lambda kt, par=par: xcT[:, kt * G + par * 128:kt * G + par * 128 + 128],
                     lambda kt, half=half: ws["wg"][:, kt * D + half * 512:kt * D + (half + 1) * 512],
                     8, lambda kt: xcT_keys + [("wg", kt)], tag=f"ng{ci}")
                p.op("act", lambda h, ps3=ps3, half=half, par=par: h.activation(
                    sg[:, par * D + half * 512:par * D + half * 512 + 512],
                    ps3[:, 0:512], AF.Silu),
                    reads=[pk3], writes=[("sgb", par, half)], tag="neg")

        if g + 1 < NT // 2:
            emit_ln1(g + 1, 1)
            emit_ln1(g + 1, 2)

        for par in range(2):
            ci = ck0 + par
            v_c = vno[:, par * D:(par + 1) * D]
            kc_keys = [("khnb", par, 0), ("khnb", par, 1)]
            vc_keys = [("vnob", par, 0), ("vnob", par, 1)]
            PO, POK = PS[4], PSK[4]
            for pr in range(NP):
                qsl = qhT[:, pr * G + par * 128:pr * G + par * 128 + 128]
                qkey = ("qhTb", pr, par)
                ksl = khT[:, pr * G + par * 128:pr * G + par * 128 + 128]
                kkey = ("khTb", pr, par)
                for hh in range(2):
                    PA, PAK = PS[hh], PSK[hh]
                    p.op("pe", lambda h, PA=PA, hh=hh, qsl=qsl, ksl=ksl: h.matmul(
                        PA[:, 0:128], ksl[64 * hh:64 * (hh + 1), :],
                        qsl[64 * hh:64 * (hh + 1), :], start=True, stop=True,
                        tile_position=(64 * hh, 0)),
                        reads=[qkey, kkey], writes=[PAK], tag="mmA")
                    p.op("dve", lambda h, PA=PA, hh=hh: h.tensor_tensor(
                        aT[:, hh * 128:(hh + 1) * 128], PA[:, 0:128],
                        mskT[:], ALU.mult),
                        reads=[PAK, ("mskT",)], writes=[("aTb", hh)], tag="mask")
                for hh in range(2):
                    hcol = (2 * pr + hh) * HD
                    p.op("pe", lambda h, PO=PO, hh=hh, hcol=hcol, v_c=v_c: h.matmul(
                        PO[:, hh * 64:(hh + 1) * 64],
                        aT[:, hh * 128:(hh + 1) * 128],
                        v_c[:, hcol:hcol + HD], start=(hh == 0), stop=False),
                        reads=[("aTb", hh)] + vc_keys + [POK], writes=[POK],
                        tag="mmI")
                    p.op("pe", lambda h, PO=PO, hh=hh, pr=pr, qsl=qsl: h.matmul(
                        PO[:, hh * 64:(hh + 1) * 64],
                        qsl[64 * hh:64 * (hh + 1), :],
                        rbf[64 * hh:64 * (hh + 1), pr * HD:(pr + 1) * HD],
                        start=False, stop=(hh == 1), tile_position=(64 * hh, 0)),
                        reads=[qkey, ("rbf",), POK], writes=[POK], tag="mmX")
                p.op("act", lambda h, PO=PO, pr=pr: h.activation(
                    ro[:, pr * 128:(pr + 1) * 128], PO[:, 0:128], AF.Copy),
                    reads=[POK], writes=[("rob", pr)], tag="evo")
            state_update(khn[:, par * D:(par + 1) * D], v_c,
                         kc_keys + vc_keys, tag=f"A{ci}")
            p.op("dve", lambda h: h.tensor_copy(rbf[:], rcur[:]),
                reads=[("rcur",)], writes=[("rbf",)], tag="rbfA")

            ro_keys = [("rob", pr) for pr in range(NP)]
            cs = ln_stats(ro[:], HD, EPSH, ro_keys, nst=16, tag=f"hl{ci}")
            p.op("dve", lambda h, cs=cs: h.tensor_tensor(
                scrg[:, 0:D].rearrange("p (g w) -> p g w", w=HD),
                ro[:].rearrange("p (g w) -> p g w", w=HD),
                stv(2, cs)[:, :, None].broadcast_to((128, 16, HD)),
                ALU.subtract),
                reads=ro_keys + [ST], writes=GKS, tag="gnt1")
            p.op("dve", lambda h, par=par, cs=cs: h.tensor_tensor(
                zzb[:, par * D:(par + 1) * D].rearrange("p (g w) -> p g w", w=HD),
                sg[:, par * D:(par + 1) * D].rearrange("p (g w) -> p g w", w=HD),
                stv(6, cs)[:, :, None].broadcast_to((128, 16, HD)), ALU.mult),
                reads=[("sgb", par, 0), ("sgb", par, 1), ST],
                writes=[("zzbb", par)], tag="gnz")
            p.op("dve", lambda h, par=par, xrb=xrb: h.tensor_tensor(
                xrb[:, par * D:(par + 1) * D], scrg[:, 0:D],
                zzb[:, par * D:(par + 1) * D], ALU.mult),
                reads=GKS + [("zzbb", par)], writes=[("xrb", gp, par)],
                tag="gnxr")
            transpose_tile(xrb[:, par * D:(par + 1) * D], [("xrb", gp, par)],
                           lambda kt, par=par: xrT[:, par * D + kt * 128:par * D + (kt + 1) * 128],
                           lambda kt, par=par: ("xrTb", par, kt), tag=f"xr{ci}")
            p.op("dve", lambda h, par=par, xb=xb: h.scalar_tensor_tensor(
                scrh[:, 0:D], xb[:, par * D:(par + 1) * D], 2.0, bout[:],
                ALU.mult, ALU.add),
                reads=[("x0b", gp, par), ("bout",)], writes=HKS, tag="x0b2")
            for half in range(2):
                ps, pk = PS[half], PSK[half]
                proj(ps[:, 0:512], pk,
                     lambda kt, par=par: xrT[:, par * D + kt * 128:par * D + (kt + 1) * 128],
                     lambda kt, half=half: ws["wo"][:, kt * D + half * 512:kt * D + (half + 1) * 512],
                     8, lambda kt, par=par: [("xrTb", par, kt), ("wo", kt)],
                     tag=f"po{ci}")
                sl = slice(half * 512, (half + 1) * 512)
                p.op("dve", lambda h, ps=ps, sl=sl, par=par, xb=xb: h.tensor_tensor(
                    x1b[:, par * D:(par + 1) * D][:, sl], ps[:, 0:512],
                    xb[:, par * D:(par + 1) * D][:, sl], ALU.add),
                    reads=[pk, ("x0b", gp, par)], writes=[("x1bb", par, half)],
                    tag="evx1")
                p.op("dve", lambda h, ps=ps, sl=sl, par=par: h.tensor_tensor(
                    xab[:, par * D:(par + 1) * D][:, sl], ps[:, 0:512],
                    scrh[:, 0:D][:, sl], ALU.add),
                    reads=[pk] + HKS, writes=[("xabb", par, half)],
                    tag="evxa")
            p.dma("sp1", "sp",
                  lambda h, ci=ci, par=par: h.dma_start(
                      out=spill_x1[ci * 128:(ci + 1) * 128, :],
                      in_=x1b[:, par * D:(par + 1) * D]),
                  reads=[("x1bb", par, 0), ("x1bb", par, 1)],
                  writes=[("sx1", ci)], tag="spx1")
            p.dma("sp1", "sp",
                  lambda h, ci=ci, par=par: h.dma_start(
                      out=spill_xa[ci * 128:(ci + 1) * 128, :],
                      in_=xab[:, par * D:(par + 1) * D]),
                  reads=[("xabb", par, 0), ("xabb", par, 1)],
                  writes=[("sxa", ci)], tag="spxa")

    p.fence("AB")
    for kt in range(8):
        ld("w", "pool", w_in_s[:, kt * WIN:(kt + 1) * WIN],
           w_in_h[kt * 128:(kt + 1) * 128, :], [("w_in", kt)])
    for kt in range(22):
        c0, cw = ACT_TILES[kt]
        ld("w", "pool", w_out_s[0:cw, kt * D:(kt + 1) * D],
           w_out_h[c0:c0 + cw, :], [("w_out", kt)])

    for g in range(NT // 2):
        ck0 = 2 * g
        for par in range(2):
            ci = ck0 + par
            ld("w", "pool", x1g[:, par * D:(par + 1) * D],
               spill_x1[ci * 128:(ci + 1) * 128, :], [("x1gb", par)],
               reads=[("sx1", ci)])
            ld("w", "pool", xag[:, par * D:(par + 1) * D],
               spill_xa[ci * 128:(ci + 1) * 128, :], [("xagb", par)],
               reads=[("sxa", ci)])
            ln_chunk(x1g[:, par * D:(par + 1) * D], [("x1gb", par)], EPS1,
                     (ln2w, ("ln2w",), ln2b, ("ln2b",)),
                     z1b[:, par * D:(par + 1) * D], ("z1bb", par),
                     lambda kt, par=par: xc2T[:, kt * GB + par * 128:kt * GB + par * 128 + 128],
                     lambda kt, par=par: ("x2Tb", par, kt), tag=f"B{ci}")
        x2T_keys = [("x2Tb", pp, kt) for pp in range(2) for kt in range(8)]

        for ct in range(22):
            c0a, cw = ACT_TILES[ct]
            psA, pkA = PS[ct % 2], PSK[ct % 2]
            psB, pkB = PS[2 + ct % 2], PSK[2 + ct % 2]
            proj(psA[0:cw, 0:GB], pkA,
                 lambda kt, c0a=c0a, cw=cw: w_in_s[:, kt * WIN + c0a:kt * WIN + c0a + cw],
                 lambda kt: xc2T[:, kt * GB:(kt + 1) * GB],
                 8, lambda kt: x2T_keys + [("w_in", kt)], tag=f"pA{ct}")
            proj(psB[0:cw, 0:GB], pkB,
                 lambda kt, c0a=c0a, cw=cw: w_in_s[:, kt * WIN + HID + c0a:kt * WIN + HID + c0a + cw],
                 lambda kt: xc2T[:, kt * GB:(kt + 1) * GB],
                 8, lambda kt: x2T_keys + [("w_in", kt)], tag=f"pB{ct}")
            p.op("act", lambda h, psA=psA, cw=cw, ct=ct: h.activation(
                abA[0:cw, :], psA[0:cw, 0:GB], AF.Identity,
                bias=bina[0:cw, ct:ct + 1]),
                reads=[pkA, ("bina",)], writes=[("abA",)], tag="evA")
            p.op("act", lambda h, psB=psB, cw=cw, ct=ct: h.activation(
                abB[0:cw, :], psB[0:cw, 0:GB], AF.Silu,
                bias=binb[0:cw, ct:ct + 1]),
                reads=[pkB, ("binb",)], writes=[("abB",)], tag="evB")
            p.op("dve", lambda h, cw=cw, ct=ct: h.tensor_tensor(
                hT[0:cw, ct * GB:(ct + 1) * GB], abA[0:cw, :], abB[0:cw, :],
                ALU.mult),
                reads=[("abA",), ("abB",)], writes=[("hTb", ct)], tag="hTm")

        for par in range(2):
            ci = ck0 + par
            for half in range(2):
                ps, pk = PS[4 + half], PSK[4 + half]
                def lf(kt, par=par):
                    _, cw = ACT_TILES[kt]
                    return hT[0:cw, kt * GB + par * 128:kt * GB + par * 128 + 128]
                def rf(kt, half=half):
                    _, cw = ACT_TILES[kt]
                    return w_out_s[0:cw, kt * D + half * 512:kt * D + (half + 1) * 512]
                proj(ps[:, 0:512], pk, lf, rf, 22,
                     lambda kt: [("hTb", kt), ("w_out", kt)], tag=f"pw2{ci}")
                sl = slice(half * 512, (half + 1) * 512)
                p.op("dve", lambda h, ps=ps, sl=sl, par=par: h.tensor_tensor(
                    scrg[:, 0:D][:, sl], ps[:, 0:512],
                    xag[:, par * D:(par + 1) * D][:, sl], ALU.add),
                    reads=[pk, ("xagb", par)], writes=GKS, tag="z1in")
            cs = ln_stats(scrg[:, 0:D], D, EPS1, GKS, tag=f"p1{ci}")
            p.op("dve", lambda h, cs=cs, par=par: h.tensor_scalar(
                z1b[:, par * D:(par + 1) * D], scrg[:, 0:D],
                stv(2, cs), stv(6, cs), ALU.subtract, ALU.mult),
                reads=GKS + [ST], writes=[("z1bb", par)], tag="z1ap")
            p.dma("sp1", "sp",
                  lambda h, ci=ci, par=par: h.dma_start(
                      out=spill_z1[ci * 128:(ci + 1) * 128, :],
                      in_=z1b[:, par * D:(par + 1) * D]),
                  reads=[("z1bb", par)], writes=[("sz1", ci)], tag="spz1")

    p.fence("BD")
    for kt in range(8):
        ld("w", "pool", wgq[:, kt * FFD:(kt + 1) * FFD],
           agout_wg[kt * 128:(kt + 1) * 128, :], [("wgq", kt)],
           reads=[("ag", "wg")], tag="lwgq")
        ld("w", "pool", wuq[:, kt * FFD:(kt + 1) * FFD],
           agout_wu[kt * 128:(kt + 1) * 128, :], [("wuq", kt)],
           reads=[("ag", "wu")], tag="lwuq")
    for kt in range(22):
        r_, pos = (kt // 3, kt % 3) if kt <= 20 else (7, 2)
        srow = r_ * 384 + pos * 128
        ld("w", "pool", wdq[:, kt * D:(kt + 1) * D],
           agout_wd[srow:srow + 128, :], [("wdq", kt)],
           reads=[("ag", "wd")], tag="lwdq")
    def hq_tiles(sl_i):
        return list(range(11)) if sl_i == 0 else list(range(11, 22))

    def stage_x(ci):
        par = ci % 2
        zsl = z1g[:, par * D:(par + 1) * D]
        ld("w", "pool", zsl, spill_z1[ci * 128:(ci + 1) * 128, :],
           [("z1gb", par)], reads=[("sz1", ci)])
        quant_tile(zsl, [("z1gb", par)], D, xq[:, 0:D], ("xq",), tag="x",
                   nsl=1)

    def stage_a(ci):
        transpose_tile(xq[:, 0:D], [("xq",)],
                       lambda kt: xqT[:, kt * 128:(kt + 1) * 128],
                       lambda kt: ("xqT",), tag=f"xq{ci}")
        for nm, wbuf, wname, obuf, obkey, func, pb in [
            ("gm", wgq, "wgq", sgt, ("sgt",), AF.Silu, 0),
            ("um", wuq, "wuq", sut, ("sut",), AF.Copy, 2),
        ]:
            for nc_i in range(6):
                n0 = nc_i * 512
                nw = min(512, FFD - n0)
                ps, pk = PS[pb + nc_i % 2], PSK[pb + nc_i % 2]
                proj(ps[:, 0:nw], pk,
                     lambda kt: xqT[:, kt * 128:(kt + 1) * 128],
                     lambda kt, n0=n0, nw=nw, wbuf=wbuf: wbuf[:, kt * FFD + n0:kt * FFD + n0 + nw],
                     8, lambda kt, wname=wname: [("xqT",), (wname, kt)],
                     tag=f"{nm}{nc_i}")
                p.op("act", lambda h, ps=ps, n0=n0, nw=nw, obuf=obuf,
                     func=func: h.activation(
                         obuf[:, n0:n0 + nw], ps[:, 0:nw], func),
                     reads=[pk], writes=[obkey], tag=nm + "ev")

    def stage_b(ci):
        p.op("dve", lambda h: h.tensor_tensor(
            sgt[:, 0:FFD], sgt[:, 0:FFD], sut[:, 0:FFD], ALU.mult),
            reads=[("sgt",), ("sut",)], writes=[("sgt",)], tag="hff")
        quant_tile(sgt[:, 0:FFD], [("sgt",)], FFD, hqT[:, 0:FFD],
                   ("hqTp",), tag="h",
                   out_wkeys_fn=lambda sl_i: [("hqT", t)
                                              for t in hq_tiles(sl_i)])

    def stage_c1(ci):
        for grp in range(6):
            nts = list(range(grp * 4, min(22, grp * 4 + 4)))
            bank, bkey = PT[grp % 2], PTK[grp % 2]
            for j, ntl in enumerate(nts):
                p.op("pe", lambda h, j=j, ntl=ntl, bank=bank: h.transpose(
                    bank[:, j * 128:(j + 1) * 128],
                    hqT[:, ntl * 128:(ntl + 1) * 128], ident[:]),
                    reads=[("hqT", ntl), ("ident",)], writes=[bkey],
                    tag="trh")
            for j, ntl in enumerate(nts):
                p.op("act", lambda h, j=j, ntl=ntl, bank=bank: h.activation(
                    hqT[:, ntl * 128:(ntl + 1) * 128],
                    bank[:, j * 128:(j + 1) * 128], AF.Copy),
                    reads=[bkey], writes=[("hqT", ntl)], tag="trhev")
        for half in range(2):
            ps, pk = PS[4 + half], PSK[4 + half]
            proj(ps[:, 0:512], pk,
                 lambda kt: hqT[:, kt * 128:(kt + 1) * 128],
                 lambda kt, half=half: wdq[:, kt * D + half * 512:kt * D + (half + 1) * 512],
                 22, lambda kt: [("hqT", kt), ("wdq", kt)], tag=f"dm{half}")

    def stage_c2(ci):
        par = ci % 2
        zsl = z1g[:, par * D:(par + 1) * D]
        for half in range(2):
            ps, pk = PS[4 + half], PSK[4 + half]
            sl = slice(half * 512, (half + 1) * 512)
            p.op("dve", lambda h, ps=ps, sl=sl, zsl=zsl: h.tensor_tensor(
                outf[:, sl], ps[:, 0:512], zsl[:, sl], ALU.add),
                reads=[pk, ("z1gb", par)], writes=[("outfb",)], tag="z2in")
        cs = ln_stats(outf[:], D, EPS1, [("outfb",)], tag=f"p2{ci}",
                      sq_ap=xqT[:, 0:D], sq_keys=[("xqT",)])
        p.op("dve", lambda h, cs=cs: h.tensor_scalar(
            outf[:], outf[:], stv(2, cs), stv(6, cs),
            ALU.subtract, ALU.mult),
            reads=[("outfb",), ST], writes=[("outfb",)], tag="outap")
        p.dma("out", "sp",
              lambda h, ci=ci: h.dma_start(
                  out=y[ci * 128:(ci + 1) * 128, :], in_=outf[:]),
              reads=[("outfb",)], writes=[("y", ci)], tag="sty")

    stage_x(0)
    stage_a(0)
    stage_x(1)
    stage_b(0)
    for ci in range(NT):
        if ci + 1 < NT:
            stage_a(ci + 1)
        stage_c1(ci)
        if ci + 1 < NT:
            stage_b(ci + 1)
        stage_c2(ci)
        if ci + 2 < NT:
            stage_x(ci + 2)

    p.op("sp", lambda h: None, reads=[("y", ci) for ci in range(NT)], tag="fin")
    p.emit()
    return nc


_NC_CACHE = [None]


def _consts():
    g = np.array(GAMMA, dtype=np.float64)
    idx = np.arange(128, dtype=np.float64)
    gq = np.zeros((128, NP * 128))
    gk = np.zeros((128, NP * 128))
    gcv = np.zeros((128, NP))
    for pr in range(NP):
        for r in range(128):
            hh = 2 * pr + (1 if r >= 64 else 0)
            gq[r, pr * 128:(pr + 1) * 128] = 0.125 * g[hh] ** idx
            gk[r, pr * 128:(pr + 1) * 128] = g[hh] ** (-idx)
            gcv[r, pr] = g[hh] ** 128
    gkn = np.zeros((128, D))
    for col in range(D):
        gkn[:, col] = g[col // HD] ** (-idx)
    mskT = (idx[None, :] >= idx[:, None]).astype(np.float64)
    ident = np.eye(128)
    out = {k: np.ascontiguousarray(v, dtype=np.float32) for k, v in
           dict(mskT=mskT, ident=ident).items()}
    for name, v in dict(gq=gq, gk=gk, gkn=gkn, gc=gcv).items():
        hi = v.astype(np.float32)
        lo = (v - hi.astype(np.float64)).astype(np.float32)
        out[name + "_hi"] = np.ascontiguousarray(hi)
        out[name + "_lo"] = np.ascontiguousarray(lo)
    return out


def kernel(**inputs):
    f = np.float32
    x = np.ascontiguousarray(np.asarray(inputs["L0_attn_x"], f))
    B, S, _ = x.shape

    def cols8(v):
        return np.ascontiguousarray(np.asarray(v, f).reshape(8, 128).T)

    b_in = np.asarray(inputs["L0_attn_b_in"], f)
    b_in_a = np.zeros((22 * 128,), f)
    b_in_a[:HID] = b_in[:HID]
    b_in_b = np.zeros((22 * 128,), f)
    b_in_b[:HID] = b_in[HID:]
    common = {
        "ln1w": cols8(inputs["L0_attn_ln1_w"]),
        "ln1b": cols8(inputs["L0_attn_ln1_b"]),
        "ln2w": cols8(inputs["L0_attn_ln2_w"]),
        "ln2b": cols8(inputs["L0_attn_ln2_b"]),
        "wq": np.ascontiguousarray(np.asarray(inputs["L0_attn_wq"], f)),
        "wk": np.ascontiguousarray(np.asarray(inputs["L0_attn_wk"], f)),
        "wv": np.ascontiguousarray(np.asarray(inputs["L0_attn_wv"], f)),
        "wg": np.ascontiguousarray(np.asarray(inputs["L0_attn_wg"], f)),
        "wo": np.ascontiguousarray(np.asarray(inputs["L0_attn_wo"], f)),
        "w_in": np.ascontiguousarray(np.asarray(inputs["L0_attn_w_in"], f)),
        "w_out": np.ascontiguousarray(np.asarray(inputs["L0_attn_w_out"], f)),
        "b_in_a": np.ascontiguousarray(b_in_a.reshape(22, 128).T),
        "b_in_b": np.ascontiguousarray(b_in_b.reshape(22, 128).T),
        "b_out_bc": np.ascontiguousarray(np.broadcast_to(
            np.asarray(inputs["L0_attn_b_out"], f), (128, D))),
        **_consts(),
    }
    w_gate = np.asarray(inputs["L0_ffn_w_gate"], f)
    w_up = np.asarray(inputs["L0_ffn_w_up"], f)
    w_down = np.asarray(inputs["L0_ffn_w_down"], f)

    if _NC_CACHE[0] is None:
        _NC_CACHE[0] = build_nc()
    nc = _NC_CACHE[0]

    in_maps = []
    for c in range(8):
        b, hf = c // 2, c % 2
        m = dict(common)
        m["x_loc"] = np.ascontiguousarray(x[b, hf * SC:(hf + 1) * SC, :])
        m["x_rem"] = np.ascontiguousarray(x[b, 0:SC, :])
        m["flag"] = np.full((128, 1), float(hf), f)
        m["wg_shard"] = np.ascontiguousarray(w_gate[:, 128 * c:128 * (c + 1)])
        m["wu_shard"] = np.ascontiguousarray(w_up[:, 128 * c:128 * (c + 1)])
        k0 = 384 * c if c < 7 else 2816 - 384
        m["wd_shard"] = np.ascontiguousarray(w_down[:, k0:k0 + 384])
        in_maps.append(m)

    res = run_bass_kernel_spmd(nc, in_maps, core_ids=list(range(8)))
    out = np.empty((B, S, D), f)
    for c in range(8):
        b, hf = c // 2, c % 2
        out[b, hf * SC:(hf + 1) * SC, :] = res.results[c]["y"]
    return out


# version 19
# speedup vs baseline: 1.0523x; 1.0058x over previous
import numpy as np
import ml_dtypes
import concourse.bass as bass
import concourse.mybir as mybir
from concourse.bass_utils import run_bass_kernel_spmd

F32 = mybir.dt.float32
BF16 = mybir.dt.bfloat16
AF = mybir.ActivationFunctionType
ALU = mybir.AluOpType
AX = mybir.AxisListType.X

D = 1024
SC = 2048
NT = SC // 128
NH = 16
HD = 64
NP = 8
HID = 2730
WIN = 2 * HID
FFD = 2816
GAMMA = [float(1.0 - 2.0 ** (-5.0 - h)) for h in range(NH)]
EPS1 = 1e-5
EPSH = 1e-6
ACT_TILES = [(ct * 128, min(128, HID - ct * 128)) for ct in range(22)]
SBUF_TOP = 229344


class Prog:
    ENGINES = ("pe", "act", "dve", "sp", "pool")

    def __init__(self, nc):
        self.nc = nc
        self.ops = {e: [] for e in self.ENGINES}
        self.res = {}
        self.streams = {}
        self.fence_tok = None

    def op(self, eng, fn, reads=(), writes=(), tag=""):
        deps = set()
        for k in reads:
            r = self.res.get(k)
            if r is None:
                raise KeyError(f"read before write: {k!r} (op {tag})")
            deps.add(r["w"])
        for k in writes:
            r = self.res.get(k)
            if r is not None:
                deps.add(r["w"])
                deps.update(r["r"])
            elif self.fence_tok is not None:
                deps.add(self.fence_tok)
        idx = len(self.ops[eng])
        tok = ("e", eng, idx)
        self.ops[eng].append({"fn": fn, "deps": deps, "dma": None, "tag": tag})
        for k in reads:
            self.res[k]["r"].append(tok)
        for k in writes:
            self.res[k] = {"w": tok, "r": []}
        return tok

    def dma(self, stream, eng, fn, reads=(), writes=(), tag=""):
        s = self.streams.setdefault(stream, {"engine": eng, "n": 0})
        assert s["engine"] == eng
        deps = set()
        for k in reads:
            r = self.res.get(k)
            if r is not None:
                deps.add(r["w"])
        for k in writes:
            r = self.res.get(k)
            if r is not None:
                deps.add(r["w"])
                deps.update(r["r"])
            elif self.fence_tok is not None:
                deps.add(self.fence_tok)
        s["n"] += 1
        tok = ("d", stream, s["n"])
        self.ops[eng].append({"fn": fn, "deps": deps, "dma": (stream, s["n"]),
                              "tag": tag})
        for k in reads:
            if k in self.res:
                self.res[k]["r"].append(tok)
        for k in writes:
            self.res[k] = {"w": tok, "r": []}
        return tok

    def fence(self, tag=""):
        alldeps = set()
        for r in self.res.values():
            alldeps.add(r["w"])
            alldeps.update(r["r"])
        toks = []
        for e in self.ENGINES:
            if not self.ops[e]:
                continue
            idx = len(self.ops[e])
            toks.append(("e", e, idx))
            self.ops[e].append({"fn": lambda h: None, "deps": set(alldeps),
                                "dma": None, "tag": f"fence_{tag}"})
        master = ("e", "sp", len(self.ops["sp"]))
        self.ops["sp"].append({"fn": lambda h: None, "deps": set(toks),
                               "dma": None, "tag": f"fencem_{tag}"})
        for k in self.res:
            self.res[k] = {"w": master, "r": []}
        self.fence_tok = master

    NSLOT = 8

    def emit(self):
        nc = self.nc
        pub = {e: [False] * len(self.ops[e]) for e in self.ENGINES}
        for e in self.ENGINES:
            for o in self.ops[e]:
                for d in o["deps"]:
                    if d[0] == "e" and not (d[1] == e == "pe"):
                        pub[d[1]][d[2]] = True
        cum = {}
        for e in self.ENGINES:
            c, lst = 0, []
            for f in pub[e]:
                c += int(f)
                lst.append(c)
            cum[e] = lst

        ctxs, sems, ssems = [], {}, {}
        for e in self.ENGINES:
            if any(pub[e]):
                cm = nc.semaphore(f"sem_{e}")
                sems[e] = cm.__enter__()
                ctxs.append(cm)
        for sname in self.streams:
            lst = []
            for sl in range(self.NSLOT):
                cm = nc.semaphore(f"dma_{sname}{sl}")
                lst.append(cm.__enter__())
                ctxs.append(cm)
            ssems[sname] = lst
        blk_cm = nc.Block()
        block = blk_cm.__enter__()
        ctxs.append(blk_cm)

        eng_attr = {"pe": "tensor", "act": "scalar", "dve": "vector",
                    "sp": "sync", "pool": "gpsimd"}

        def slotcyc(n):
            return (n - 1) % self.NSLOT, (n - 1) // self.NSLOT + 1

        def make_body(e):
            def body(handle):
                waited = {}
                for i, o in enumerate(self.ops[e]):
                    deps = []
                    for d in sorted(o["deps"], key=repr):
                        if d[0] == "e":
                            if d[1] == e == "pe":
                                continue
                            deps.append((sems[d[1]], cum[d[1]][d[2]]))
                        else:
                            sl, cyc = slotcyc(d[2])
                            deps.append((ssems[d[1]][sl], 16 * cyc))
                    if o["dma"] is not None:
                        sname, n = o["dma"]
                        sl, cyc = slotcyc(n)
                        if cyc > 1:
                            deps.append((ssems[sname][sl], 16 * (cyc - 1)))
                    for sem, val in deps:
                        if val > 0 and waited.get(sem.name, -1) < val:
                            handle.wait_ge(sem, val)
                            waited[sem.name] = val
                    inst = o["fn"](handle)
                    if o["dma"] is not None:
                        sname, n = o["dma"]
                        sl, cyc = slotcyc(n)
                        inst.then_inc(ssems[sname][sl], 16)
                    elif pub[e][i]:
                        if inst is None:
                            inst = handle.nop()
                        inst.then_inc(sems[e], 1)
            return body

        for e in self.ENGINES:
            if self.ops[e]:
                getattr(block, eng_attr[e])(make_body(e))
        for cm in reversed(ctxs):
            cm.__exit__(None, None, None)


def build_nc():
    nc = bass.Bass("TRN2", target_bir_lowering=False)
    p = Prog(nc)

    def din(name, shape):
        return nc.dram_tensor(name, shape, F32, kind="ExternalInput").ap()

    x_loc = din("x_loc", [SC, D])
    x_rem = din("x_rem", [SC, D])
    ln1w_h = din("ln1w", [128, 8])
    ln1b_h = din("ln1b", [128, 8])
    ln2w_h = din("ln2w", [128, 8])
    ln2b_h = din("ln2b", [128, 8])
    w_h = {n: din(n, [D, D]) for n in ("wq", "wk", "wv", "wg", "wo")}
    w_in_h = din("w_in", [D, WIN])
    w_out_h = din("w_out", [HID, D])
    b_in_a_h = din("b_in_a", [128, 22])
    b_in_b_h = din("b_in_b", [128, 22])
    b_out_h = din("b_out_bc", [128, D])
    wgsh_h = din("wg_shard", [FFD, 128])
    wush_h = din("wu_shard", [FFD, 128])
    wdsh_h = din("wd_shard", [D, 3 * 128])
    agin_wg = nc.dram_tensor("agin_wg", [128, FFD], BF16, kind="Internal")
    agin_wu = nc.dram_tensor("agin_wu", [128, FFD], BF16, kind="Internal")
    agin_wd = nc.dram_tensor("agin_wd", [384, D], BF16, kind="Internal")
    agout_wg = nc.dram_tensor("agout_wg", [1024, FFD], BF16, kind="Internal")
    agout_wu = nc.dram_tensor("agout_wu", [1024, FFD], BF16, kind="Internal")
    agout_wd = nc.dram_tensor("agout_wd", [3072, D], BF16, kind="Internal")
    gqhl_h = [din(f"gq_{s}", [128, NP * 128]) for s in ("hi", "lo")]
    gkhl_h = [din(f"gk_{s}", [128, NP * 128]) for s in ("hi", "lo")]
    gknhl_h = [din(f"gkn_{s}", [128, D]) for s in ("hi", "lo")]
    gchl_h = [din(f"gc_{s}", [128, NP]) for s in ("hi", "lo")]
    mskT_h = din("mskT", [128, 128])
    ident_h = din("ident", [128, 128])
    flag_h = din("flag", [128, 1])

    y = nc.dram_tensor("y", [SC, D], F32, kind="ExternalOutput").ap()
    spill_x1 = nc.dram_tensor("spill_x1", [SC, D], BF16, kind="Internal").ap()
    spill_xa = nc.dram_tensor("spill_xa", [SC, D], BF16, kind="Internal").ap()
    spill_z1 = nc.dram_tensor("spill_z1", [SC, D], BF16, kind="Internal").ap()

    sb = nc.alloc_sbuf_tensor
    ident = sb("ident_sb", [128, 128], BF16)
    gc = sb("gc_sb", [128, NP], F32)
    ln1w = sb("ln1w_s", [128, 8], F32)
    ln1b = sb("ln1b_s", [128, 8], F32)
    ln2w = sb("ln2w_s", [128, 8], F32)
    ln2b = sb("ln2b_s", [128, 8], F32)
    bina = sb("bina_s", [128, 22], F32)
    binb = sb("binb_s", [128, 22], F32)
    bout = sb("bout_s", [128, D], BF16)
    flag = sb("flag_s", [128, 1], F32)
    WMAX = 3072
    scrf = sb("scrf", [128, WMAX], F32)
    scrg = sb("scrg", [128, WMAX], F32)
    scrh = sb("scrh", [128, WMAX], F32)
    scrp = sb("scrp", [128, WMAX], BF16)
    st = sb("st_all", [128, 14 * 16], F32)
    NB = 192
    qm = sb("qm_all", [128, 2 * NB], F32)

    ARENA = 164352
    base = SBUF_TOP - ARENA
    assert nc.sbuf_bytes_remaining >= ARENA + 64, nc.sbuf_bytes_remaining

    def at(name, off, shape, dtype):
        return nc.alloc_sbuf_tensor_at(name, shape, dtype, offset=base + off)

    ws = {}
    for i, n in enumerate(("wq", "wk", "wv", "wg", "wo")):
        ws[n] = at(n + "_s", i * 16384, [128, 8 * D], BF16)
    x0 = at("x0", 81920, [128, 2 * D], BF16)
    xrm = at("xrm", 86016, [128, 2 * D], BF16)
    G = 256
    xcT = at("xcT", 90112, [128, 8 * G], BF16)
    qhT = at("qhT", 94208, [128, NP * G], BF16)
    khT = at("khT", 98304, [128, NP * G], BF16)
    khn = at("khn", 102400, [128, 2 * D], BF16)
    vno = at("vno", 106496, [128, 2 * D], BF16)
    sg = at("sg", 110592, [128, 2 * D], BF16)
    xr = at("xr", 114688, [128, 2 * D], BF16)
    xrT = at("xrT", 118784, [128, 2 * D], BF16)
    zzb = at("zzb", 122880, [128, 2 * D], BF16)
    aT = at("aT", 126976, [128, 2 * 128], BF16)
    ro = at("ro", 127488, [128, D], F32)
    rcur = at("rcur", 131584, [128, NP * HD], F32)
    rnxt = at("rnxt", 133632, [128, NP * HD], F32)
    rbf = at("rbf", 135680, [128, NP * HD], BF16)
    x1b = at("x1b", 136704, [128, 2 * D], BF16)
    xab = at("xab", 140800, [128, 2 * D], BF16)

    w_in_s = at("w_in_s", 0, [128, 8 * WIN], BF16)
    w_out_s = at("w_out_s", 87360, [128, 22 * D], BF16)
    GB = 256
    xc2T = at("xc2T", 132416, [128, 8 * GB], BF16)
    hT = at("hT", 136512, [128, 22 * GB], BF16)
    abA = at("abA", 147776, [128, GB], BF16)
    abB = at("abB", 148288, [128, GB], BF16)
    x1g = at("x1g", 148800, [128, 2 * D], BF16)
    xag = at("xag", 152896, [128, 2 * D], BF16)
    z1b = at("z1b", 156992, [128, 2 * D], BF16)

    wgq = at("wgq", 0, [128, 8 * FFD], BF16)
    wuq = at("wuq", 45056, [128, 8 * FFD], BF16)
    wdq = at("wdq", 90112, [128, 22 * D], BF16)
    hqT = at("hqT", 135168, [128, 22 * 128], BF16)
    stage = at("prepstage", 144896, [128, 3 * D], F32)
    x0B = at("x0B", 144896, [128, 2 * D], BF16)
    xrB = at("xrB", 148992, [128, 2 * D], BF16)
    sqB = at("sqB", 153088, [128, D], F32)
    gq = at("gq_sb", 157184, [128, NP * 128], BF16)
    gk = at("gk_sb", 159232, [128, NP * 128], BF16)
    gkn = at("gkn_sb", 161280, [128, D], BF16)
    mskT = at("mskT_sb", 163328, [128, 128], BF16)
    identf = at("identf_sb", 163840, [128, 128], F32)
    xq = at("xq", 140800, [128, D], BF16)
    xqT = at("xqT", 142848, [128, D], BF16)
    sgt = at("sgt", 144896, [128, FFD], BF16)
    sut = at("sut", 150528, [128, FFD], BF16)
    z1g = at("z1g", 156160, [128, 2 * D], BF16)
    outf = at("outf", 160256, [128, D], F32)

    PS = [nc.alloc_psum_tensor(f"PS{i}", [128, 512], F32) for i in range(6)]
    PT = [nc.alloc_psum_tensor(f"PT{i}", [128, 512], BF16) for i in range(2)]
    PSK = [("PS", i) for i in range(6)]
    PTK = [("PT", i) for i in range(2)]
    p.op("sp", lambda h: None, writes=PSK + PTK, tag="psinit")

    def stv(j, cs=slice(0, 1), so=0):
        return st[:, 16 * (so + j):16 * (so + j + 1)][:, cs]
    ST = ("st", 0)

    def qmv(j, nb):
        return qm[:, NB * j:NB * j + nb]
    QM = ("qm",)
    REGS = [0, 88, 96, 176]
    FKS = [("scrf", r) for r in REGS]
    GKS = [("scrg", r) for r in REGS]
    HKS = [("scrh", r) for r in REGS]
    QMS = [("qm", r) for r in REGS]
    QOFF = 3072
    scrfb = scrf.bitcast(BF16)
    scrgb = scrg.bitcast(BF16)
    scrhb = scrh.bitcast(BF16)
    QREGS = [200, 288, 296, 376]
    HKS_ALL = HKS + [("scrh", r) for r in QREGS]
    PPK_ALL = [("scrp", r) for r in QREGS]

    def ld(stream, eng, dst_ap, src_ap, writes, reads=(), tag="ld"):
        def fn(h, dst=dst_ap, src=src_ap):
            return h.dma_start(out=dst, in_=src)
        p.dma(stream, eng, fn, reads=reads, writes=writes, tag=tag)

    for dst, src, key in [
        (mskT[:], mskT_h[:], ("mskT",)), (ident[:], ident_h[:], ("ident",)),
        (identf[:], ident_h[:], ("identf",)),
        (bout[:], b_out_h[:], ("bout",)),
    ]:
        ld("w", "pool", dst, src, [key])
    for i, (hl, dst, key, w_) in enumerate([
        (gqhl_h, gq, ("gq",), NP * 128), (gkhl_h, gk, ("gk",), NP * 128),
        (gknhl_h, gkn, ("gkn",), D), (gchl_h, gc, ("gc",), NP),
    ]):
        ld("w", "pool", scrf[:, 0:w_], hl[0][:], FKS)
        ld("w", "pool", scrg[:, 0:w_], hl[1][:], GKS)
        p.op("dve", lambda h, dst=dst, w_=w_: h.tensor_tensor(
            dst[:, 0:w_], scrf[:, 0:w_], scrg[:, 0:w_], ALU.add),
            reads=FKS + GKS, writes=[key], tag=f"ghl{i}")
    for dst, src, key in [
        (ln1w[:], ln1w_h[:], ("ln1w",)), (ln1b[:], ln1b_h[:], ("ln1b",)),
        (ln2w[:], ln2w_h[:], ("ln2w",)), (ln2b[:], ln2b_h[:], ("ln2b",)),
        (bina[:], b_in_a_h[:], ("bina",)), (binb[:], b_in_b_h[:], ("binb",)),
        (flag[:], flag_h[:], ("flag",)),
    ]:
        ld("c", "sp", dst, src, [key])
    for nm in ("wq", "wk", "wv", "wg", "wo"):
        for kt in range(8):
            ld("w", "pool", ws[nm][:, kt * D:(kt + 1) * D],
               w_h[nm][kt * 128:(kt + 1) * 128, :], [(nm, kt)])

    def ln_stats(src_ap, W, eps, src_keys, nst=1, tag="", sq_ap=None,
                 sq_keys=None, so=0):
        if sq_ap is None:
            sq_ap, sq_keys = scrf[:, 0:W], FKS
        ST = ("st", so)
        cs = slice(0, nst)
        if nst == 1:
            p.op("dve", lambda h, s=src_ap: h.reduce_sum(
                stv(0, cs, so), s, axis=AX), reads=src_keys, writes=[ST],
                tag="lnrs" + tag)
            p.op("act", lambda h, s=src_ap: h.activation(
                sq_ap, s, AF.Square, accum_out=stv(1, cs, so)),
                reads=list(src_keys) + [ST], writes=[ST] + list(sq_keys),
                tag="lnsq" + tag)
        else:
            v3 = src_ap.rearrange("p (g w) -> p g w", w=W)
            p.op("dve", lambda h, s=v3: h.reduce_sum(
                stv(0, cs, so), s, axis=AX), reads=src_keys, writes=[ST],
                tag="lnrs" + tag)
            p.op("act", lambda h, s=src_ap: h.activation(
                scrf[:, 0:nst * W], s, AF.Square),
                reads=src_keys, writes=FKS, tag="lnsq" + tag)
            p.op("dve", lambda h: h.reduce_sum(
                stv(1, cs, so), scrf[:, 0:nst * W].rearrange(
                    "p (g w) -> p g w", w=W), axis=AX),
                reads=FKS + [ST], writes=[ST], tag="lnss" + tag)
        p.op("dve", lambda h: h.tensor_scalar(
            stv(2, cs, so), stv(0, cs, so), 1.0 / W, None, ALU.mult),
            reads=[ST], writes=[ST], tag="lnmu" + tag)
        p.op("dve", lambda h: h.tensor_tensor(
            stv(3, cs, so), stv(2, cs, so), stv(2, cs, so), ALU.mult),
            reads=[ST], writes=[ST], tag="lnm2" + tag)
        p.op("dve", lambda h: h.scalar_tensor_tensor(
            stv(4, cs, so), stv(1, cs, so), 1.0 / W, stv(3, cs, so),
            ALU.mult, ALU.subtract),
            reads=[ST], writes=[ST], tag="lnva" + tag)
        p.op("dve", lambda h: h.tensor_scalar(
            stv(4, cs, so), stv(4, cs, so), float(eps), None, ALU.add),
            reads=[ST], writes=[ST], tag="lnve" + tag)
        p.op("act", lambda h: h.activation(
            stv(5, cs, so), stv(4, cs, so), AF.Sqrt),
            reads=[ST], writes=[ST], tag="lnsd" + tag)
        p.op("dve", lambda h: h.reciprocal(stv(6, cs, so), stv(5, cs, so)),
            reads=[ST], writes=[ST], tag="lnrd" + tag)
        return cs

    def transpose_tile(src_ap, src_keys, dstT_fn, dst_wkeys, wb=None, tag=""):
        for half in range(2):
            bank, bkey = PT[half], PTK[half]
            for j in range(4):
                kt = half * 4 + j
                p.op("pe", lambda h, kt=kt, j=j, bank=bank, s=src_ap: h.transpose(
                    bank[:, j * 128:(j + 1) * 128],
                    s[:, kt * 128:(kt + 1) * 128], ident[:]),
                    reads=list(src_keys) + [("ident",)],
                    writes=[bkey], tag="tr" + tag)
            for j in range(4):
                kt = half * 4 + j
                kw, rd, func = {}, [bkey], AF.Copy
                if wb is not None:
                    kw["scale"] = wb[0][:, kt:kt + 1]
                    kw["bias"] = wb[2][:, kt:kt + 1]
                    rd += [wb[1], wb[3]]
                    func = AF.Identity
                p.op("act", lambda h, kt=kt, j=j, bank=bank, kw=kw,
                     func=func: h.activation(
                    dstT_fn(kt), bank[:, j * 128:(j + 1) * 128], func, **kw),
                    reads=rd, writes=[dst_wkeys(kt)], tag="trev" + tag)

    def ln_chunk(src_ap, src_keys, eps, wb, xc_ap, xc_key, dstT_fn, dstT_wkeys,
                 tag="", so=0, sq_ap=None, sq_keys=None):
        cs = ln_stats(src_ap, D, eps, src_keys, tag=tag, so=so,
                      sq_ap=sq_ap, sq_keys=sq_keys)
        p.op("dve", lambda h, s=src_ap, cs=cs, so=so: h.tensor_scalar(
            xc_ap, s, stv(2, cs, so), stv(6, cs, so),
            ALU.subtract, ALU.mult),
            reads=list(src_keys) + [("st", so)], writes=[xc_key],
            tag="lnap" + tag)
        transpose_tile(xc_ap, [xc_key], dstT_fn, dstT_wkeys, wb=wb, tag=tag)

    def proj(ps_ap, bank_key, lhsT_fn, rhs_fn, nk, rd_fn, tag=""):
        for kt in range(nk):
            p.op("pe", lambda h, kt=kt: h.matmul(
                ps_ap, lhsT_fn(kt), rhs_fn(kt),
                start=(kt == 0), stop=(kt == nk - 1)),
                reads=rd_fn(kt) + ([bank_key] if kt else []),
                writes=[bank_key], tag=tag)

    def state_update(khn_ap, v_ap, kv_keys, tag=""):
        PD, PDK = PS[5], PSK[5]
        p.op("dve", lambda h: h.tensor_tensor(
            scrg[:, 0:NP * HD].rearrange("p (g w) -> p g w", w=HD),
            rcur[:].rearrange("p (g w) -> p g w", w=HD),
            gc[:, :, None].broadcast_to((128, NP, HD)), ALU.mult),
            reads=[("rcur",), ("gc",)], writes=GKS, tag="rsc" + tag)
        for pr in range(NP):
            c0 = pr * HD
            p.op("pe", lambda h, pr=pr: h.matmul(
                PD[0:64, 0:64], khn_ap[:, pr * 128:pr * 128 + 64],
                v_ap[:, pr * 128:pr * 128 + 64], start=True, stop=True,
                tile_position=(0, 0)),
                reads=kv_keys, writes=[PDK], tag="sdA" + tag)
            p.op("pe", lambda h, pr=pr: h.matmul(
                PD[64:128, 0:64], khn_ap[:, pr * 128 + 64:pr * 128 + 128],
                v_ap[:, pr * 128 + 64:pr * 128 + 128], start=True, stop=True,
                tile_position=(0, 64)),
                reads=kv_keys + [PDK], writes=[PDK], tag="sdB" + tag)
            p.op("dve", lambda h, pr=pr, c0=c0: h.scalar_tensor_tensor(
                rnxt[:, c0:c0 + HD], PD[:, 0:64], gc[:, pr:pr + 1],
                scrg[:, c0:c0 + HD], ALU.mult, ALU.add),
                reads=[PDK, ("gc",)] + GKS, writes=[("rnxt", pr)],
                tag="rup" + tag)
        p.op("dve", lambda h: h.tensor_copy(rcur[:], rnxt[:]),
            reads=[("rnxt", pr) for pr in range(NP)], writes=[("rcur",)],
            tag="rsw" + tag)

    def quant_tile(x_ap, x_keys, W, out_ap, out_wkey, rows=128, tag="",
                   soff=0, nsl=2, sseq=False, out_wkeys_fn=None):
        ws = W // nsl
        assert ws % 16 == 0
        SL = []
        for j in range(nsl):
            cx = j * ws
            c0 = soff if sseq else soff + cx
            nb, b0 = ws // 16, c0 // 16
            ki = 200 + c0 // 16
            x3 = x_ap[:, cx:cx + ws].rearrange("p (b k) -> p b k", k=16)
            m0 = qmv(0, NB)[0:rows, b0:b0 + nb]
            m1 = qmv(1, NB)[0:rows, b0:b0 + nb]
            SL.append((c0 + QOFF, ws, x3, m0, m1, nb, cx, ki))
        ORD = {s[7]: j for j, s in enumerate(SL)}

        STAGES = []

        def sl_op(eng, fn, rd, wr, tag_):
            STAGES.append((eng, fn, rd, wr, tag_))

        XK = list(x_keys)
        sl_op("dve", lambda c0, w, x3, m0, m1, nb, cx, ki:
              (lambda h: h.reduce_max(m0, x3, axis=AX,
                                      apply_absolute_value=True)),
              lambda i: XK, lambda i: [("qm", i)], "qbm" + tag)
        sl_op("act", lambda c0, w, x3, m0, m1, nb, cx, ki:
              (lambda h: h.activation(scrgb[0:rows, c0:c0 + w],
                                      x_ap[:, cx:cx + w], AF.Abs)),
              lambda i: XK, lambda i: [("scrg", i)], "qab" + tag)
        sl_op("dve", lambda c0, w, x3, m0, m1, nb, cx, ki:
              (lambda h: h.tensor_scalar(m1, m0, 1.0 / 6.0, 1e-12,
                                         ALU.mult, ALU.max)),
              lambda i: [("qm", i)], lambda i: [("qm", i)], "qs6")
        sl_op("dve", lambda c0, w, x3, m0, m1, nb, cx, ki:
              (lambda h: h.reciprocal(m0, m1)),
              lambda i: [("qm", i)], lambda i: [("qm", i)], "qre")
        sl_op("pool", lambda c0, w, x3, m0, m1, nb, cx, ki:
              (lambda h: h.tensor_tensor(
                  scrfb[0:rows, c0:c0 + w].rearrange("p (b k) -> p b k", k=16),
                  scrgb[0:rows, c0:c0 + w].rearrange("p (b k) -> p b k", k=16),
                  m0[:, :, None].broadcast_to((rows, nb, 16)), ALU.mult)),
              lambda i: [("scrg", i), ("qm", i)],
              lambda i: [("scrf", i)], "qt_" + tag)
        sl_op("dve", lambda c0, w, x3, m0, m1, nb, cx, ki:
              (lambda h: h.tensor_scalar(scrhb[0:rows, c0:c0 + w],
                                         scrfb[0:rows, c0:c0 + w], 0.25, 1.0,
                                         ALU.is_gt, ALU.mult)),
              lambda i: [("scrf", i)], lambda i: [("scrh", i)], "qt0")
        for t in (0.75, 1.25, 1.75):
            sl_op("dve", lambda c0, w, x3, m0, m1, nb, cx, ki, t=t:
                  (lambda h: h.scalar_tensor_tensor(
                      scrhb[0:rows, c0:c0 + w], scrfb[0:rows, c0:c0 + w], t,
                      scrhb[0:rows, c0:c0 + w], ALU.is_gt, ALU.add)),
                  lambda i: [("scrf", i), ("scrh", i)],
                  lambda i: [("scrh", i)], "qtl")
        sl_op("dve", lambda c0, w, x3, m0, m1, nb, cx, ki:
              (lambda h: h.tensor_scalar(scrp[0:rows, c0 - QOFF:c0 - QOFF + w],
                                         scrfb[0:rows, c0:c0 + w], 2.5, 1.0,
                                         ALU.is_gt, ALU.mult)),
              lambda i: [("scrf", i)], lambda i: [("scrp", i)], "qb0")
        for t in (3.5, 5.0, 5.0):
            sl_op("dve", lambda c0, w, x3, m0, m1, nb, cx, ki, t=t:
                  (lambda h: h.scalar_tensor_tensor(
                      scrp[0:rows, c0 - QOFF:c0 - QOFF + w],
                      scrfb[0:rows, c0:c0 + w], t,
                      scrp[0:rows, c0 - QOFF:c0 - QOFF + w],
                      ALU.is_gt, ALU.add)),
                  lambda i: [("scrf", i), ("scrp", i)],
                  lambda i: [("scrp", i)], "qbl")
        sl_op("act", lambda c0, w, x3, m0, m1, nb, cx, ki:
              (lambda h: h.activation(scrgb[0:rows, c0:c0 + w],
                                      x_ap[:, cx:cx + w], AF.Sign)),
              lambda i: XK + [("scrf", i)], lambda i: [("scrg", i)],
              "qsg" + tag)
        sl_op("dve", lambda c0, w, x3, m0, m1, nb, cx, ki:
              (lambda h: h.scalar_tensor_tensor(
                  scrhb[0:rows, c0:c0 + w], scrhb[0:rows, c0:c0 + w], 0.5,
                  scrp[0:rows, c0 - QOFF:c0 - QOFF + w], ALU.mult, ALU.add)),
              lambda i: [("scrh", i), ("scrp", i)],
              lambda i: [("scrh", i)], "qcm")
        sl_op("pool", lambda c0, w, x3, m0, m1, nb, cx, ki:
              (lambda h: h.tensor_tensor(
                  scrhb[0:rows, c0:c0 + w].rearrange("p (b k) -> p b k", k=16),
                  scrhb[0:rows, c0:c0 + w].rearrange("p (b k) -> p b k", k=16),
                  m1[:, :, None].broadcast_to((rows, nb, 16)), ALU.mult)),
              lambda i: [("scrh", i), ("qm", i)],
              lambda i: [("scrh", i)], "qds")
        sl_op("pool", lambda c0, w, x3, m0, m1, nb, cx, ki:
              (lambda h: h.tensor_tensor(
                  out_ap[:, cx:cx + w], scrhb[0:rows, c0:c0 + w],
                  scrgb[0:rows, c0:c0 + w], ALU.mult)),
              lambda i: [("scrh", i), ("scrg", i)],
              (lambda i: [out_wkey] if out_wkeys_fn is None
               else [out_wkey] + out_wkeys_fn(ORD[i])), "qo" + tag)

        if sseq:
            for s in SL:
                for eng, fn, rd, wr, tag_ in STAGES:
                    p.op(eng, fn(*s), reads=rd(s[7]), writes=wr(s[7]),
                         tag=tag_)
        else:
            for eng, fn, rd, wr, tag_ in STAGES:
                for s in SL:
                    p.op(eng, fn(*s), reads=rd(s[7]), writes=wr(s[7]),
                         tag=tag_)

    def prep_shard(src_h, nt, kc, agin_t, agkey, tag):
        W = nt * kc
        ld("w", "pool",
           stage[:, 0:W].rearrange("p (t k) -> p t k", k=kc),
           src_h.rearrange("(t p) k -> p t k", p=128),
           [("prepstage",)], tag="psh" + tag)
        quant_tile(stage[:, 0:W], [("prepstage",)], W,
                   stage[:, 0:W], ("prepstage",), tag="pq" + tag)
        ntile = W // 128
        for grp in range((ntile + 3) // 4):
            tls = list(range(grp * 4, min(ntile, grp * 4 + 4)))
            bank, bkey = PS[2 + grp % 2], PSK[2 + grp % 2]
            for j, tl in enumerate(tls):
                p.op("pe", lambda h, j=j, tl=tl, bank=bank: h.transpose(
                    bank[:, j * 128:(j + 1) * 128],
                    stage[:, tl * 128:(tl + 1) * 128], identf[:]),
                    reads=[("prepstage",), ("identf",)],
                    writes=[bkey], tag="ptr" + tag)
            for j, tl in enumerate(tls):
                nkc = kc // 128
                t_, jj = tl // nkc, tl % nkc
                dcol = jj * (nt * 128) + t_ * 128
                p.op("act", lambda h, j=j, dcol=dcol, bank=bank: h.activation(
                    scrp[:, dcol:dcol + 128], bank[:, j * 128:(j + 1) * 128],
                    AF.Copy),
                    reads=[bkey], writes=PPK_ALL, tag="pev" + tag)
        p.dma("sp1", "sp",
              lambda h, agin_t=agin_t, nt=nt, kc=kc: h.dma_start(
                  out=agin_t.ap().rearrange("(jj p) n -> p jj n", p=128),
                  in_=scrp[:, 0:W].rearrange("p (jj n) -> p jj n",
                                             n=nt * 128)),
              reads=PPK_ALL, writes=[agkey], tag="pst" + tag)

    prep_shard(wgsh_h, 22, 128, agin_wg, ("agin", "wg"), "g")
    prep_shard(wush_h, 22, 128, agin_wu, ("agin", "wu"), "u")
    prep_shard(wdsh_h, 8, 384, agin_wd, ("agin", "wd"), "d")

    p.op("dve", lambda h: h.memset(rcur[:], 0.0), writes=[("rcur",)], tag="ri")

    for rc in range(NT):
        par = rc % 2
        xs = xrm[:, par * D:(par + 1) * D]
        XB = ("xrmb", par)
        ld("w", "pool", xs, x_rem[rc * 128:(rc + 1) * 128, :], [XB])
        ln_chunk(xs, [XB], EPS1, (ln1w, ("ln1w",), ln1b, ("ln1b",)),
                 xr[:, par * D:(par + 1) * D], ("xrb", 0, par),
                 lambda kt, par=par: xcT[:, kt * G + par * 128:kt * G + par * 128 + 128],
                 lambda kt, par=par: ("xcTb", par, kt), tag=f"a{rc}")
        xcT_keys = [("xcTb", par, kt) for kt in range(8)]
        for half in range(2):
            ps, pk = PS[half], PSK[half]
            proj(ps[:, 0:512], pk,
                 lambda kt, par=par: xcT[:, kt * G + par * 128:kt * G + par * 128 + 128],
                 lambda kt, half=half: ws["wk"][:, kt * D + half * 512:kt * D + (half + 1) * 512],
                 8, lambda kt: xcT_keys + [("wk", kt)], tag=f"a0k{rc}")
            p.op("dve", lambda h, ps=ps, half=half, par=par: h.tensor_tensor(
                khn[:, par * D + half * 512:par * D + (half + 1) * 512],
                ps[:, 0:512], gkn[:, half * 512:(half + 1) * 512], ALU.mult),
                reads=[pk, ("gkn",)], writes=[("khnb", par, half)], tag="a0ek")
            ps2, pk2 = PS[2 + half], PSK[2 + half]
            proj(ps2[:, 0:512], pk2,
                 lambda kt, par=par: xcT[:, kt * G + par * 128:kt * G + par * 128 + 128],
                 lambda kt, half=half: ws["wv"][:, kt * D + half * 512:kt * D + (half + 1) * 512],
                 8, lambda kt: xcT_keys + [("wv", kt)], tag=f"a0v{rc}")
            p.op("act", lambda h, ps2=ps2, half=half, par=par: h.activation(
                vno[:, par * D + half * 512:par * D + (half + 1) * 512],
                ps2[:, 0:512], AF.Copy),
                reads=[pk2], writes=[("vnob", par, half)], tag="a0ev")
        state_update(khn[:, par * D:(par + 1) * D], vno[:, par * D:(par + 1) * D],
                     [("khnb", par, 0), ("khnb", par, 1),
                      ("vnob", par, 0), ("vnob", par, 1)], tag=f"a0{rc}")

    p.op("dve", lambda h: h.tensor_scalar(
        rcur[:], rcur[:], flag[:, 0:1], None, ALU.mult),
        reads=[("rcur",), ("flag",)], writes=[("rcur",)], tag="rflag")
    p.op("dve", lambda h: h.tensor_copy(rbf[:], rcur[:]),
        reads=[("rcur",)], writes=[("rbf",)], tag="rbf0")

    RG8 = [[0, 1, 2, 3, 4, 5, 6, 7]]

    def emit_ags():
        for tg, ain, aout in [("wg", agin_wg, agout_wg),
                              ("wu", agin_wu, agout_wu),
                              ("wd", agin_wd, agout_wd)]:
            p.op("pool", lambda h, ain=ain, aout=aout: h.collective_compute(
                "AllGather", ALU.bypass, replica_groups=RG8,
                ins=[ain.ap().opt()], outs=[aout.ap().opt()]),
                reads=[("agin", tg)], writes=[("ag", tg)], tag="ag" + tg)

    def emit_ln1(g, part):
        gp = g % 2
        xb = x0 if gp == 0 else x0B
        xrb = xr if gp == 0 else xrB
        so = 0 if gp == 0 else 7
        sqa = None if gp == 0 else sqB[:, 0:D]
        sqk = None if gp == 0 else [("sqB",)]
        if g == 1 and part == 1:
            p.op("sp", lambda h: None, reads=[("prepstage",)],
                 writes=[("x0b", 1, 0), ("x0b", 1, 1), ("xrb", 1, 0),
                         ("xrb", 1, 1), ("sqB",)], tag="claimB")
        for par in range(2):
            ci = 2 * g + par
            XK = ("x0b", gp, par)
            if part == 1:
                ld("w", "pool", xb[:, par * D:(par + 1) * D],
                   x_loc[ci * 128:(ci + 1) * 128, :], [XK])
                cs = ln_stats(xb[:, par * D:(par + 1) * D], D, EPS1, [XK],
                              tag=f"A{ci}", so=so, sq_ap=sqa, sq_keys=sqk)
                p.op("dve", lambda h, par=par, cs=cs, so=so, xb=xb,
                     xrb=xrb: h.tensor_scalar(
                         xrb[:, par * D:(par + 1) * D],
                         xb[:, par * D:(par + 1) * D],
                         stv(2, cs, so), stv(6, cs, so),
                         ALU.subtract, ALU.mult),
                     reads=[XK, ("st", so)], writes=[("xrb", gp, par)],
                     tag=f"lnapA{ci}")
            else:
                transpose_tile(
                    xrb[:, par * D:(par + 1) * D], [("xrb", gp, par)],
                    lambda kt, par=par: xcT[:, kt * G + par * 128:kt * G + par * 128 + 128],
                    lambda kt, par=par: ("xcTb", par, kt),
                    wb=(ln1w, ("ln1w",), ln1b, ("ln1b",)), tag=f"A{ci}")

    emit_ln1(0, 1)
    emit_ln1(0, 2)
    for g in range(NT // 2):
        gp = g % 2
        xb = x0 if gp == 0 else x0B
        xrb = xr if gp == 0 else xrB
        if g == 3:
            emit_ags()
        ck0 = 2 * g
        xcT_keys = [("xcTb", pp, kt) for pp in range(2) for kt in range(8)]

        for pr in range(NP):
            ps, pk = PS[pr % 2], PSK[pr % 2]
            proj(ps[:, 0:G], pk,
                 lambda kt, pr=pr: ws["wq"][:, kt * D + pr * 128:kt * D + (pr + 1) * 128],
                 lambda kt: xcT[:, kt * G:(kt + 1) * G],
                 8, lambda kt: xcT_keys + [("wq", kt)], tag=f"pq{pr}")
            for cc in range(2):
                p.op("dve", lambda h, ps=ps, pr=pr, cc=cc: h.tensor_tensor(
                    qhT[:, pr * G + cc * 128:pr * G + (cc + 1) * 128],
                    ps[:, cc * 128:(cc + 1) * 128],
                    gq[:, pr * 128:(pr + 1) * 128], ALU.mult),
                    reads=[pk, ("gq",)], writes=[("qhTb", pr, cc)], tag="evq")
            ps2, pk2 = PS[2 + pr % 2], PSK[2 + pr % 2]
            proj(ps2[:, 0:G], pk2,
                 lambda kt, pr=pr: ws["wk"][:, kt * D + pr * 128:kt * D + (pr + 1) * 128],
                 lambda kt: xcT[:, kt * G:(kt + 1) * G],
                 8, lambda kt: xcT_keys + [("wk", kt)], tag=f"pkT{pr}")
            for cc in range(2):
                p.op("dve", lambda h, ps2=ps2, pr=pr, cc=cc: h.tensor_tensor(
                    khT[:, pr * G + cc * 128:pr * G + (cc + 1) * 128],
                    ps2[:, cc * 128:(cc + 1) * 128],
                    gk[:, pr * 128:(pr + 1) * 128], ALU.mult),
                    reads=[pk2, ("gk",)], writes=[("khTb", pr, cc)], tag="evk")

        for par in range(2):
            ci = ck0 + par
            for pr in range(NP):
                bank, bkey = PT[pr % 2], PTK[pr % 2]
                p.op("pe", lambda h, pr=pr, par=par, bank=bank: h.transpose(
                    bank[:, 0:128],
                    khT[:, pr * G + par * 128:pr * G + par * 128 + 128],
                    ident[:]),
                    reads=[("khTb", pr, par), ("ident",)], writes=[bkey],
                    tag="ktr")
                p.op("act", lambda h, pr=pr, par=par, bank=bank: h.activation(
                    khn[:, par * D + pr * 128:par * D + (pr + 1) * 128],
                    bank[:, 0:128], AF.Copy),
                    reads=[bkey], writes=[("khnb", par, pr // 4)], tag="kte")
            for half in range(2):
                ps2, pk2 = PS[5], PSK[5]
                proj(ps2[:, 0:512], pk2,
                     lambda kt, par=par: xcT[:, kt * G + par * 128:kt * G + par * 128 + 128],
                     lambda kt, half=half: ws["wv"][:, kt * D + half * 512:kt * D + (half + 1) * 512],
                     8, lambda kt: xcT_keys + [("wv", kt)], tag=f"nv{ci}")
                p.op("act", lambda h, ps2=ps2, half=half, par=par: h.activation(
                    vno[:, par * D + half * 512:par * D + (half + 1) * 512],
                    ps2[:, 0:512], AF.Copy),
                    reads=[pk2], writes=[("vnob", par, half)], tag="nev")
                ps3, pk3 = PS[half], PSK[half]
                proj(ps3[:, 0:512], pk3,
                     lambda kt, par=par: xcT[:, kt * G + par * 128:kt * G + par * 128 + 128],
                     lambda kt, half=half: ws["wg"][:, kt * D + half * 512:kt * D + (half + 1) * 512],
                     8, lambda kt: xcT_keys + [("wg", kt)], tag=f"ng{ci}")
                p.op("act", lambda h, ps3=ps3, half=half, par=par: h.activation(
                    sg[:, par * D + half * 512:par * D + half * 512 + 512],
                    ps3[:, 0:512], AF.Silu),
                    reads=[pk3], writes=[("sgb", par, half)], tag="neg")

        if g + 1 < NT // 2:
            emit_ln1(g + 1, 1)
            emit_ln1(g + 1, 2)

        for par in range(2):
            ci = ck0 + par
            v_c = vno[:, par * D:(par + 1) * D]
            kc_keys = [("khnb", par, 0), ("khnb", par, 1)]
            vc_keys = [("vnob", par, 0), ("vnob", par, 1)]
            PO, POK = PS[4], PSK[4]
            for pr in range(NP):
                qsl = qhT[:, pr * G + par * 128:pr * G + par * 128 + 128]
                qkey = ("qhTb", pr, par)
                ksl = khT[:, pr * G + par * 128:pr * G + par * 128 + 128]
                kkey = ("khTb", pr, par)
                for hh in range(2):
                    PA, PAK = PS[hh], PSK[hh]
                    p.op("pe", lambda h, PA=PA, hh=hh, qsl=qsl, ksl=ksl: h.matmul(
                        PA[:, 0:128], ksl[64 * hh:64 * (hh + 1), :],
                        qsl[64 * hh:64 * (hh + 1), :], start=True, stop=True,
                        tile_position=(64 * hh, 0)),
                        reads=[qkey, kkey], writes=[PAK], tag="mmA")
                    p.op("dve", lambda h, PA=PA, hh=hh: h.tensor_tensor(
                        aT[:, hh * 128:(hh + 1) * 128], PA[:, 0:128],
                        mskT[:], ALU.mult),
                        reads=[PAK, ("mskT",)], writes=[("aTb", hh)], tag="mask")
                for hh in range(2):
                    hcol = (2 * pr + hh) * HD
                    p.op("pe", lambda h, PO=PO, hh=hh, hcol=hcol, v_c=v_c: h.matmul(
                        PO[:, hh * 64:(hh + 1) * 64],
                        aT[:, hh * 128:(hh + 1) * 128],
                        v_c[:, hcol:hcol + HD], start=(hh == 0), stop=False),
                        reads=[("aTb", hh)] + vc_keys + [POK], writes=[POK],
                        tag="mmI")
                    p.op("pe", lambda h, PO=PO, hh=hh, pr=pr, qsl=qsl: h.matmul(
                        PO[:, hh * 64:(hh + 1) * 64],
                        qsl[64 * hh:64 * (hh + 1), :],
                        rbf[64 * hh:64 * (hh + 1), pr * HD:(pr + 1) * HD],
                        start=False, stop=(hh == 1), tile_position=(64 * hh, 0)),
                        reads=[qkey, ("rbf",), POK], writes=[POK], tag="mmX")
                p.op("act", lambda h, PO=PO, pr=pr: h.activation(
                    ro[:, pr * 128:(pr + 1) * 128], PO[:, 0:128], AF.Copy),
                    reads=[POK], writes=[("rob", pr)], tag="evo")
            state_update(khn[:, par * D:(par + 1) * D], v_c,
                         kc_keys + vc_keys, tag=f"A{ci}")
            p.op("dve", lambda h: h.tensor_copy(rbf[:], rcur[:]),
                reads=[("rcur",)], writes=[("rbf",)], tag="rbfA")

            ro_keys = [("rob", pr) for pr in range(NP)]
            cs = ln_stats(ro[:], HD, EPSH, ro_keys, nst=16, tag=f"hl{ci}")
            p.op("dve", lambda h, cs=cs: h.tensor_tensor(
                scrg[:, 0:D].rearrange("p (g w) -> p g w", w=HD),
                ro[:].rearrange("p (g w) -> p g w", w=HD),
                stv(2, cs)[:, :, None].broadcast_to((128, 16, HD)),
                ALU.subtract),
                reads=ro_keys + [ST], writes=GKS, tag="gnt1")
            p.op("dve", lambda h, par=par, cs=cs: h.tensor_tensor(
                zzb[:, par * D:(par + 1) * D].rearrange("p (g w) -> p g w", w=HD),
                sg[:, par * D:(par + 1) * D].rearrange("p (g w) -> p g w", w=HD),
                stv(6, cs)[:, :, None].broadcast_to((128, 16, HD)), ALU.mult),
                reads=[("sgb", par, 0), ("sgb", par, 1), ST],
                writes=[("zzbb", par)], tag="gnz")
            p.op("dve", lambda h, par=par, xrb=xrb: h.tensor_tensor(
                xrb[:, par * D:(par + 1) * D], scrg[:, 0:D],
                zzb[:, par * D:(par + 1) * D], ALU.mult),
                reads=GKS + [("zzbb", par)], writes=[("xrb", gp, par)],
                tag="gnxr")
            transpose_tile(xrb[:, par * D:(par + 1) * D], [("xrb", gp, par)],
                           lambda kt, par=par: xrT[:, par * D + kt * 128:par * D + (kt + 1) * 128],
                           lambda kt, par=par: ("xrTb", par, kt), tag=f"xr{ci}")
            p.op("dve", lambda h, par=par, xb=xb: h.scalar_tensor_tensor(
                scrh[:, 0:D], xb[:, par * D:(par + 1) * D], 2.0, bout[:],
                ALU.mult, ALU.add),
                reads=[("x0b", gp, par), ("bout",)], writes=HKS, tag="x0b2")
            for half in range(2):
                ps, pk = PS[half], PSK[half]
                proj(ps[:, 0:512], pk,
                     lambda kt, par=par: xrT[:, par * D + kt * 128:par * D + (kt + 1) * 128],
                     lambda kt, half=half: ws["wo"][:, kt * D + half * 512:kt * D + (half + 1) * 512],
                     8, lambda kt, par=par: [("xrTb", par, kt), ("wo", kt)],
                     tag=f"po{ci}")
                sl = slice(half * 512, (half + 1) * 512)
                p.op("dve", lambda h, ps=ps, sl=sl, par=par, xb=xb: h.tensor_tensor(
                    x1b[:, par * D:(par + 1) * D][:, sl], ps[:, 0:512],
                    xb[:, par * D:(par + 1) * D][:, sl], ALU.add),
                    reads=[pk, ("x0b", gp, par)], writes=[("x1bb", par, half)],
                    tag="evx1")
                p.op("dve", lambda h, ps=ps, sl=sl, par=par: h.tensor_tensor(
                    xab[:, par * D:(par + 1) * D][:, sl], ps[:, 0:512],
                    scrh[:, 0:D][:, sl], ALU.add),
                    reads=[pk] + HKS, writes=[("xabb", par, half)],
                    tag="evxa")
            p.dma("sp1", "sp",
                  lambda h, ci=ci, par=par: h.dma_start(
                      out=spill_x1[ci * 128:(ci + 1) * 128, :],
                      in_=x1b[:, par * D:(par + 1) * D]),
                  reads=[("x1bb", par, 0), ("x1bb", par, 1)],
                  writes=[("sx1", ci)], tag="spx1")
            p.dma("sp1", "sp",
                  lambda h, ci=ci, par=par: h.dma_start(
                      out=spill_xa[ci * 128:(ci + 1) * 128, :],
                      in_=xab[:, par * D:(par + 1) * D]),
                  reads=[("xabb", par, 0), ("xabb", par, 1)],
                  writes=[("sxa", ci)], tag="spxa")

    p.fence("AB")
    for kt in range(8):
        ld("w", "pool", w_in_s[:, kt * WIN:(kt + 1) * WIN],
           w_in_h[kt * 128:(kt + 1) * 128, :], [("w_in", kt)])
    for kt in range(22):
        c0, cw = ACT_TILES[kt]
        ld("w", "pool", w_out_s[0:cw, kt * D:(kt + 1) * D],
           w_out_h[c0:c0 + cw, :], [("w_out", kt)])

    for g in range(NT // 2):
        ck0 = 2 * g
        for par in range(2):
            ci = ck0 + par
            ld("w", "pool", x1g[:, par * D:(par + 1) * D],
               spill_x1[ci * 128:(ci + 1) * 128, :], [("x1gb", par)],
               reads=[("sx1", ci)])
            ld("w", "pool", xag[:, par * D:(par + 1) * D],
               spill_xa[ci * 128:(ci + 1) * 128, :], [("xagb", par)],
               reads=[("sxa", ci)])
            ln_chunk(x1g[:, par * D:(par + 1) * D], [("x1gb", par)], EPS1,
                     (ln2w, ("ln2w",), ln2b, ("ln2b",)),
                     z1b[:, par * D:(par + 1) * D], ("z1bb", par),
                     lambda kt, par=par: xc2T[:, kt * GB + par * 128:kt * GB + par * 128 + 128],
                     lambda kt, par=par: ("x2Tb", par, kt), tag=f"B{ci}")
        x2T_keys = [("x2Tb", pp, kt) for pp in range(2) for kt in range(8)]

        for ct in range(22):
            c0a, cw = ACT_TILES[ct]
            psA, pkA = PS[ct % 2], PSK[ct % 2]
            psB, pkB = PS[2 + ct % 2], PSK[2 + ct % 2]
            proj(psA[0:cw, 0:GB], pkA,
                 lambda kt, c0a=c0a, cw=cw: w_in_s[:, kt * WIN + c0a:kt * WIN + c0a + cw],
                 lambda kt: xc2T[:, kt * GB:(kt + 1) * GB],
                 8, lambda kt: x2T_keys + [("w_in", kt)], tag=f"pA{ct}")
            proj(psB[0:cw, 0:GB], pkB,
                 lambda kt, c0a=c0a, cw=cw: w_in_s[:, kt * WIN + HID + c0a:kt * WIN + HID + c0a + cw],
                 lambda kt: xc2T[:, kt * GB:(kt + 1) * GB],
                 8, lambda kt: x2T_keys + [("w_in", kt)], tag=f"pB{ct}")
            p.op("act", lambda h, psA=psA, cw=cw, ct=ct: h.activation(
                abA[0:cw, :], psA[0:cw, 0:GB], AF.Identity,
                bias=bina[0:cw, ct:ct + 1]),
                reads=[pkA, ("bina",)], writes=[("abA",)], tag="evA")
            p.op("act", lambda h, psB=psB, cw=cw, ct=ct: h.activation(
                abB[0:cw, :], psB[0:cw, 0:GB], AF.Silu,
                bias=binb[0:cw, ct:ct + 1]),
                reads=[pkB, ("binb",)], writes=[("abB",)], tag="evB")
            p.op("dve", lambda h, cw=cw, ct=ct: h.tensor_tensor(
                hT[0:cw, ct * GB:(ct + 1) * GB], abA[0:cw, :], abB[0:cw, :],
                ALU.mult),
                reads=[("abA",), ("abB",)], writes=[("hTb", ct)], tag="hTm")

        for par in range(2):
            ci = ck0 + par
            for half in range(2):
                ps, pk = PS[4 + half], PSK[4 + half]
                def lf(kt, par=par):
                    _, cw = ACT_TILES[kt]
                    return hT[0:cw, kt * GB + par * 128:kt * GB + par * 128 + 128]
                def rf(kt, half=half):
                    _, cw = ACT_TILES[kt]
                    return w_out_s[0:cw, kt * D + half * 512:kt * D + (half + 1) * 512]
                proj(ps[:, 0:512], pk, lf, rf, 22,
                     lambda kt: [("hTb", kt), ("w_out", kt)], tag=f"pw2{ci}")
                sl = slice(half * 512, (half + 1) * 512)
                p.op("dve", lambda h, ps=ps, sl=sl, par=par: h.tensor_tensor(
                    scrg[:, 0:D][:, sl], ps[:, 0:512],
                    xag[:, par * D:(par + 1) * D][:, sl], ALU.add),
                    reads=[pk, ("xagb", par)], writes=GKS, tag="z1in")
            cs = ln_stats(scrg[:, 0:D], D, EPS1, GKS, tag=f"p1{ci}")
            p.op("dve", lambda h, cs=cs, par=par: h.tensor_scalar(
                z1b[:, par * D:(par + 1) * D], scrg[:, 0:D],
                stv(2, cs), stv(6, cs), ALU.subtract, ALU.mult),
                reads=GKS + [ST], writes=[("z1bb", par)], tag="z1ap")
            p.dma("sp1", "sp",
                  lambda h, ci=ci, par=par: h.dma_start(
                      out=spill_z1[ci * 128:(ci + 1) * 128, :],
                      in_=z1b[:, par * D:(par + 1) * D]),
                  reads=[("z1bb", par)], writes=[("sz1", ci)], tag="spz1")

    p.fence("BD")
    for ci0 in range(2):
        ld("w", "pool", z1g[:, (ci0 % 2) * D:((ci0 % 2) + 1) * D],
           spill_z1[ci0 * 128:(ci0 + 1) * 128, :], [("z1gb", ci0 % 2)],
           reads=[("sz1", ci0)], tag="z1pre")
    for kt in range(8):
        ld("w", "pool", wgq[:, kt * FFD:(kt + 1) * FFD],
           agout_wg[kt * 128:(kt + 1) * 128, :], [("wgq", kt)],
           reads=[("ag", "wg")], tag="lwgq")
        ld("w", "pool", wuq[:, kt * FFD:(kt + 1) * FFD],
           agout_wu[kt * 128:(kt + 1) * 128, :], [("wuq", kt)],
           reads=[("ag", "wu")], tag="lwuq")
    for kt in range(22):
        r_, pos = (kt // 3, kt % 3) if kt <= 20 else (7, 2)
        srow = r_ * 384 + pos * 128
        ld("w", "pool", wdq[:, kt * D:(kt + 1) * D],
           agout_wd[srow:srow + 128, :], [("wdq", kt)],
           reads=[("ag", "wd")], tag="lwdq")
    def hq_tiles(sl_i):
        return list(range(11)) if sl_i == 0 else list(range(11, 22))

    def stage_x(ci):
        par = ci % 2
        zsl = z1g[:, par * D:(par + 1) * D]
        if ci >= 2:
            ld("w", "pool", zsl, spill_z1[ci * 128:(ci + 1) * 128, :],
               [("z1gb", par)], reads=[("sz1", ci)])
        quant_tile(zsl, [("z1gb", par)], D, xq[:, 0:D], ("xq",), tag="x",
                   nsl=1)

    def stage_a(ci):
        transpose_tile(xq[:, 0:D], [("xq",)],
                       lambda kt: xqT[:, kt * 128:(kt + 1) * 128],
                       lambda kt: ("xqT",), tag=f"xq{ci}")
        for nm, wbuf, wname, obuf, obkey, func, pb in [
            ("gm", wgq, "wgq", sgt, ("sgt",), AF.Silu, 0),
            ("um", wuq, "wuq", sut, ("sut",), AF.Copy, 2),
        ]:
            for nc_i in range(6):
                n0 = nc_i * 512
                nw = min(512, FFD - n0)
                ps, pk = PS[pb + nc_i % 2], PSK[pb + nc_i % 2]
                proj(ps[:, 0:nw], pk,
                     lambda kt: xqT[:, kt * 128:(kt + 1) * 128],
                     lambda kt, n0=n0, nw=nw, wbuf=wbuf: wbuf[:, kt * FFD + n0:kt * FFD + n0 + nw],
                     8, lambda kt, wname=wname: [("xqT",), (wname, kt)],
                     tag=f"{nm}{nc_i}")
                p.op("act", lambda h, ps=ps, n0=n0, nw=nw, obuf=obuf,
                     func=func: h.activation(
                         obuf[:, n0:n0 + nw], ps[:, 0:nw], func),
                     reads=[pk], writes=[obkey], tag=nm + "ev")

    def stage_b(ci):
        p.op("dve", lambda h: h.tensor_tensor(
            sgt[:, 0:FFD], sgt[:, 0:FFD], sut[:, 0:FFD], ALU.mult),
            reads=[("sgt",), ("sut",)], writes=[("sgt",)], tag="hff")
        quant_tile(sgt[:, 0:FFD], [("sgt",)], FFD, hqT[:, 0:FFD],
                   ("hqTp",), tag="h",
                   out_wkeys_fn=lambda sl_i: [("hqT", t)
                                              for t in hq_tiles(sl_i)])

    def stage_c1(ci):
        for grp in range(6):
            nts = list(range(grp * 4, min(22, grp * 4 + 4)))
            bank, bkey = PT[grp % 2], PTK[grp % 2]
            for j, ntl in enumerate(nts):
                p.op("pe", lambda h, j=j, ntl=ntl, bank=bank: h.transpose(
                    bank[:, j * 128:(j + 1) * 128],
                    hqT[:, ntl * 128:(ntl + 1) * 128], ident[:]),
                    reads=[("hqT", ntl), ("ident",)], writes=[bkey],
                    tag="trh")
            for j, ntl in enumerate(nts):
                p.op("act", lambda h, j=j, ntl=ntl, bank=bank: h.activation(
                    hqT[:, ntl * 128:(ntl + 1) * 128],
                    bank[:, j * 128:(j + 1) * 128], AF.Copy),
                    reads=[bkey], writes=[("hqT", ntl)], tag="trhev")
        for half in range(2):
            ps, pk = PS[4 + half], PSK[4 + half]
            proj(ps[:, 0:512], pk,
                 lambda kt: hqT[:, kt * 128:(kt + 1) * 128],
                 lambda kt, half=half: wdq[:, kt * D + half * 512:kt * D + (half + 1) * 512],
                 22, lambda kt: [("hqT", kt), ("wdq", kt)], tag=f"dm{half}")

    def stage_c2(ci):
        par = ci % 2
        zsl = z1g[:, par * D:(par + 1) * D]
        for half in range(2):
            ps, pk = PS[4 + half], PSK[4 + half]
            sl = slice(half * 512, (half + 1) * 512)
            p.op("dve", lambda h, ps=ps, sl=sl, zsl=zsl: h.tensor_tensor(
                outf[:, sl], ps[:, 0:512], zsl[:, sl], ALU.add),
                reads=[pk, ("z1gb", par)], writes=[("outfb",)], tag="z2in")
        cs = ln_stats(outf[:], D, EPS1, [("outfb",)], tag=f"p2{ci}",
                      sq_ap=xqT[:, 0:D], sq_keys=[("xqT",)])
        p.op("dve", lambda h, cs=cs: h.tensor_scalar(
            outf[:], outf[:], stv(2, cs), stv(6, cs),
            ALU.subtract, ALU.mult),
            reads=[("outfb",), ST], writes=[("outfb",)], tag="outap")
        p.dma("out", "sp",
              lambda h, ci=ci: h.dma_start(
                  out=y[ci * 128:(ci + 1) * 128, :], in_=outf[:]),
              reads=[("outfb",)], writes=[("y", ci)], tag="sty")

    stage_x(0)
    stage_a(0)
    stage_x(1)
    stage_b(0)
    for ci in range(NT):
        if ci + 1 < NT:
            stage_a(ci + 1)
        stage_c1(ci)
        if ci + 1 < NT:
            stage_b(ci + 1)
        stage_c2(ci)
        if ci + 2 < NT:
            stage_x(ci + 2)

    p.op("sp", lambda h: None, reads=[("y", ci) for ci in range(NT)], tag="fin")
    p.emit()
    return nc


_NC_CACHE = [None]


def _consts():
    g = np.array(GAMMA, dtype=np.float64)
    idx = np.arange(128, dtype=np.float64)
    gq = np.zeros((128, NP * 128))
    gk = np.zeros((128, NP * 128))
    gcv = np.zeros((128, NP))
    for pr in range(NP):
        for r in range(128):
            hh = 2 * pr + (1 if r >= 64 else 0)
            gq[r, pr * 128:(pr + 1) * 128] = 0.125 * g[hh] ** idx
            gk[r, pr * 128:(pr + 1) * 128] = g[hh] ** (-idx)
            gcv[r, pr] = g[hh] ** 128
    gkn = np.zeros((128, D))
    for col in range(D):
        gkn[:, col] = g[col // HD] ** (-idx)
    mskT = (idx[None, :] >= idx[:, None]).astype(np.float64)
    ident = np.eye(128)
    out = {k: np.ascontiguousarray(v, dtype=np.float32) for k, v in
           dict(mskT=mskT, ident=ident).items()}
    for name, v in dict(gq=gq, gk=gk, gkn=gkn, gc=gcv).items():
        hi = v.astype(np.float32)
        lo = (v - hi.astype(np.float64)).astype(np.float32)
        out[name + "_hi"] = np.ascontiguousarray(hi)
        out[name + "_lo"] = np.ascontiguousarray(lo)
    return out


def kernel(**inputs):
    f = np.float32
    x = np.ascontiguousarray(np.asarray(inputs["L0_attn_x"], f))
    B, S, _ = x.shape

    def cols8(v):
        return np.ascontiguousarray(np.asarray(v, f).reshape(8, 128).T)

    b_in = np.asarray(inputs["L0_attn_b_in"], f)
    b_in_a = np.zeros((22 * 128,), f)
    b_in_a[:HID] = b_in[:HID]
    b_in_b = np.zeros((22 * 128,), f)
    b_in_b[:HID] = b_in[HID:]
    common = {
        "ln1w": cols8(inputs["L0_attn_ln1_w"]),
        "ln1b": cols8(inputs["L0_attn_ln1_b"]),
        "ln2w": cols8(inputs["L0_attn_ln2_w"]),
        "ln2b": cols8(inputs["L0_attn_ln2_b"]),
        "wq": np.ascontiguousarray(np.asarray(inputs["L0_attn_wq"], f)),
        "wk": np.ascontiguousarray(np.asarray(inputs["L0_attn_wk"], f)),
        "wv": np.ascontiguousarray(np.asarray(inputs["L0_attn_wv"], f)),
        "wg": np.ascontiguousarray(np.asarray(inputs["L0_attn_wg"], f)),
        "wo": np.ascontiguousarray(np.asarray(inputs["L0_attn_wo"], f)),
        "w_in": np.ascontiguousarray(np.asarray(inputs["L0_attn_w_in"], f)),
        "w_out": np.ascontiguousarray(np.asarray(inputs["L0_attn_w_out"], f)),
        "b_in_a": np.ascontiguousarray(b_in_a.reshape(22, 128).T),
        "b_in_b": np.ascontiguousarray(b_in_b.reshape(22, 128).T),
        "b_out_bc": np.ascontiguousarray(np.broadcast_to(
            np.asarray(inputs["L0_attn_b_out"], f), (128, D))),
        **_consts(),
    }
    w_gate = np.asarray(inputs["L0_ffn_w_gate"], f)
    w_up = np.asarray(inputs["L0_ffn_w_up"], f)
    w_down = np.asarray(inputs["L0_ffn_w_down"], f)

    if _NC_CACHE[0] is None:
        _NC_CACHE[0] = build_nc()
    nc = _NC_CACHE[0]

    in_maps = []
    for c in range(8):
        b, hf = c // 2, c % 2
        m = dict(common)
        m["x_loc"] = np.ascontiguousarray(x[b, hf * SC:(hf + 1) * SC, :])
        m["x_rem"] = np.ascontiguousarray(x[b, 0:SC, :])
        m["flag"] = np.full((128, 1), float(hf), f)
        m["wg_shard"] = np.ascontiguousarray(w_gate[:, 128 * c:128 * (c + 1)])
        m["wu_shard"] = np.ascontiguousarray(w_up[:, 128 * c:128 * (c + 1)])
        k0 = 384 * c if c < 7 else 2816 - 384
        m["wd_shard"] = np.ascontiguousarray(w_down[:, k0:k0 + 384])
        in_maps.append(m)

    res = run_bass_kernel_spmd(nc, in_maps, core_ids=list(range(8)))
    out = np.empty((B, S, D), f)
    for c in range(8):
        b, hf = c // 2, c % 2
        out[b, hf * SC:(hf + 1) * SC, :] = res.results[c]["y"]
    return out
